# version 13 (final)
# baseline (speedup 1.0000x reference)
import math
import numpy as np
import ml_dtypes
from contextlib import ExitStack
import concourse.bass as bass
import concourse.mybir as mybir
from concourse.bass_utils import run_bass_kernel_spmd

F32 = mybir.dt.float32
BF16 = mybir.dt.bfloat16
FP8 = mybir.dt.float8e4
AF = mybir.ActivationFunctionType
ALU = mybir.AluOpType
AX = mybir.AxisListType
bfdt = ml_dtypes.bfloat16

NCORE = 8
D = 2048
JC = D // 128
NEG = -1.0e30
MAGIC = float(3 * 2 ** 22)
EPS_LN = 1e-5
RT128 = 1.0 / math.sqrt(128.0)

class _Dims:
    def __init__(self, seq=4096):
        self.SEQ = seq
        self.B = 2
        self.TB = seq // NCORE
        self.TL = 2 * self.TB
        self.KC = seq // 128
        self.QG = seq // 512
        self.NCH = seq // 256
DIMS = _Dims(4096)

def _set_dims(seq):
    global DIMS
    DIMS = _Dims(seq)

SPLIT = [[0, 1, 2, 3]]
UPTO = 8
SKIP_ATTN = False
DEBUG_DUMP = False



class P:
    ENGS = ("t", "v", "s", "g", "y")

    def __init__(self, nc):
        self.nc = nc
        self.sems = {}
        self.cnt = {}
        self.obs = {e: {} for e in self.ENGS}
        self.queues = None
        self.phase_n = 0

    def setup(self, ctx):
        self.ctx = ctx
        for name in ("c_t", "c_v", "c_s", "c_g", "d_c"):
            self.sems[name] = ctx.enter_context(self.nc.semaphore(name))
            self.cnt[name] = 0

    def _dsem(self, e, ch):
        name = f"d{e}_{ch}"
        if name not in self.sems:
            self.sems[name] = self.ctx.enter_context(self.nc.semaphore(name))
            self.cnt[name] = 0
        return name

    def _chain(self, e):
        return {"t": "c_t", "v": "c_v", "s": "c_s", "g": "c_g"}[e]

    class _Phase:
        def __init__(self, p, name):
            self.p, self.name = p, name
        def __enter__(self):
            assert self.p.queues is None
            self.p.queues = {e: [] for e in P.ENGS}
            return self
        def __exit__(self, et, ev, tb):
            if et is not None:
                self.p.queues = None
                return False
            p = self.p
            q = p.queues
            p.queues = None
            if any(q[e] for e in P.ENGS):
                with p.nc.Block(f"ph{p.phase_n}_{self.name}") as blk:
                    for e, attr in (("t", "tensor"), ("v", "vector"),
                                    ("s", "scalar"), ("g", "gpsimd"),
                                    ("y", "sync")):
                        if q[e]:
                            def run(eng, _ops=q[e]):
                                for f in _ops:
                                    f(eng)
                            getattr(blk, attr)(run)
            p.phase_n += 1
            for e in P.ENGS:
                for cs in ("c_t", "c_v", "c_s", "c_g"):
                    p.obs[e][cs] = p.cnt[cs]
            return False

    def phase(self, name=""):
        return P._Phase(self, name)

    def waits(self, e, toks):
        best = {}
        for t in toks:
            if t is None:
                continue
            s, v = t
            if best.get(s, -1) < v:
                best[s] = v
        for s, v in best.items():
            self.wait(e, (s, v))

    def wait(self, e, tok):
        if tok is None:
            return
        sem, val = tok
        if self.obs[e].get(sem, 0) >= val:
            return
        self.obs[e][sem] = val
        h = self.sems[sem]
        self.queues[e].append(lambda eng, h=h, v=val: eng.wait_ge(h, v))

    def op(self, e, meth, *args, deps=(), inc=True, **kwargs):
        from concourse.bass_types import TensorHandle
        args = tuple(a.ap() if isinstance(a, TensorHandle) else a for a in args)
        kwargs = {k: (v.ap() if isinstance(v, TensorHandle) else v)
                  for k, v in kwargs.items()}
        if e in ("v", "s", "g"):
            inc = True
            deps = list(deps) + [self.tok(e)]
        self.waits(e, deps)
        cs = self._chain(e)
        if inc:
            self.cnt[cs] += 1
            h = self.sems[cs]
            def f(eng, m=meth, a=args, k=kwargs, h=h):
                getattr(eng, m)(*a, **k).then_inc(h, 1)
            self.queues[e].append(f)
            return (cs, self.cnt[cs])
        def f2(eng, m=meth, a=args, k=kwargs):
            getattr(eng, m)(*a, **k)
        self.queues[e].append(f2)
        return (cs, self.cnt[cs])

    def tok(self, e):
        cs = self._chain(e)
        return (cs, self.cnt[cs])

    def fence(self):
        return [self.tok("t"), self.tok("v"), self.tok("s"), self.tok("g")]

    def dma(self, e, out, in_, deps=(), ch="c", **kwargs):
        self.waits(e, deps)
        sem = self._dsem(e, ch)
        self.cnt[sem] += 16
        h = self.sems[sem]
        def f(eng, o=out, i=in_, h=h, k=kwargs):
            try:
                eng.dma_start(o, i, **k).then_inc(h, 16)
            except Exception:
                import sys
                print("DMA FAIL out=", o, "in=", i, file=sys.stderr)
                raise
        self.queues[e].append(f)
        return (sem, self.cnt[sem])

    def dtok(self, e, ch):
        sem = self._dsem(e, ch)
        return (sem, self.cnt[sem])

    def cc(self, kind, aop, ins, outs, deps=()):
        self.waits("g", deps)
        self.cnt["d_c"] += 1
        h = self.sems["d_c"]
        def f(eng, k=kind, o=aop, i=ins, u=outs, h=h):
            eng.collective_compute(k, o, replica_groups=[list(range(NCORE))],
                                  ins=i, outs=u).then_inc(h)
        self.queues["g"].append(f)
        return ("d_c", self.cnt["d_c"])

    def mm(self, out, lhsT, rhs, start=True, stop=True, deps=(), inc=True):
        from concourse.bass_types import TensorHandle
        if isinstance(out, TensorHandle):
            out = out.ap()
        if isinstance(rhs, TensorHandle):
            rhs = rhs.ap()
        if isinstance(lhsT, TensorHandle):
            lhsT = lhsT.ap()
        n = out.shape[-1]
        lim = 512 if rhs.dtype == F32 else 1024
        if n <= lim:
            return self.op("t", "matmul", out, lhsT, rhs, start=start,
                           stop=stop, deps=deps, inc=inc)
        t = None
        nch = (n + lim - 1) // lim
        for c in range(nch):
            c0 = c * lim
            cw = min(lim, n - c0)
            t = self.op("t", "matmul", out[:, c0:c0 + cw], lhsT,
                        rhs[:, c0:c0 + cw], start=start, stop=stop,
                        deps=deps if c == 0 else (), inc=(inc and c == nch - 1))
        return t

    def reset_sems(self):
        with self.phase("flush"):
            for name in list(self.sems):
                if name.startswith("dy") or name.startswith("ds"):
                    self.wait("y", (name, self.cnt[name]))
                elif name.startswith("dg"):
                    self.wait("g", (name, self.cnt[name]))
            self.wait("g", ("d_c", self.cnt["d_c"]))
        with self.phase("semclr"):
            for name in self.sems:
                h = self.sems[name]
                self.queues["y"].append(lambda eng, h=h: eng.sem_clear(h))
        for name in self.sems:
            self.cnt[name] = 0
        for e in self.ENGS:
            self.obs[e] = {}


def _later(a, b):
    if a is None:
        return b
    if b is None:
        return a
    if a[0] == b[0]:
        return a if a[1] >= b[1] else b
    return b



def _tile_lhsT(Wkm):
    K, M = Wkm.shape
    return np.ascontiguousarray(
        Wkm.reshape(K // 128, 128, M // 128, 128).transpose(2, 0, 1, 3))


def _prep_host(inp):
    d = DIMS
    B, S, TB = d.B, d.SEQ, d.TB
    f32 = np.float32
    C = np.ascontiguousarray
    g = lambda k: np.asarray(inp[k], dtype=f32)
    sh = {}

    sh["ident_f"] = np.eye(128, dtype=f32)
    sh["ident_b"] = np.eye(128, dtype=f32).astype(bfdt)
    sh["ones_col_b"] = np.ones((128, 1), dtype=bfdt)
    sh["ones_row_f"] = np.ones((1, 128), dtype=f32)
    sh["ones_row_b"] = np.ones((1, 128), dtype=bfdt)
    tri = np.zeros((4, 128, 512), dtype=f32)
    for r in range(4):
        allow = np.arange(512)[None, :] <= (r * 128 + np.arange(128))[:, None]
        tri[r] = np.where(allow, 0.0, NEG)
    sh["tri4"] = C(tri.transpose(1, 0, 2).reshape(128, 4 * 512).astype(bfdt))
    ind = np.zeros((d.NCH, S), dtype=f32)
    for t in range(S // 512):
        ind[2 * t, t * 512:t * 512 + 256] = 1.0
        ind[2 * t + 1, t * 512 + 256:(t + 1) * 512] = 1.0
    sh["ind16"] = C(ind.astype(bfdt))
    sh["d2_row"] = (2.0 * (np.arange(128) % 64)).astype(f32).reshape(1, 128)
    sh["pos_row"] = np.arange(S, dtype=f32).reshape(1, S)
    NCH = d.NCH
    pos = np.arange(S)
    ch_end = (np.arange(NCH) + 1) * 256
    gmask = np.where(pos[None, :] >= ch_end[:, None], 0.0, NEG).astype(f32)
    cur = (pos // 256)[None, :] == np.arange(NCH)[:, None]
    sh["l3_gmask"] = C(np.where(cur, -NEG, gmask).astype(bfdt))

    sh["l0_gout_cols"] = C(g("L0_attn_g_out").reshape(JC, 128).T)
    sh["l0_gnorm_cols"] = C(g("L0_attn_g_norm").reshape(JC, 128).T)
    sh["l0_wout"] = _tile_lhsT(g("L0_attn_Wout"))
    sh["l0_w1"] = _tile_lhsT(C(g("L0_ffn_w1").T))
    sh["l0_w3"] = _tile_lhsT(C(g("L0_ffn_w3").T))
    sh["l0_w2"] = _tile_lhsT(C(g("L0_ffn_w2").T))
    sh["l1_gatew"] = C(g("L1_ffn_gate_weight").T.reshape(JC, 128, 8))
    sh["l1_guw"] = np.stack([_tile_lhsT(C(g("L1_ffn_gate_up_weights")[e].T))
                             for e in range(8)])
    sh["l1_down"] = np.stack([_tile_lhsT(C(g("L1_ffn_down_weights")[e].T))
                              for e in range(8)])
    sh["l1_sgu"] = _tile_lhsT(C(g("L1_ffn_shared_gate_up").T))
    sh["l1_sdown"] = _tile_lhsT(C(g("L1_ffn_shared_down").T))
    sh["l2_wo"] = C(g("L2_attn_W_O").reshape(16, 128, JC, 128).transpose(2, 0, 1, 3))
    sh["l2_bO_cols"] = C(g("L2_attn_b_O").reshape(JC, 128).T)
    sh["l2_wf"] = C(_tile_lhsT(C(np.asarray(inp["L2_ffn_weight"]).T)))
    sh["l3_wo"] = _tile_lhsT(g("L3_attn_wo"))
    sh["l3_wg"] = _tile_lhsT(C(g("L3_ffn_w_gate").T))
    sh["l3_wu"] = _tile_lhsT(C(g("L3_ffn_w_up").T))
    sh["l3_wd"] = _tile_lhsT(C(g("L3_ffn_w_down").T))

    Wq0 = g("L0_attn_Wq")
    Wkv0 = g("L0_attn_Wkv")
    wqkvT = C(g("L1_attn_w_qkv").T)
    bqkv = g("L1_attn_b_qkv")
    WQ2, WK2, WV2 = g("L2_attn_W_Q"), g("L2_attn_W_K"), g("L2_attn_W_V")
    bQ2, bK2, bV2 = g("L2_attn_b_Q"), g("L2_attn_b_K"), g("L2_attn_b_V")
    wq3, wk3, wv3 = g("L3_attn_wq"), g("L3_attn_wk"), g("L3_attn_wv")
    x = np.asarray(inp["L0_attn_x"], dtype=f32)

    def qkv_pack(qt, kt, vt):
        nkv = len(kt)
        out = np.empty((JC, 4 + 2 * nkv, 128, 128), f32)
        for j in range(JC):
            for i in range(4):
                out[j, i] = qt[i][j]
            for i in range(nkv):
                out[j, 4 + i] = kt[i][j]
                out[j, 4 + nkv + i] = vt[i][j]
        return C(out)

    maps = []
    for r in range(NCORE):
        m = dict(sh)
        xs = np.concatenate([x[b, r * TB:(r + 1) * TB, :].T for b in range(B)], 1)
        m["x_in"] = C(xs)
        heads = [4 * (r % 4) + i for i in range(4)]
        mkv = r % 4
        tl = lambda W: _tile_lhsT(W)[0]
        m["l0_qkv"] = qkv_pack([tl(Wq0[:, h * 128:(h + 1) * 128]) for h in heads],
                               [_tile_lhsT(Wkv0)[0]], [_tile_lhsT(Wkv0)[1]])
        m["l1_qkv"] = qkv_pack(
            [tl(wqkvT[:, h * 384:h * 384 + 128]) for h in heads],
            [tl(wqkvT[:, h * 384 + 128:h * 384 + 256]) for h in heads],
            [tl(wqkvT[:, h * 384 + 256:h * 384 + 384]) for h in heads])
        m["l2_qkv"] = qkv_pack([tl(WQ2[h]) for h in heads],
                               [tl(WK2[h]) for h in heads],
                               [tl(WV2[h]) for h in heads])
        m["l3_qkv"] = qkv_pack([tl(wq3[:, h * 128:(h + 1) * 128]) for h in heads],
                               [tl(wk3[:, mkv * 128:(mkv + 1) * 128])],
                               [tl(wv3[:, mkv * 128:(mkv + 1) * 128])])
        m["l1_bq"] = C(np.stack([bqkv[h * 384:h * 384 + 128] for h in heads], 1))
        m["l1_bk"] = C(np.stack([bqkv[h * 384 + 128:h * 384 + 256] for h in heads], 1))
        m["l1_bv"] = C(np.stack([bqkv[h * 384 + 256:h * 384 + 384] for h in heads], 1))
        m["l2_bq"] = C(bQ2[heads].T)
        m["l2_bk"] = C(bK2[heads].T)
        m["l2_bv"] = C(bV2[heads].T)
        maps.append(m)
    return maps



class G:
    pass


def _declare(nc, map0, layers):
    g = G()
    g.nc = nc
    g.map0 = map0
    d = DIMS
    g.h = {}
    def inp(k):
        if k not in g.h:
            v = map0[k]
            dt = F32 if v.dtype == np.float32 else BF16
            g.h[k] = nc.dram_tensor(k, list(v.shape), dt, kind="ExternalInput")
        return g.h[k]
    g.inp = inp
    g.out = nc.dram_tensor("x_out", [D, d.TL], F32, kind="ExternalOutput")
    if (1 in layers) or (3 in layers):
        g.rope_sin = nc.dram_tensor("sc_sin", [128, d.SEQ], F32)
        g.rope_cos = nc.dram_tensor("sc_cos", [128, d.SEQ], F32)
    g.agx_in = {}; g.agx_out = {}; g.kvs = {}; g.agh_in = {}; g.agh_out = {}
    g.qs = {}
    for L in layers:
        g.agx_in[L] = nc.dram_tensor(f"agxi{L}", [D, d.TL], BF16)
        g.agx_out[L] = nc.dram_tensor(f"agxo{L}", [NCORE * D, d.TL], BF16)
        kk = "ExternalOutput" if DEBUG_DUMP else "Internal"
        g.kvs[L] = nc.dram_tensor(f"kvs{L}", [8 * 128, d.SEQ], BF16, kind=kk)
        g.qs[L] = nc.dram_tensor(f"qs{L}", [4 * 128, d.SEQ], BF16, kind=kk)
        g.agh_in[L] = nc.dram_tensor(f"aghi{L}", [4 * 128, d.SEQ], BF16)
        if DEBUG_DUMP:
            g.dbg_hid = nc.dram_tensor(f"dbg_hid{L}", [4 * 128, d.SEQ], BF16,
                                       kind="ExternalOutput")
            g.dbg_bias = nc.dram_tensor(f"dbg_bias{L}", [128, 32], F32,
                                        kind="ExternalOutput")
        g.agh_out[L] = nc.dram_tensor(f"agho{L}", [NCORE * 4 * 128, d.SEQ], BF16)
    if 3 in layers:
        g.mvd = nc.dram_tensor("mvd", [d.NCH, d.SEQ], BF16)
    g.axl = nc.dram_tensor("axl", [NCORE * D, d.TB], BF16)
    g.hidl = nc.dram_tensor("hidl", [2 * 16 * 128, d.TB], BF16)
    g.rows_d = nc.dram_tensor("rows_d", [1, 8192], F32)
    if 1 in layers:
        g.cmb_d = nc.dram_tensor("cmb_d", [8, d.TL], BF16)
    g.wq_slot = 48
    g.wq_done = {}
    g.qw = {}
    for nm, nt in (("l0_w1", 44 * JC), ("l0_w3", 44 * JC), ("l0_w2", JC * 44),
                   ("l3_wg", 64 * JC), ("l3_wu", 64 * JC), ("l3_wd", JC * 64)):
        if (nm.startswith("l0") and 0 in layers) or (nm.startswith("l3") and 3 in layers):
            g.qw[nm] = nc.dram_tensor("q_" + nm, [nt * 128, 128], BF16)
    if (0 in layers) or (3 in layers):
        g.qshd = nc.dram_tensor("qshard", [16384, 128], BF16)
        g.qsf = nc.dram_tensor("qsf", [16384, 128], F32)
        g.wqar_i = nc.dram_tensor("wqar_i", [1, 16], F32)
        g.wqar_o = nc.dram_tensor("wqar_o", [1, 16], F32)
        g._wqoff = {}
    if 2 in layers:
        g.ar_in = nc.dram_tensor("ar_in", [1, 128], F32)
        g.ar_out = nc.dram_tensor("ar_out", [1, 128], F32)
        g.xq8d = nc.dram_tensor("xq8d", [D, d.TL], FP8)
    return g


def _alloc_sbuf(g, ctx, layers):
    nc = g.nc
    d = DIMS
    sb = lambda n, shp, dt=F32: ctx.enter_context(nc.sbuf_tensor(n, shp, dt))
    g.cst = sb("cst", [128, 16])
    g.cvals = []
    def cconst(val):
        val = float(val)
        if val in g.cvals:
            return
        idx = len(g.cvals)
        g.cvals.append(val)
        nc.const_aps.register(F32, val, g.cst[:, idx:idx + 1])
    g.cconst = cconst
    g.XT = sb("XT", [128, JC * d.TL])
    g.ARW = max(8 * d.SEQ, 4 * d.SEQ + 16384, 12 * JC * 128)
    g.AR = sb("AR", [128, g.ARW], BF16)
    g.SM = sb("SMa", [128, 2048])
    g.SM2 = sb("SM2", [128, 1024])
    g.WR = sb("WRa", [128, 4096], BF16)
    g.RO = sb("ROa", [128, 1024])
    g.ROWS = sb("ROWS", [1, 3232])
    g.E = sb("Ea", [128, 2048], BF16)
    g.idf = sb("idf", [128, 128])
    g.idb = sb("idb", [128, 128], BF16)
    g.ones_cb = sb("ones_cb", [128, 1], BF16)
    g.ones_rf = sb("ones_rf", [1, 128])
    g.ones_rb = sb("ones_rb", [1, 128], BF16)
    g.tri = sb("tri", [128, 2048], BF16)
    g.smalls = sb("smalls", [128, 96])
    g.mrow_s = sb("mrow_s", [8, 128])
    if 2 in layers:
        g.X8 = sb("X8", [128, 2048], FP8)
        g.W8 = sb("W8", [128, 4096], FP8)
    if 1 in layers:
        g.cvt8 = sb("cvt8", [128, 128], BF16)
        g.cvt8b = sb("cvt8b", [8, 1024], BF16)
        g.cmbr = sb("cmbr", [1, 1024], BF16)
    if 3 in layers:
        g.gms = sb("gms", [16, 512], BF16)
        g.mvs = sb("mvs", [16, 512], BF16)
        g.MV = sb("MVsb", [d.NCH, d.SEQ], BF16)
        g.ind16 = sb("ind16sb", [d.NCH, d.SEQ], BF16)
    if 1 in layers or 2 in layers or 3 in layers:
        g.Et = sb("Eta", [128, 2048], BF16)
    return g



def _featmaj_ap(t, TL):
    return bass.AP(t, 0, [[TL, 128], [128 * TL, JC], [1, TL]])

def AP(t, off, pat):
    return bass.AP(t, off, pat)


def _load_consts(p, g, layers):
    d = DIMS
    for v in (EPS_LN, 1e-12, -16.0, 16.0, RT128, -math.log(10000.0) / 128.0):
        g.cconst(v)
    with p.phase("init"):
        toks = []
        toks.append(p.dma("y", g.cst[:, 0:16], g.inp("cvals")[:]))
        loads = [(g.idf, "ident_f"), (g.idb, "ident_b"),
                 (g.ones_cb, "ones_col_b"), (g.ones_rf, "ones_row_f"),
                 (g.ones_rb, "ones_row_b"), (g.tri, "tri4")]
        if 3 in layers:
            loads.append((g.ind16, "ind16"))
        for dst, src in loads:
            toks.append(p.dma("y", dst[:], g.inp(src)[:]))
        t_x = p.dma("y", g.XT[:, 0:JC * d.TL],
                    _featmaj_ap(g.inp("x_in"), d.TL))
        p.wait("y", t_x)


def _bld_rope_tables(p, g):
    d = DIMS
    nc = g.nc
    LN4 = math.log(10000.0)
    with p.phase("rope_tab"):
        t_d2 = p.dma("y", g.ROWS[0:1, 0:128], g.inp("d2_row")[:])
        t_if = p.op("s", "activation", g.ROWS[0:1, 128:256], g.ROWS[0:1, 0:128],
                    AF.Exp, scale=-LN4 / 128.0, deps=[t_d2])
        with nc.psum_tensor("ps_rope", [128, 512], F32) as ps:
            t_prev = None
            for c in range(d.SEQ // 512):
                t_pr = p.dma("y", g.ROWS[0:1, 256:768],
                             g.inp("pos_row")[0:1, c * 512:(c + 1) * 512],
                             deps=[t_prev] if t_prev else [], ch="ro0")
                t_mm = p.mm(ps[:, 0:512], g.ROWS[0:1, 128:256],
                            g.ROWS[0:1, 256:768],
                            deps=[x for x in (t_if, t_pr, t_prev) if x])
                a = g.SM[:, 0:512]; b = g.SM[:, 512:1024]
                t1 = p.op("v", "tensor_scalar", a, ps[:, 0:512],
                          1.0 / (2 * math.pi), MAGIC, ALU.mult, ALU.add,
                          deps=[t_mm])
                t2 = p.op("v", "tensor_scalar", a, a, MAGIC, None, ALU.subtract,
                          deps=[t1])
                t3 = p.op("v", "scalar_tensor_tensor", b, a, -2 * math.pi,
                          ps[:, 0:512], ALU.mult, ALU.add, deps=[t2])
                t3 = p.op("v", "tensor_scalar", b, b, 3.141592, -3.141592,
                          ALU.min, ALU.max, deps=[t3])
                t4 = p.op("s", "activation", g.SM2[:, 0:512], b, AF.Sin, deps=[t3])
                td = p.dma("y", g.rope_sin[:, c * 512:(c + 1) * 512],
                           g.SM2[:, 0:512], deps=[t4], ch="rt0")
                t0 = p.op("v", "tensor_scalar", a, ps[:, 0:512],
                          1.0 / (2 * math.pi), 0.25, ALU.mult, ALU.add,
                          deps=[t_mm, t3])
                t1 = p.op("v", "tensor_scalar", a, a, MAGIC, None, ALU.add,
                          deps=[t0])
                t2 = p.op("v", "tensor_scalar", a, a, MAGIC, None,
                          ALU.subtract, deps=[t1])
                t3b = p.op("v", "scalar_tensor_tensor", b, a, -2 * math.pi,
                           ps[:, 0:512], ALU.mult, ALU.add, deps=[t2, t4])
                t3c = p.op("v", "tensor_scalar", b, b, math.pi / 2.0, None,
                           ALU.add, deps=[t3b])
                t3c = p.op("v", "tensor_scalar", b, b, 3.141592, -3.141592,
                           ALU.min, ALU.max, deps=[t3c])
                t4b = p.op("s", "activation", g.SM2[:, 512:1024], b, AF.Sin,
                           deps=[t3c])
                td2 = p.dma("y", g.rope_cos[:, c * 512:(c + 1) * 512],
                            g.SM2[:, 512:1024], deps=[t4b], ch="rt1")
                t_prev = t4b
                p.wait("y", td)
                p.wait("y", td2)


def _ln_rows(p, g, rows_for_ffn=False, fold2=False):
    d = DIMS
    nc = g.nc
    TL = d.TL
    un = p.phase_n
    with nc.psum_tensor(f"lns{un}", [1, 1024], F32) as ps_s, \
         nc.psum_tensor(f"lnq{un}", [1, 1024], F32) as ps_q, \
         nc.psum_tensor(f"lnb{un}", [128, 1024], F32) as ps_b:
        halves = [(h0, min(512, TL - h0)) for h0 in range(0, TL, 512)]
        t_s = t_q = None
        stage_use = [[], []]
        sq_use = None
        fen = p.fence() + ([g._wq_st] if getattr(g, "_wq_st", None) else [])
        for j in range(JC):
            par = j % 2
            sl = g.E[:, par * 1024:par * 1024 + TL]
            tc = p.op("v", "tensor_copy", sl, g.XT[:, j * TL:(j + 1) * TL],
                      deps=fen + stage_use[par])
            sqv = g.SM2.bitcast(BF16)[:, 0:TL]
            tsq = p.op("s", "square", sqv, sl,
                       deps=[tc] + ([sq_use] if sq_use else []))
            for h0, hw in halves:
                t_s = p.mm(ps_s[0:1, h0:h0 + hw], g.ones_cb, sl[:, h0:h0 + hw],
                           start=(j == 0), stop=(j == JC - 1), deps=[tc])
                t_q = p.mm(ps_q[0:1, h0:h0 + hw], g.ones_cb, sqv[:, h0:h0 + hw],
                           start=(j == 0), stop=(j == JC - 1), deps=[tsq])
            stage_use[par] = [t_s, tsq]
            sq_use = t_q
        R = g.ROWS
        mu = R[0:1, 0:TL]
        rr = R[0:1, TL:2 * TL]
        t1 = p.op("v", "tensor_scalar", mu, ps_s[0:1, 0:TL], 1.0 / D, None,
                  ALU.mult, deps=[t_s])
        t2 = p.op("v", "tensor_mul", rr, mu, mu, deps=[t1])
        t3 = p.op("v", "scalar_tensor_tensor", rr, ps_q[0:1, 0:TL], 1.0 / D,
                  rr, ALU.mult, ALU.subtract, deps=[t_q, t2])
        if fold2:
            v2 = g.ROWS[0:1, 2 * TL:3 * TL]
            ta = p.op("v", "tensor_scalar", v2, rr, EPS_LN, None, ALU.add,
                      deps=[t3])
            tb = p.op("s", "activation", v2, v2, AF.Sqrt, deps=[ta])
            tcx = p.op("v", "reciprocal", v2, v2, deps=[tb])
            td = p.op("v", "tensor_mul", rr, rr, v2, deps=[tcx])
            te = p.op("v", "tensor_mul", rr, rr, v2, deps=[td])
            tf = p.op("s", "activation", rr, rr, AF.Sqrt, bias=EPS_LN, deps=[te])
            tg2 = p.op("v", "reciprocal", rr, rr, deps=[tf])
            t5 = p.op("v", "tensor_mul", rr, rr, v2, deps=[tg2])
        else:
            t4 = p.op("s", "activation", rr, rr, AF.Sqrt, bias=EPS_LN, deps=[t3])
            t5 = p.op("v", "reciprocal", rr, rr, deps=[t4])
        tb1 = p.mm(ps_b[:, 0:TL], g.ones_rf, rr, deps=[t5])
        tv1 = p.op("s", "copy", g.SM[:, 0:TL], ps_b[:, 0:TL], deps=[tb1])
        if not rows_for_ffn:
            t6 = p.op("v", "scalar_tensor_tensor", mu, mu, -1.0, rr, ALU.mult,
                      ALU.mult, deps=[t5])
            tb2 = p.mm(ps_b[:, 0:TL], g.ones_rf, mu, deps=[t6, tv1])
            tv2 = p.op("s", "copy", g.SM[:, TL:2 * TL], ps_b[:, 0:TL], deps=[tb2])
        else:
            tb2 = p.mm(ps_b[:, 0:TL], g.ones_rf, mu, deps=[t1, tv1])
            tv2 = p.op("s", "copy", g.SM[:, TL:2 * TL], ps_b[:, 0:TL], deps=[tb2])
        return tv1, tv2, t5



ACFG = {
    0: dict(kv="mq", rope=False, bias=False, l2n=True, causal=False,
            gate=False, wout="l0_wout", post_ln=True, scale=16.0),
    1: dict(kv="mha", rope=True, bias=True, l2n=False, causal=True,
            gate=False, wout=None, post_ln=False, scale=RT128, exact=True),
    2: dict(kv="mha", rope=False, bias=True, l2n=False, causal=True,
            gate=False, wout="l2_wo", post_ln=False, scale=RT128, exact=True),
    3: dict(kv="gqa", rope=True, bias=False, l2n=False, causal=True,
            gate=True, wout="l3_wo", post_ln=False, scale=RT128, exact=True),
}
S_Q0, S_Q1, S_H0, S_H1, S_K0, S_K1, S_V0, S_V1 = range(8)


def _attn_layer(p, g, L):
    d = DIMS
    nc = g.nc
    TL, TB, SEQ, KC, QG, NCH = d.TL, d.TB, d.SEQ, d.KC, d.QG, d.NCH
    cfg = ACFG[L]
    nkv = 1 if cfg["kv"] in ("mq", "gqa") else 4
    NT = 4 + 2 * nkv
    qkv_h = g.inp(f"l{L}_qkv")
    kvs = g.kvs[L]
    qs = g.qs[L]
    un = L * 100

    with p.phase(f"L{L}a_ln"):
        gain = (L == 0)
        if gain:
            t_gn = p.dma("y", g.smalls[:, 16:32], g.inp("l0_gnorm_cols")[:])
        tv1, tv2, _ = _ln_rows(p, g, fold2=gain)
        xn_toks = []
        for j in range(JC):
            ta = p.op("v", "tensor_mul", g.SM2[:, 0:TL],
                      g.XT[:, j * TL:(j + 1) * TL], g.SM[:, 0:TL], deps=[tv1])
            if gain:
                tb = p.op("v", "tensor_add", g.SM2[:, 0:TL],
                          g.SM2[:, 0:TL], g.SM[:, TL:2 * TL], deps=[ta, tv2])
                tb = p.op("v", "tensor_scalar", g.AR[:, j * TL:(j + 1) * TL],
                          g.SM2[:, 0:TL], g.smalls[:, 16 + j:17 + j], None,
                          ALU.mult, deps=[tb, t_gn])
            else:
                tb = p.op("v", "tensor_add", g.AR[:, j * TL:(j + 1) * TL],
                          g.SM2[:, 0:TL], g.SM[:, TL:2 * TL], deps=[ta, tv2])
            xn_toks.append(tb)
        t_st = p.dma("y", _featmaj_ap(g.agx_in[L], TL),
                     g.AR[:, 0:JC * TL], deps=xn_toks, ch="cp")
        p.wait("g", t_st)
        g.t_agx = p.cc("AllGather", ALU.bypass, [g.agx_in[L].ap()],
                       [g.agx_out[L].ap()])
        wa0 = g.ARW - NT * JC * 128
        t_w = [None, None]
        for w in range(NT):
            wpar = w % 2
            dd = [x for x in (t_w[wpar],) if x]
            if wa0 + w * JC * 128 < JC * TL:
                dd = dd + [t_st]
            t_w[wpar] = p.dma("g", g.AR[:, wa0 + w * JC * 128:
                                        wa0 + (w + 1) * JC * 128],
                              AP(qkv_h, w * 16384,
                                 [[128, 128], [NT * 16384, JC], [1, 128]]),
                              deps=dd, ch=f"w{wpar}")
        g._wa_toks = [x for x in t_w if x]

    agxo = g.agx_out[L]

    def slab_ap(tc, j):
        return AP(g.axl, (tc * D + j * 128) * TB, [[TB, 128], [1, TB]])

    def wtile(w, j):
        c = g.ARW - NT * JC * 128 + (w * JC + j) * 128
        return g.AR[:, c:c + 128]

    with p.phase(f"L{L}a_proj"):
        wqst = getattr(g, "_wq_st", None)
        if wqst:
            p.wait("v", wqst)
        t_parts = []
        for piece in range(4):
            rr0 = piece * 4096
            t_parts.append(p.dma("y" if piece % 2 == 0 else "s",
                                 AP(g.axl, rr0 * TB, [[TB, 4096], [1, TB]]),
                                 AP(agxo, rr0 * TL + g.boff, [[TL, 4096], [1, TB]]),
                                 deps=[g.t_agx], ch="cp"))
        g.t_agx = t_parts[0]
        g._agx2 = t_parts
        if cfg["bias"]:
            p.dma("y", g.smalls[:, 0:4], g.inp(f"l{L}_bq")[:])
            p.dma("y", g.smalls[:, 4:8], g.inp(f"l{L}_bk")[:])
            t_bv = p.dma("y", g.smalls[:, 8:12], g.inp(f"l{L}_bv")[:])
            t_bias = [t_bv]
        else:
            t_bias = []
        if nkv == 1:
            if cfg["l2n"]:
                passes = [[("q", 0, 0), ("q", 1, 1), ("q", 2, 2), ("q", 3, 3)],
                          [("k", 0, 4), ("v", 0, 5)]]
            else:
                passes = [[("q", 0, 0), ("q", 1, 1), ("q", 2, 2), ("q", 3, 3),
                           ("k", 0, 4), ("v", 0, 5)]]
        else:
            passes = [[("q", i, i) for i in range(4)] +
                      [("k", i, 4 + i) for i in range(4)],
                      [("v", i, 8 + i) for i in range(4)]]
        for pn, outs in enumerate(passes):
            nw = len(outs)
            has_v = any(o[0] == "v" for o in outs)
            ctx2 = ExitStack()
            psA = ctx2.enter_context(
                nc.psum_tensor(f"pA{un}_{pn}", [128, nw * 512], F32))
            psT = ctx2.enter_context(
                nc.psum_tensor(f"pT{un}_{pn}", [128, 512], BF16)) if has_v else None
            psB = ctx2.enter_context(
                nc.psum_tensor(f"pB{un}_{pn}", [128, 1024], F32)) if cfg["l2n"] else None
            bcp = psB[:, 0:512] if psB is not None else None
            rowp = psB[0:1, 512:1024] if psB is not None else None
            with ctx2:
                for tc in range(NCORE):
                    if cfg["rope"] and outs[0][0] != "v":
                        t_cos = p.dma("y", g.RO[:, 0:TB],
                                      g.rope_cos[:, tc * TB:(tc + 1) * TB],
                                      deps=p.fence(), ch="ro0")
                        t_sin = p.dma("y", g.RO[:, 512:512 + TB],
                                      g.rope_sin[:, tc * TB:(tc + 1) * TB],
                                      ch="ro1")
                    else:
                        t_cos = t_sin = None
                    t_slb = [None, None]
                    fen = p.fence()
                    guard = getattr(g, "_eslab_guard", [])
                    tlast = None
                    for j in range(JC):
                        par = j % 2
                        t_slb[par] = p.dma("y", g.E[:, par * 512:par * 512 + TB],
                                           slab_ap(tc, j),
                                           deps=g._agx2 + guard +
                                                ([t_slb[par]] if t_slb[par] else fen) +
                                                ([tlast] if tlast else []),
                                           ch=f"sl{par}")
                        dd = [t_slb[par]] + g._wa_toks
                        st = (j == 0); sp = (j == JC - 1)
                        for wi in range(nw):
                            tlast = p.mm(psA[:, wi * 512:wi * 512 + TB],
                                         wtile(outs[wi][2], j),
                                         g.E[:, par * 512:par * 512 + TB],
                                         start=st, stop=sp, deps=dd,
                                         inc=(wi == nw - 1))
                    t_prev_e = None
                    t_rowfree = None
                    t_bcfree = None
                    prev_std = getattr(g, "_prev_std", None)
                    prev_ttr = getattr(g, "_prev_ttr", None)
                    for n, (kind, i, _w) in enumerate(outs):
                        src = psA[:, n * 512:n * 512 + TB]
                        bias = 0.0
                        if cfg["bias"]:
                            bb = {"q": 0, "k": 4, "v": 8}[kind] + i
                            bias = g.smalls[:, bb:bb + 1]
                        stg = g.SM[:, 1024 + (n % 2) * 512:1024 + (n % 2) * 512 + TB]
                        dd = [tlast] + t_bias + ([t_prev_e] if t_prev_e else [])
                        if cfg["bias"]:
                            t_e = p.op("v", "tensor_scalar", stg, src, bias,
                                       None, ALU.add, deps=dd)
                        else:
                            t_e = p.op("s", "copy", stg, src, deps=dd)
                        if cfg["rope"] and kind in ("q", "k"):
                            t_e = _rope_apply(p, g, stg, t_e, t_cos, t_sin, TB)
                        if cfg["l2n"] and kind in ("q", "k"):
                            t_e = _l2n_tile(p, g, stg, t_e, bcp, rowp, TB, un,
                                            t_bcfree, t_rowfree)
                            t_bcfree = t_e
                            t_rowfree = t_e
                        t_c = p.op("v", "tensor_copy", g.E[:, 1024:1024 + TB],
                                   stg, deps=[t_e] +
                                   ([t_prev_e] if t_prev_e else []) +
                                   ([prev_std] if prev_std else []) +
                                   ([prev_ttr] if prev_ttr else []))
                        prev_ttr = None
                        if kind == "q":
                            t_d = p.dma("y", AP(qs, i * 128 * SEQ + tc * TB,
                                                [[SEQ, 128], [1, TB]]),
                                        g.E[:, 1024:1024 + TB], deps=[t_c],
                                        ch="sq")
                            t_prev_e = t_c
                            prev_std = t_d
                        elif kind == "k":
                            t_d = p.dma("y", AP(kvs, i * 128 * SEQ + tc * TB,
                                                [[SEQ, 128], [1, TB]]),
                                        g.E[:, 1024:1024 + TB], deps=[t_c],
                                        ch="sk")
                            t_prev_e = t_c
                            prev_std = t_d
                        else:
                            ntr = TB // 128
                            ttr = []
                            for q in range(ntr):
                                ttr.append(p.op("t", "transpose",
                                                psT[:, q * 128:(q + 1) * 128],
                                                g.E[:, 1024 + q * 128:1024 + (q + 1) * 128],
                                                g.idb[:], deps=[t_c]))
                            t_c2 = p.op("v", "tensor_copy", g.E[:, 0:ntr * 128],
                                        psT[:, 0:ntr * 128], deps=ttr)
                            t_d = p.dma("y", AP(kvs, (4 + i) * 128 * SEQ + tc * TB * 128,
                                                [[128, 128], [16384, ntr], [1, 128]]),
                                        g.E[:, 0:ntr * 128]
                                        .rearrange("p (n f) -> p n f", n=ntr),
                                        deps=[t_c2], ch="sv")
                            t_prev_e = t_c2
                            prev_std = t_d
                            prev_ttr = ttr[-1]
                            g._eslab_guard = [t_d]
                        tlast = _later(tlast, p.tok("t"))
                        g._prev_std = prev_std
                        g._prev_ttr = prev_ttr
        g._tk_st = [p.dtok("y", "sq"), p.dtok("y", "sk"), p.dtok("y", "sv")]

    with p.phase(f"L{L}a_att"):
        if DEBUG_DUMP and cfg["scale"] != 16.0:
            t_d1 = p.dma("y", AP(g.dbg_bias, 0, [[32, 128], [1, 4]]),
                         g.SM2[:, 896:900], ch="dbg")
            t_d2 = p.dma("y", AP(g.dbg_bias, 4, [[32, 1], [1, 28]]),
                         g.ROWS[0:1, 2048:2076], ch="dbg")
            p.wait("y", t_d2)
        hid_toks = _attn_inner(p, g, L, cfg, nkv, un)
        g.t_agh = p.cc("AllGather", ALU.bypass, [g.agh_in[L].ap()],
                       [g.agh_out[L].ap()], deps=[hid_toks[-1]])

    with p.phase(f"L{L}a_out"):
        if DEBUG_DUMP:
            t_dbg = p.dma("y", g.dbg_hid[:], g.agh_in[L][:], ch="dbg")
            p.wait("y", t_dbg)
        _attn_out(p, g, L, cfg, un)


def _rope_apply(p, g, stg, t_src, t_cos, t_sin, W):
    rot = g.SM2[:, 0:W]
    t1 = p.op("v", "tensor_scalar", rot[0:64, :], stg[64:128, :], -1.0, None,
              ALU.mult, deps=[t_src])
    t2 = p.op("v", "tensor_copy", rot[64:128, :], stg[0:64, :], deps=[t_src])
    t3 = p.op("v", "tensor_mul", rot, rot, g.RO[:, 512:512 + W],
              deps=[t1, t2] + ([t_sin] if t_sin else []))
    t4 = p.op("v", "tensor_mul", stg, stg, g.RO[:, 0:W],
              deps=([t_cos] if t_cos else []) + [t3])
    t5 = p.op("v", "tensor_add", stg, stg, rot, deps=[t4])
    return t5


def _l2n_tile(p, g, stg, t_src, bcp, rowp, W, un, t_bcfree, t_rowfree):
    sq = g.SM2[:, 512:512 + W]
    t1 = p.op("s", "square", sq, stg, deps=[t_src])
    t2 = p.mm(rowp[0:1, 0:W], g.smalls[:, 63:64], sq,
              deps=[t1] + ([t_rowfree] if t_rowfree else []))
    t3 = p.op("s", "activation", g.ROWS[0:1, 2560:2560 + W],
              rowp[0:1, 0:W], AF.Sqrt, bias=1e-12, deps=[t2])
    t4 = p.op("v", "reciprocal", g.ROWS[0:1, 2560:2560 + W],
              g.ROWS[0:1, 2560:2560 + W], deps=[t3])
    t5 = p.mm(bcp[:, 0:W], g.ones_rf, g.ROWS[0:1, 2560:2560 + W],
              deps=[t4] + ([t_bcfree] if t_bcfree else []))
    t6 = p.op("s", "copy", g.SM2[:, 512:512 + W], bcp[:, 0:W], deps=[t5])
    t7 = p.op("v", "tensor_mul", stg, stg, g.SM2[:, 512:512 + W], deps=[t6])
    return t7


def _bound_finalize(p, g, cfg, nkv, un):
    nc = g.nc
    scale = cfg["scale"]
    with nc.psum_tensor(f"bc{un}", [128, 64], F32) as ps_c:
        fen = p.fence()
        for i in range(4):
            kvi = 0 if nkv == 1 else i
            t1 = p.op("v", "tensor_mul", g.ROWS[0:1, 2080 + i:2081 + i],
                      g.ROWS[0:1, 2048 + i:2049 + i],
                      g.ROWS[0:1, 2056 + kvi:2057 + kvi], deps=fen)
            t2 = p.op("s", "activation", g.ROWS[0:1, 2080 + i:2081 + i],
                      g.ROWS[0:1, 2080 + i:2081 + i], AF.Sqrt, deps=[t1])
            t3 = p.op("v", "tensor_scalar", g.ROWS[0:1, 2080 + i:2081 + i],
                      g.ROWS[0:1, 2080 + i:2081 + i], -scale, None, ALU.mult,
                      deps=[t2])
            t4 = p.mm(ps_c[:, i:i + 1], g.ones_rf, g.ROWS[0:1, 2080 + i:2081 + i],
                      deps=[t3])
            p.op("s", "copy", g.SM2[:, 896 + i:897 + i], ps_c[:, i:i + 1],
                 deps=[t4])


def _l3_gate(p, g, h, t_kld, t_qld, un):
    d = DIMS
    nc = g.nc
    SEQ, NCH = d.SEQ, d.NCH
    nblk = SEQ // 128
    qoff = (h % 2) * SEQ
    koff = S_K0 * SEQ
    t_out = []
    with nc.psum_tensor(f"gg{un}_{h}", [16, 512], F32) as ps_g, \
         nc.psum_tensor(f"gt{un}_{h}", [128, 512], F32) as ps_t:
        fen = p.fence()
        t1 = p.op("v", "tensor_reduce", g.SM2[:, 0:NCH],
                  g.AR[:, koff:koff + SEQ].rearrange("p (n c) -> p n c", n=NCH),
                  AX.X, ALU.add, deps=fen + [t_kld])
        t2 = p.op("v", "tensor_copy", g.E[:, 0:NCH], g.SM2[:, 0:NCH], deps=[t1])
        tgl = []
        for c in range(SEQ // 512):
            t_gm = p.dma("y", g.gms[0:NCH, 0:512],
                         g.inp("l3_gmask")[0:NCH, c * 512:(c + 1) * 512],
                         deps=[tgl[-1]] if tgl else fen, ch="gm")
            tg1 = p.mm(ps_g[0:NCH, 0:512], g.E[:, 0:NCH],
                       g.AR[:, qoff + c * 512:qoff + (c + 1) * 512],
                       deps=[t2, t_qld] + ([tgl[-1]] if tgl else []))
            tg2 = p.op("v", "tensor_add", ps_g[0:NCH, 0:512], ps_g[0:NCH, 0:512],
                       g.gms[0:NCH, 0:512], deps=[tg1, t_gm])
            tg3 = p.op("s", "copy", g.SM2[0:NCH, 0:512], ps_g[0:NCH, 0:512],
                       deps=[tg2])
            for q in range(4):
                blk = c * 4 + q
                tg4 = p.op("t", "transpose", ps_t[:, 0:NCH],
                           g.SM2[0:NCH, q * 128:(q + 1) * 128],
                           g.idf[0:NCH, 0:NCH],
                           deps=[tg3] + ([tgl[-1]] if tgl else []))
                tgl.append(p.op("s", "copy", g.SM[:, blk * NCH:blk * NCH + NCH],
                                ps_t[:, 0:NCH], deps=[tg4]))
        TG = g.SM[:, 0:nblk * NCH]
        TGv = TG.rearrange("p (n c) -> p n c", n=nblk)
        W = g.SM2[:, 0:nblk * NCH]
        Wv = W.rearrange("p (n c) -> p n c", n=nblk)
        M = g.RO[:, 0:nblk]
        Mv = M.rearrange("p n -> p n ()").broadcast_to([128, nblk, NCH])
        mk = g.RO[:, 512:512 + nblk * NCH]
        mkv = mk.rearrange("p (n c) -> p n c", n=nblk)
        tw = p.op("v", "tensor_copy", W, TG, deps=tgl)
        for it in range(3):
            tm = p.op("v", "tensor_reduce", M, Wv, AX.X, ALU.max, deps=[tw])
            tmask = p.op("v", "tensor_tensor", mkv, Wv, Mv, ALU.is_ge, deps=[tm])
            tw = p.op("v", "scalar_tensor_tensor", W, mk, NEG, W, ALU.mult,
                      ALU.add, deps=[tmask])
        tm4 = p.op("v", "tensor_reduce", M, Wv, AX.X, ALU.max, deps=[tw])
        tal = p.op("v", "tensor_tensor", mkv, TGv, Mv, ALU.is_ge, deps=[tm4])
        tmv = p.op("v", "tensor_scalar", g.SM2[:, 0:nblk * NCH], mk, 1.0, -NEG,
                   ALU.subtract, ALU.mult, deps=[tal])
        for blk in range(nblk):
            tb1 = p.op("t", "transpose", ps_t[0:NCH, 0:128],
                       g.SM2[:, blk * NCH:blk * NCH + NCH], g.idf[:, 0:128],
                       deps=[tmv] + ([t_out[-1]] if t_out else []))
            tb2 = p.op("v", "tensor_copy", g.mvs[0:NCH, 0:128],
                       ps_t[0:NCH, 0:128], deps=[tb1])
            mvg = getattr(g, "_mv2g", None)
            t_out.append(p.dma("y", AP(g.mvd, blk * 128, [[SEQ, NCH], [1, 128]]),
                               g.mvs[0:NCH, 0:128],
                               deps=[tb2, p.tok("t")] + ([mvg] if mvg else []),
                               ch="mv"))
    return t_out


def _attn_inner_exact(p, g, L, cfg, nkv, un):
    d = DIMS
    nc = g.nc
    SEQ, QG, NCH = d.SEQ, d.QG, d.NCH
    kvs = g.kvs[L]
    qs = g.qs[L]
    scale = cfg["scale"]
    gate = cfg["gate"]
    hid_st = []
    last_q = {}
    last_k = {}
    last_v = {}
    last_hid = {}
    kv_state = {}
    for i in range(4):
        par = i % 2
        kvi = 0 if nkv == 1 else i
        fen = p.fence()
        t_qld = p.dma("y", g.AR[:, par * SEQ:par * SEQ + SEQ],
                      AP(qs, i * 128 * SEQ, [[SEQ, 128], [1, SEQ]]),
                      deps=fen + g._tk_st +
                           [x for x in (last_q.get(par),) if x],
                      ch=f"qq{par}")
        if nkv == 4 or i == 0:
            t_kld = p.dma("y", g.AR[:, (S_K0 + par) * SEQ:(S_K0 + par + 1) * SEQ],
                          AP(kvs, kvi * 128 * SEQ, [[SEQ, 128], [1, SEQ]]),
                          deps=fen + g._tk_st +
                               [x for x in (last_k.get(par),) if x],
                          ch=f"kk{par}")
            t_vld = p.dma("y", g.AR[:, (S_V0 + par) * SEQ:(S_V0 + par + 1) * SEQ],
                          AP(kvs, (4 + kvi) * 128 * SEQ,
                             [[128, 128], [16384, SEQ // 128], [1, 128]]),
                          deps=fen + g._tk_st +
                               [x for x in (last_v.get(par),) if x],
                          ch=f"vv{par}")
            kv_state = (S_K0 + par, S_V0 + par, t_kld, t_vld)
        kslot, vslot, t_kld, t_vld = kv_state
        Kh = g.AR[:, kslot * SEQ:(kslot + 1) * SEQ]
        Vh = g.AR[:, vslot * SEQ:(vslot + 1) * SEQ]
        Qh = g.AR[:, par * SEQ:(par + 1) * SEQ]
        if gate:
            t_mv = _l3_gate(p, g, i, t_kld, t_qld, un)
            t_mvl = p.dma("y", g.MV[0:NCH, 0:SEQ], g.mvd[:],
                          deps=[t_mv[-1], p.tok("t"), p.tok("s")], ch="mv2")
            g._mv2g = p.dtok("y", "mv2")
        hslot = S_H0 + par
        Hh = g.AR[:, hslot * SEQ:(hslot + 1) * SEQ]
        t_hid_prev = last_hid.get(par)
        with nc.psum_tensor(f"xA{un}_{i}", [128, 1024], F32) as psA, \
             nc.psum_tensor(f"xT{un}_{i}", [128, 2048], BF16) as psT, \
             nc.psum_tensor(f"xO{un}_{i}", [128, 512], F32) as pO, \
             nc.psum_tensor(f"xR{un}_{i}", [1, 512], F32) as pR, \
             nc.psum_tensor(f"xB{un}_{i}", [128, 512], F32) as pB:
            t_red = [None, None]
            t_etr = [None, None]
            t_cp = None
            t_epi = None
            sp = 0

            def score_mm(qb, t, qg, dd):
                q0 = qg * 512 + qb * 128
                S = psA[:, (sp % 2) * 512:(sp % 2) * 512 + 512]
                tm = p.mm(S, Qh[:, q0:q0 + 128], Kh[:, t * 512:(t + 1) * 512],
                          start=True, stop=not gate, deps=dd, inc=not gate)
                if gate:
                    tm = p.mm(S, g.MV[0:NCH, q0:q0 + 128],
                              g.ind16[0:NCH, t * 512:(t + 1) * 512],
                              start=False, stop=True, deps=[t_mvl], inc=True)
                tt = None
                if t == qg:
                    tt = p.op("v", "tensor_add", S, S,
                              g.tri[:, qb * 512:(qb + 1) * 512], deps=[tm])
                return S, tm, tt

            for qg in range(QG):
                ntile = qg + 1
                kc_hi = 4 * ntile
                fen_s = p.tok("s")
                t_mx = [None] * 4
                for qb in range(4):
                    for t in range(ntile):
                        dd = [t_qld, t_kld] + \
                             ([t_red[sp % 2]] if t_red[sp % 2] else [])
                        S, tm, tt = score_mm(qb, t, qg, dd)
                        rdd = [tm] + ([tt] if tt else [])
                        if t == 0:
                            tr = p.op("v", "tensor_reduce",
                                      g.SM2[:, 896 + qb:897 + qb], S,
                                      AX.X, ALU.max,
                                      deps=rdd + [x for x in (fen_s,) if x])
                        else:
                            tr = p.op("v", "tensor_reduce", g.SM2[:, 904:905],
                                      S, AX.X, ALU.max, deps=rdd)
                            tr = p.op("v", "tensor_max",
                                      g.SM2[:, 896 + qb:897 + qb],
                                      g.SM2[:, 896 + qb:897 + qb],
                                      g.SM2[:, 904:905], deps=[tr])
                        t_red[sp % 2] = tr
                        sp += 1
                    t_mx[qb] = p.op("v", "tensor_scalar",
                                    g.SM2[:, 896 + qb:897 + qb],
                                    g.SM2[:, 896 + qb:897 + qb], -scale, None,
                                    ALU.mult, deps=[t_red[(sp - 1) % 2]])
                t_sum = t_pv = None
                for t in range(ntile):
                    for qb in range(4):
                        dd = [t_qld, t_kld] + \
                             ([t_red[sp % 2]] if t_red[sp % 2] else [])
                        S, tm, tt = score_mm(qb, t, qg, dd)
                        es = qb % 2
                        Etile = g.E[:, es * 512:(es + 1) * 512]
                        t_e = p.op("s", "activation", Etile, S, AF.Exp,
                                   scale=scale,
                                   bias=g.SM2[:, 896 + qb:897 + qb],
                                   deps=[tm] + ([tt] if tt else []) +
                                        [t_mx[qb]] +
                                        ([t_etr[es]] if t_etr[es] else []))
                        t_red[sp % 2] = t_e
                        sp += 1
                        ttr = None
                        for ks in range(4):
                            ttr = p.op("t", "transpose",
                                       psT[:, ks * 512 + qb * 128:
                                           ks * 512 + qb * 128 + 128],
                                       Etile[:, ks * 128:(ks + 1) * 128],
                                       g.idb[:],
                                       deps=[t_e] + ([t_cp] if t_cp else []))
                        t_etr[es] = ttr
                    t_cp = p.op("v", "tensor_copy", g.Et[:, 0:2048],
                                psT[:, 0:2048], deps=[p.tok("t")])
                    for ks in range(4):
                        kc = t * 4 + ks
                        t_sum = p.mm(pR[0:1, 0:512], g.ones_cb,
                                     g.Et[:, ks * 512:(ks + 1) * 512],
                                     start=(kc == 0), stop=(kc == kc_hi - 1),
                                     deps=[t_cp] +
                                          ([t_epi] if t_epi and kc == 0 else []))
                        t_pv = p.mm(pO[:, 0:512], Vh[:, kc * 128:(kc + 1) * 128],
                                    g.Et[:, ks * 512:(ks + 1) * 512],
                                    start=(kc == 0), stop=(kc == kc_hi - 1),
                                    deps=[t_cp, t_vld])
                t_r = p.op("v", "reciprocal", g.ROWS[0:1, 2560:3072],
                           pR[0:1, 0:512], deps=[t_sum])
                t_b = p.mm(pB[:, 0:512], g.ones_rf, g.ROWS[0:1, 2560:3072],
                           deps=[t_r] + ([t_epi] if t_epi else []))
                t_bc = p.op("s", "copy", g.SM[:, 1024:1536], pB[:, 0:512],
                            deps=[t_b])
                t_h = p.op("v", "scalar_tensor_tensor",
                           Hh[:, qg * 512:(qg + 1) * 512],
                           pO[:, 0:512], 1.0, g.SM[:, 1024:1536],
                           ALU.mult, ALU.mult,
                           deps=[t_bc, t_pv] +
                                ([p.dtok("y", "hs")] if t_hid_prev else []))
                t_hid_prev = None
                t_epi = t_h
            t_hs = p.dma("y", AP(g.agh_in[L], i * 128 * SEQ,
                                 [[SEQ, 128], [1, SEQ]]),
                         Hh, deps=[t_epi], ch="hs")
        hid_st.append(t_hs)
        last_hid[par] = True
        last_q[par] = p.tok("t")
        last_k[par] = p.tok("t")
        last_v[par] = p.tok("t")
    return hid_st


def _attn_inner(p, g, L, cfg, nkv, un):
    if cfg.get("exact"):
        return _attn_inner_exact(p, g, L, cfg, nkv, un)
    d = DIMS
    nc = g.nc
    SEQ, KC, QG, NCH = d.SEQ, d.KC, d.QG, d.NCH
    kvs = g.kvs[L]
    qs = g.qs[L]
    scale = cfg["scale"]
    hid_st = []
    last_q = {}
    last_k = {}
    last_v = {}
    last_hid = {}
    kv_state = {}
    for i in range(4):
        par = i % 2
        kvi = 0 if nkv == 1 else i
        if True:
            fen = p.fence()
            t_qld = p.dma("y", g.AR[:, par * SEQ:par * SEQ + SEQ],
                          AP(qs, i * 128 * SEQ, [[SEQ, 128], [1, SEQ]]),
                          deps=fen + g._tk_st +
                               [x for x in (last_q.get(par),) if x],
                          ch=f"qq{par}")
            if nkv == 4 or i == 0:
                t_kld = p.dma("y", g.AR[:, (S_K0 + par) * SEQ:(S_K0 + par + 1) * SEQ],
                              AP(kvs, kvi * 128 * SEQ, [[SEQ, 128], [1, SEQ]]),
                              deps=fen + g._tk_st +
                                   [x for x in (last_k.get(par),) if x],
                              ch=f"kk{par}")
                t_vld = p.dma("y", g.AR[:, (S_V0 + par) * SEQ:(S_V0 + par + 1) * SEQ],
                              AP(kvs, (4 + kvi) * 128 * SEQ,
                                 [[128, 128], [16384, SEQ // 128], [1, 128]]),
                              deps=fen + g._tk_st +
                                   [x for x in (last_v.get(par),) if x],
                              ch=f"vv{par}")
                kv_state = (S_K0 + par, S_V0 + par, t_kld, t_vld)
            kslot, vslot, t_kld, t_vld = kv_state
            Kh = g.AR[:, kslot * SEQ:(kslot + 1) * SEQ]
            Vh = g.AR[:, vslot * SEQ:(vslot + 1) * SEQ]
            Qh = g.AR[:, par * SEQ:(par + 1) * SEQ]
            t_mv = _l3_gate(p, g, i, t_kld, t_qld, un) if cfg["gate"] else []
            gate_guard = [p.tok("t"), p.tok("v"), p.tok("s")] if cfg["gate"] else []
        with nc.psum_tensor(f"aS0_{un}_{i}", [128, 1024], F32) as pS0, \
             nc.psum_tensor(f"aS1_{un}_{i}", [128, 1024], F32) as pS1, \
             nc.psum_tensor(f"aO_{un}_{i}", [128, 512], F32) as pO, \
             nc.psum_tensor(f"aR_{un}_{i}", [1, 512], F32) as pR, \
             nc.psum_tensor(f"aB_{un}_{i}", [128, 512], F32) as pB:
            pS = [pS0, pS1]
            bias_kw = (dict(bias=g.SM2[:, 896 + i:897 + i]) if scale != 16.0
                       else dict(bias=-16.0))
            hslot = S_H0 + par
            Hh = g.AR[:, hslot * SEQ:(hslot + 1) * SEQ]
            t_exp = [None, None]
            t_hid_prev = last_hid.get(par)
            t_epi = None
            for qg in range(QG):
                kc_hi = (4 * qg + 4) if cfg["causal"] else KC
                t_sum = t_pv = None
                t_mrl = None
                if cfg["gate"]:
                    nch_need = (kc_hi + 1) // 2
                    t_mrl = p.dma("y", g.MR[0:1, 0:nch_need * 512],
                                  AP(g.mvd, qg * 512, [[SEQ, nch_need], [1, 512]]),
                                  deps=[t_mv[-1], p.tok("t")], ch="mr")
                nstep = (kc_hi + 1) // 2
                for stp in range(nstep):
                    kc0 = 2 * stp
                    ncur = min(2, kc_hi - kc0)
                    sp = stp % 2
                    S2 = pS[sp]
                    jobs = []
                    for u in range(ncur):
                        kc = kc0 + u
                        jobs.append((S2[:, u * 512:(u + 1) * 512],
                                     Kh[:, kc * 128:(kc + 1) * 128],
                                     Qh[:, qg * 512:(qg + 1) * 512],
                                     True, not cfg["gate"], None))
                        if cfg["gate"]:
                            n = kc // 2
                            jobs.append((S2[:, u * 512:(u + 1) * 512],
                                         g.ones_rb,
                                         g.MR[0:1, n * 512:(n + 1) * 512],
                                         False, True, t_mrl))
                    t_score = None
                    for nj, (o_, l_, r_, st_, sp_, xd) in enumerate(jobs):
                        dd = ([t_exp[sp]] if t_exp[sp] else []) + \
                             [t_kld, t_qld] + ([xd] if xd else [])
                        t_score = p.mm(o_, l_, r_, start=st_, stop=sp_,
                                       deps=dd, inc=(nj == len(jobs) - 1))
                    t_tri = None
                    if cfg["causal"]:
                        for u in range(ncur):
                            kc = kc0 + u
                            if kc >= 4 * qg:
                                pp = kc - 4 * qg
                                t_tri = p.op("v", "tensor_add",
                                             S2[:, u * 512:(u + 1) * 512],
                                             S2[:, u * 512:(u + 1) * 512],
                                             g.tri[:, pp * 512:(pp + 1) * 512],
                                             deps=[t_score])
                    dd = [t_score] + ([t_tri] if t_tri else []) + gate_guard
                    gate_guard = []
                    t_e = p.op("s", "activation",
                               g.E[:, sp * 1024:sp * 1024 + ncur * 512],
                               S2[:, 0:ncur * 512], AF.Exp, scale=scale,
                               deps=dd, **bias_kw)
                    t_exp[sp] = t_e
                    for u in range(ncur):
                        kc = kc0 + u
                        Eh = g.E[:, sp * 1024 + u * 512:sp * 1024 + (u + 1) * 512]
                        dsum = [t_e] + ([t_epi] if t_epi and kc == 0 else [])
                        t_sum = p.mm(pR[0:1, 0:512], g.ones_cb, Eh,
                                     start=(kc == 0), stop=(kc == kc_hi - 1),
                                     deps=dsum)
                        t_pv = p.mm(pO[:, 0:512], Vh[:, kc * 128:(kc + 1) * 128],
                                    Eh, start=(kc == 0), stop=(kc == kc_hi - 1),
                                    deps=[t_e, t_vld])
                    t_exp[sp] = _later(t_exp[sp], t_pv)
                t_r = p.op("v", "reciprocal", g.ROWS[0:1, 2560:3072],
                           pR[0:1, 0:512], deps=[t_sum])
                t_b = p.mm(pB[:, 0:512], g.ones_rf, g.ROWS[0:1, 2560:3072],
                           deps=[t_r] + ([t_epi] if t_epi else []))
                t_bc = p.op("s", "copy", g.SM[:, 1024:1536], pB[:, 0:512],
                            deps=[t_b])
                t_h = p.op("v", "scalar_tensor_tensor",
                           Hh[:, qg * 512:(qg + 1) * 512],
                           pO[:, 0:512], 1.0, g.SM[:, 1024:1536],
                           ALU.mult, ALU.mult,
                           deps=[t_bc, t_pv] +
                                ([p.dtok("y", "hs")] if t_hid_prev else []))
                t_hid_prev = None
                t_epi = t_h
                t_exp[0] = _later(t_exp[0], t_h)
                t_exp[1] = _later(t_exp[1], t_h)
            t_hs = p.dma("y", AP(g.agh_in[L], i * 128 * SEQ,
                                 [[SEQ, 128], [1, SEQ]]),
                         Hh, deps=[t_epi], ch="hs")
            hid_st.append(t_hs)
            last_hid[par] = True
            last_q[par] = p.tok("t")
            last_k[par] = p.tok("t")
            last_v[par] = p.tok("t")
    return hid_st


def _attn_out(p, g, L, cfg, un):
    d = DIMS
    nc = g.nc
    SEQ, TB, TL = d.SEQ, d.TB, d.TL
    agho = g.agh_out[L]

    def hid_ap(b, hc):
        return AP(g.hidl, (b * 16 + hc) * 128 * TB, [[TB, 128], [1, TB]])

    t_hp = []
    for b in range(2):
        t_hp.append(p.dma("y" if b == 0 else "s",
                          AP(g.hidl, b * 16 * 128 * TB,
                             [[4 * 128 * TB, 4], [128 * TB, 4], [TB, 128], [1, TB]]),
                          AP(agho, (4 * b * 512) * SEQ + g.coff,
                             [[512 * SEQ, 4], [128 * SEQ, 4], [SEQ, 128], [1, TB]]),
                          deps=[g.t_agh], ch="cp"))
    g._agh2 = t_hp

    if cfg["wout"] is None:
        t_pr = None
        for b in range(2):
            for j in range(JC):
                t_l = p.dma("y", g.E[:, 0:TB], hid_ap(b, j),
                            deps=g._agh2 + ([t_pr] if t_pr else []),
                            ch="hl0")
                t_pr = p.op("v", "tensor_add",
                            g.XT[:, j * TL + b * TB:j * TL + b * TB + TB],
                            g.XT[:, j * TL + b * TB:j * TL + b * TB + TB],
                            g.E[:, 0:TB], deps=[t_l])
        return

    wo = g.inp(cfg["wout"])
    post_ln = cfg["post_ln"]
    if L == 2:
        t_bo = p.dma("y", g.smalls[:, 32:48], g.inp("l2_bO_cols")[:])
    with nc.psum_tensor(f"po{un}", [128, 1024], F32) as ps_o:
        t_wu = [None, None]
        t_y = []
        t_ld = [None, None]
        for b in range(2):
            for oc in range(JC):
                par = oc % 2
                t_w = p.dma("g", g.WR[:, par * 2048:par * 2048 + 2048],
                            AP(wo, oc * JC * 16384, [[128, 128], [16384, JC], [1, 128]]),
                            deps=[x for x in (t_wu[par],) if x], ch=f"w{par}")
                tmm = None
                for hc in range(JC):
                    hpar = hc % 2
                    t_l = p.dma("y", g.E[:, hpar * 512:hpar * 512 + TB],
                                hid_ap(b, hc),
                                deps=g._agh2 + ([t_ld[hpar]] if t_ld[hpar] else []),
                                ch=f"hl{hpar}")
                    tmm = p.mm(ps_o[:, par * 512:par * 512 + TB],
                               g.WR[:, par * 2048 + hc * 128:par * 2048 + (hc + 1) * 128],
                               g.E[:, hpar * 512:hpar * 512 + TB],
                               start=(hc == 0), stop=(hc == JC - 1),
                               deps=[t_w, t_l], inc=True)
                    t_ld[hpar] = tmm
                t_wu[par] = tmm
                if post_ln:
                    ty = p.op("s", "copy",
                              g.AR[:, oc * TL + b * TB:oc * TL + b * TB + TB],
                              ps_o[:, par * 512:par * 512 + TB], deps=[tmm])
                else:
                    if L == 2:
                        ty = p.op("v", "tensor_scalar", g.SM2[:, 0:TB],
                                  ps_o[:, par * 512:par * 512 + TB],
                                  g.smalls[:, 32 + oc:33 + oc], None, ALU.add,
                                  deps=[tmm, t_bo])
                    else:
                        ty = p.op("v", "tensor_copy", g.SM2[:, 0:TB],
                                  ps_o[:, par * 512:par * 512 + TB], deps=[tmm])
                    ty = p.op("v", "tensor_add",
                              g.XT[:, oc * TL + b * TB:oc * TL + b * TB + TB],
                              g.XT[:, oc * TL + b * TB:oc * TL + b * TB + TB],
                              g.SM2[:, 0:TB], deps=[ty])
                t_wu[par] = _later(t_wu[par], ty)
                t_y.append(ty)
    if post_ln:
        _post_ln_add(p, g, t_y, un)


def _post_ln_add(p, g, y_toks, un):
    d = DIMS
    nc = g.nc
    TL = d.TL
    t_g = p.dma("y", g.smalls[:, 16:32], g.inp("l0_gout_cols")[:])
    with nc.psum_tensor(f"pl{un}s", [1, 1024], F32) as ps_s, \
         nc.psum_tensor(f"pl{un}q", [1, 1024], F32) as ps_q, \
         nc.psum_tensor(f"pl{un}b", [128, 1024], F32) as ps_b:
        halves = [(h0, min(512, TL - h0)) for h0 in range(0, TL, 512)]
        t_s = t_q = None
        sq_use = None
        for j in range(JC):
            Y = g.AR[:, j * TL:(j + 1) * TL]
            sq = g.SM2.bitcast(BF16)[:, 0:TL]
            tsq = p.op("s", "square", sq, Y,
                       deps=y_toks + ([sq_use] if sq_use else []))
            for h0, hw in halves:
                t_s = p.mm(ps_s[0:1, h0:h0 + hw], g.ones_cb, Y[:, h0:h0 + hw],
                           start=(j == 0), stop=(j == JC - 1), deps=y_toks)
                t_q = p.mm(ps_q[0:1, h0:h0 + hw], g.ones_cb, sq[:, h0:h0 + hw],
                           start=(j == 0), stop=(j == JC - 1), deps=[tsq])
            sq_use = t_q
        R = g.ROWS
        mu = R[0:1, 0:TL]
        rr = R[0:1, TL:2 * TL]
        t1 = p.op("v", "tensor_scalar", mu, ps_s[0:1, 0:TL], 1.0 / D, None,
                  ALU.mult, deps=[t_s])
        t2 = p.op("v", "tensor_mul", rr, mu, mu, deps=[t1])
        t3 = p.op("v", "scalar_tensor_tensor", rr, ps_q[0:1, 0:TL], 1.0 / D,
                  rr, ALU.mult, ALU.subtract, deps=[t_q, t2])
        t4 = p.op("s", "activation", rr, rr, AF.Sqrt, bias=EPS_LN, deps=[t3])
        t5 = p.op("v", "reciprocal", rr, rr, deps=[t4])
        t6 = p.op("v", "scalar_tensor_tensor", mu, mu, -1.0, rr, ALU.mult,
                  ALU.mult, deps=[t5])
        tb1 = p.mm(ps_b[:, 0:TL], g.ones_rf, rr, deps=[t5])
        tv1 = p.op("s", "copy", g.SM[:, 0:TL], ps_b[:, 0:TL], deps=[tb1])
        tb2 = p.mm(ps_b[:, 0:TL], g.ones_rf, mu, deps=[t6, tv1])
        tv2 = p.op("s", "copy", g.SM[:, TL:2 * TL], ps_b[:, 0:TL], deps=[tb2])
        for j in range(JC):
            Y = g.AR[:, j * TL:(j + 1) * TL]
            ta = p.op("v", "tensor_mul", g.SM2[:, 0:TL], Y, g.SM[:, 0:TL],
                      deps=[tv1])
            tb = p.op("v", "tensor_add", g.SM2[:, 0:TL], g.SM2[:, 0:TL],
                      g.SM[:, TL:2 * TL], deps=[ta, tv2])
            tc = p.op("v", "tensor_scalar", g.SM2[:, 0:TL], g.SM2[:, 0:TL],
                      g.smalls[:, 16 + j:17 + j], None, ALU.mult,
                      deps=[tb, t_g])
            p.op("v", "tensor_add", g.XT[:, j * TL:(j + 1) * TL],
                 g.XT[:, j * TL:(j + 1) * TL], g.SM2[:, 0:TL], deps=[tc])



def _part_reduce_to_row(p, g, src, W, psT, dst_off, op, dep):
    nb = W // 128
    cols = []
    for q in range(nb):
        t3 = p.op("t", "transpose", psT[:, 0:128],
                  src[:, q * 128:(q + 1) * 128], g.idf[:],
                  deps=[dep] + cols[-1:])
        t4 = p.op("v", "tensor_reduce", g.SM2[:, 896 + q:897 + q],
                  psT[:, 0:128], AX.X, op, deps=[t3])
        cols.append(t4)
    t5 = p.op("t", "transpose", psT[0:nb, 0:128],
              g.SM2[:, 896:896 + nb], g.idf[:, 0:128], deps=cols)
    t6 = p.op("v", "tensor_copy", g.mrow_s[0:nb, 0:128], psT[0:nb, 0:128],
              deps=[t5])
    t7 = p.dma("y", AP(g.rows_d, dst_off, [[128, nb], [1, 128]]),
               g.mrow_s[0:nb, 0:128], deps=[t6], ch="pr")
    return t7


def _row_bcast(p, g, row_ap, psB, dst_sm, W, dep):
    ones = g.ones_rb if row_ap.dtype == BF16 else g.ones_rf
    t1 = None
    for h0 in range(0, W, 512):
        hw = min(512, W - h0)
        t1 = p.mm(psB[:, h0:h0 + hw], ones, row_ap[0:1, h0:h0 + hw],
                  deps=([dep] if dep else []) + ([t1] if t1 else []))
    t2 = p.op("s", "copy", dst_sm, psB[:, 0:W], deps=[t1])
    return t2


def _wquant_ternary(p, g, wname, n_tiles, mode):
    d = DIMS
    nc = g.nc
    wsrc = g.inp(wname)
    qdst = g.qw[wname]
    EPS = 1e-5
    base = g.wq_slot
    g.wq_slot += 3
    un = p.phase_n
    TPS = 8
    shard = n_tiles // NCORE
    if shard * 16384 not in g._wqoff:
        g._wqoff[shard * 16384] = nc.partition_id() * (shard * 16384)
    woff = g._wqoff[shard * 16384]
    nslab = (shard + TPS - 1) // TPS
    WRf = g.WR.bitcast(F32)
    with nc.psum_tensor(f"wq{un}", [128, 512], F32) as psT:
        with p.phase(f"wq_{wname}_m"):
            fen = p.fence()
            qsl = getattr(g, "_qsf_loads", [])
            t_cpy = p.dma("g", AP(g.qsf, 0, [[128, shard * 128], [1, 128]]),
                          AP(wsrc, woff, [[128, shard * 128], [1, 128]]),
                          deps=qsl, ch="wqc")
            t_ld = [None, None]
            acc = None
            for s in range(nslab):
                par = s % 2
                nt = min(TPS, shard - s * TPS)
                t_l = p.dma("y", WRf[:, par * 1024:par * 1024 + nt * 128],
                            AP(g.qsf, s * TPS * 16384,
                               [[128, 128], [16384, nt], [1, 128]]),
                            deps=[t_cpy] +
                                 ([t_ld[par]] if t_ld[par] else fen),
                            ch=f"wq{par}")
                t_r = p.op("v", "tensor_reduce", g.SM2[:, 512 + s % 4:513 + s % 4],
                           WRf[:, par * 1024:par * 1024 + nt * 128],
                           AX.X, ALU.add, apply_absolute_value=True, deps=[t_l])
                if acc is None:
                    acc = p.op("v", "tensor_copy", g.SM2[:, 520:521],
                               g.SM2[:, 512 + s % 4:513 + s % 4], deps=[t_r])
                else:
                    acc = p.op("v", "tensor_add", g.SM2[:, 520:521],
                               g.SM2[:, 520:521], g.SM2[:, 512 + s % 4:513 + s % 4],
                               deps=[t_r])
                t_ld[par] = t_r
            t5 = p.op("t", "transpose", psT[0:1, 0:128], g.SM2[:, 520:521],
                      g.idf[:, 0:128], deps=[acc])
            t6 = p.op("v", "tensor_reduce", g.ROWS[0:1, 2048:2049],
                      psT[0:1, 0:128], AX.X, ALU.add, deps=[t5])
            arg = getattr(g, "_wqar_g", None)
            t6z = p.op("v", "memset", g.ROWS[0:1, 2049:2064], 0.0, deps=[t6])
            t6d = p.dma("y", g.wqar_i[0:1, 0:16], g.ROWS[0:1, 2048:2064],
                        deps=[t6, t6z] + ([arg] if arg else []), ch="wqr")
            p.wait("g", t6d)
            t6c = p.cc("AllReduce", ALU.add, [g.wqar_i.ap()], [g.wqar_o.ap()])
            g._wqar_g = t6c
            t6l = p.dma("y", g.ROWS[0:1, 2048:2049], g.wqar_o[0:1, 0:1],
                        deps=[t6c], ch="wqr")
            N = n_tiles * 16384
            t7 = p.op("v", "tensor_scalar", g.ROWS[0:1, 2048:2049],
                      g.ROWS[0:1, 2048:2049], 1.0 / N, None, ALU.mult, deps=[t6l])
            if mode == "l0":
                t8 = p.op("v", "tensor_scalar", g.ROWS[0:1, 2049:2050],
                          g.ROWS[0:1, 2048:2049], EPS, None, ALU.max, deps=[t7])
            else:
                t8 = p.op("v", "tensor_scalar", g.ROWS[0:1, 2049:2050],
                          g.ROWS[0:1, 2048:2049], EPS, None, ALU.add, deps=[t7])
            t9 = p.op("v", "tensor_scalar", g.ROWS[0:1, 2050:2051],
                      g.ROWS[0:1, 2049:2050], 0.5, None, ALU.mult, deps=[t8])
            ta = p.op("v", "tensor_scalar", g.ROWS[0:1, 2051:2052],
                      g.ROWS[0:1, 2050:2051], -1.0, None, ALU.mult, deps=[t9])
            for ii, rofs in enumerate((2049, 2050, 2051)):
                tb = p.mm(psT[:, 128 + ii:129 + ii], g.ones_rf,
                          g.ROWS[0:1, rofs:rofs + 1], deps=[ta])
                p.op("s", "copy", g.smalls[:, base + ii:base + ii + 1],
                     psT[:, 128 + ii:129 + ii], deps=[tb])
        with p.phase(f"wq_{wname}_q"):
            fen = p.fence()
            t_ld = [None, None]
            t_st = getattr(g, "_wq_st", None)
            qg_prev = getattr(g, "_qsh_guard", None)
            qsf_rd = [None, None]
            for s in range(nslab):
                par = s % 2
                nt = min(TPS, shard - s * TPS)
                t_l = p.dma("y", WRf[:, par * 1024:par * 1024 + nt * 128],
                            AP(g.qsf, s * TPS * 16384,
                               [[128, 128], [16384, nt], [1, 128]]),
                            deps=([t_ld[par]] if t_ld[par] else fen), ch=f"wq{par}")
                qsf_rd[par] = t_l
                Wv = WRf[:, par * 1024:par * 1024 + nt * 128]
                A = g.E[:, 0:nt * 128]
                B = g.SM[:, 0:nt * 128]
                t1 = p.op("v", "tensor_scalar", A, Wv,
                          g.smalls[:, base + 1:base + 2], None, ALU.is_gt,
                          deps=[t_l])
                t2 = p.op("v", "tensor_scalar", B, Wv,
                          g.smalls[:, base + 2:base + 3], None, ALU.is_gt,
                          deps=[t_l])
                t3 = p.op("v", "scalar_tensor_tensor",
                          g.E[:, 1024:1024 + nt * 128],
                          A, -1.0, B, ALU.add, ALU.add,
                          deps=[t1, t2] + ([t_st] if t_st else []))
                t_st = p.dma("y", AP(g.qshd, s * TPS * 16384,
                                     [[128, 128], [16384, nt], [1, 128]]),
                             g.E[:, 1024:1024 + nt * 128]
                             .rearrange("p (n f) -> p n f", n=nt),
                             deps=[t3] + ([qg_prev] if qg_prev else []),
                             ch="wqs")
                qg_prev = None
                t_ld[par] = t3
            g._wq_st = t_st
            g._qsf_loads = [x for x in qsf_rd if x]
            p.wait("g", t_st)
            t_ag = p.cc("AllGather", ALU.bypass,
                        [AP(g.qshd, 0, [[128, shard * 128], [1, 128]])],
                        [qdst.ap()])
            g._qsh_guard = t_ag
    return t_ag, base


def _emit_xn(p, g, tv1, tv2):
    d = DIMS
    TL = d.TL
    toks = []
    for j in range(JC):
        ta = p.op("v", "tensor_mul", g.SM2[:, 0:TL],
                  g.XT[:, j * TL:(j + 1) * TL], g.SM[:, 0:TL], deps=[tv1])
        tb = p.op("v", "tensor_add", g.AR[:, j * TL:(j + 1) * TL],
                  g.SM2[:, 0:TL], g.SM[:, TL:2 * TL], deps=[ta, tv2])
        toks.append(tb)
    return toks


def _ffn_ternary(p, g, L):
    d = DIMS
    nc = g.nc
    TL, SEQ = d.TL, d.SEQ
    EPS = 1e-5
    if L == 0:
        FC = 44
        wg_n, wu_n, wd_n = "l0_w1", "l0_w3", "l0_w2"
    else:
        FC = 64
        wg_n, wu_n, wd_n = "l3_wg", "l3_wu", "l3_wd"
    if wg_n not in g.wq_done:
        for nm, nt in ((wg_n, FC * JC), (wu_n, FC * JC), (wd_n, JC * FC)):
            g.wq_done[nm] = _wquant_ternary(p, g, nm, nt, "l0" if L == 0 else "l3")
    cm_g = g.smalls[:, g.wq_done[wg_n][1]:g.wq_done[wg_n][1] + 1]
    cm_u = g.smalls[:, g.wq_done[wu_n][1]:g.wq_done[wu_n][1] + 1]
    cm_d = g.smalls[:, g.wq_done[wd_n][1]:g.wq_done[wd_n][1] + 1]
    un = L * 100 + 50

    with p.phase(f"L{L}f_ln"):
        tv1, tv2, t_r = _ln_rows(p, g, rows_for_ffn=True)
        with nc.psum_tensor(f"fb{un}", [128, 1024], F32) as psB, \
             nc.psum_tensor(f"ft{un}", [128, 512], F32) as psT:
            trm = None
            fen0 = p.fence()
            for j in range(JC):
                t1 = p.op("v", "tensor_sub", g.SM2[:, 0:TL],
                          g.XT[:, j * TL:(j + 1) * TL], g.SM[:, TL:2 * TL],
                          deps=[tv2] + fen0)
                ta_ = p.op("v", "scalar_tensor_tensor", g.SM2[:, 0:TL],
                           g.SM2[:, 0:TL], -1.0, g.SM2[:, 0:TL], ALU.mult,
                           ALU.max, deps=[t1])
                if trm is None:
                    trm = p.op("v", "tensor_copy", g.RO[:, 0:TL],
                               g.SM2[:, 0:TL], deps=[ta_])
                else:
                    trm = p.op("v", "tensor_max", g.RO[:, 0:TL],
                               g.RO[:, 0:TL], g.SM2[:, 0:TL], deps=[ta_])
            t_m = _part_reduce_to_row(p, g, g.RO[:, 0:TL], TL, psT, 0,
                                      ALU.max, trm)
            t_ml = p.dma("y", g.ROWS[0:1, 2 * TL:3 * TL],
                         AP(g.rows_d, 0, [[1, 1], [1, TL]]), deps=[t_m],
                         ch="pr2")
            R = g.ROWS
            mu = R[0:1, 0:TL]; rr = R[0:1, TL:2 * TL]; mm_ = R[0:1, 2 * TL:3 * TL]
            if L == 0:
                t1 = p.op("v", "tensor_mul", mm_, mm_, rr, deps=[t_ml, t_r])
                t2 = p.op("v", "tensor_scalar", mm_, mm_, EPS, None, ALU.max,
                          deps=[t1])
                t3 = p.op("v", "reciprocal", g.SM2[0:1, 0:TL], mm_, deps=[t2])
                t4 = p.op("v", "tensor_scalar", g.SM2[0:1, 0:TL],
                          g.SM2[0:1, 0:TL], 127.0, None, ALU.mult, deps=[t3])
                t5 = p.op("v", "tensor_mul", g.SM2[0:1, 0:TL],
                          g.SM2[0:1, 0:TL], rr, deps=[t4])
                t6 = p.op("v", "tensor_scalar", mm_, mm_, 1.0 / 127.0, None,
                          ALU.mult, deps=[t5])
            else:
                t1 = p.op("v", "tensor_mul", mm_, mm_, rr, deps=[t_ml, t_r])
                t2 = p.op("v", "tensor_scalar", mm_, mm_, 1.0 / 127.0, EPS,
                          ALU.mult, ALU.add, deps=[t1])
                t3 = p.op("v", "reciprocal", g.SM2[0:1, 0:TL], mm_, deps=[t2])
                t5 = p.op("v", "tensor_mul", g.SM2[0:1, 0:TL],
                          g.SM2[0:1, 0:TL], rr, deps=[t3])
            tbc = _row_bcast(p, g, g.SM2[0:1, 0:TL], psB, g.SM[:, 0:TL], TL, t5)
            tqx = []
            for j in range(JC):
                q1 = p.op("v", "tensor_sub", g.SM2[:, 0:TL],
                          g.XT[:, j * TL:(j + 1) * TL], g.SM[:, TL:2 * TL],
                          deps=[tv2, tbc])
                q2 = p.op("v", "tensor_mul", g.SM2[:, 0:TL], g.SM2[:, 0:TL],
                          g.SM[:, 0:TL], deps=[q1])
                q3 = p.op("v", "tensor_scalar", g.SM2[:, 0:TL], g.SM2[:, 0:TL],
                          MAGIC, None, ALU.add, deps=[q2])
                q4 = p.op("v", "tensor_scalar", g.SM2[:, 0:TL], g.SM2[:, 0:TL],
                          MAGIC, None, ALU.subtract, deps=[q3])
                q5 = p.op("v", "tensor_copy", g.AR[:, j * TL:(j + 1) * TL],
                          g.SM2[:, 0:TL], deps=[q4])
                tqx.append(q5)
            t_dq = p.dma("y", AP(g.rows_d, TL, [[1, 1], [1, TL]]), mm_,
                         deps=[t5] if L else [t6], ch="pr")
            g._t_dq = t_dq
            g._tqx = tqx[-1]

    wg_q, wu_q, wd_q = g.qw[wg_n], g.qw[wu_n], g.qw[wd_n]
    t_wq_done = [g.wq_done[wg_n][0], g.wq_done[wu_n][0], g.wq_done[wd_n][0]]
    TSW = min(256, TL)
    NSL = TL // TSW
    HOFF = 4 * SEQ
    for ts in range(NSL):
        with p.phase(f"L{L}f_s{ts}"):
            with nc.psum_tensor(f"fg{un}_{ts}", [128, 1024], F32) as psG, \
                 nc.psum_tensor(f"fr{un}_{ts}", [128, 1024], F32) as psR, \
                 nc.psum_tensor(f"fp{un}_{ts}", [128, 512], F32) as psT2:
                fen = p.fence()
                t_dql = p.dma("y", g.ROWS[0:1, 2 * TL:2 * TL + TSW],
                              AP(g.rows_d, TL + ts * TSW, [[1, 1], [1, TSW]]),
                              deps=[g._t_dq] + fen, ch="pr2")
                tbq = _row_bcast(p, g, g.ROWS[0:1, 2 * TL:2 * TL + TSW],
                                 psR, g.SM[:, 1536:1536 + TSW], TSW, t_dql)
                t_w = [None, None]
                runm = None
                prev_act = None
                for sub, (wq_t, cmcol) in enumerate(((wg_q, cm_g), (wu_q, cm_u))):
                    for fc in range(FC):
                        par = fc % 2
                        t_wl = p.dma("g", g.WR[:, par * 2048:par * 2048 + 2048],
                                     AP(wq_t, fc * JC * 16384,
                                        [[128, 128], [16384, JC], [1, 128]]),
                                     deps=([t_w[par]] if t_w[par] else
                                           fen + t_wq_done),
                                     ch=f"w{(sub * FC + fc) % 4}")
                        tmm = None
                        for j in range(JC):
                            tmm = p.mm(psG[:, par * 512:par * 512 + TSW],
                                       g.WR[:, par * 2048 + j * 128:par * 2048 + (j + 1) * 128],
                                       g.AR[:, j * TL + ts * TSW:j * TL + ts * TSW + TSW],
                                       start=(j == 0), stop=(j == JC - 1),
                                       deps=[t_wl, g._tqx], inc=(j == JC - 1))
                        if sub == 0:
                            e1 = p.op("v", "tensor_mul", g.SM2[:, 0:TSW],
                                      psG[:, par * 512:par * 512 + TSW],
                                      g.SM[:, 1536:1536 + TSW],
                                      deps=[tmm, tbq] +
                                           ([prev_act] if prev_act else []))
                            e2 = p.op("s", "activation",
                                      g.AR[:, HOFF + fc * TSW:HOFF + fc * TSW + TSW],
                                      g.SM2[:, 0:TSW], AF.Silu,
                                      scale=cmcol, deps=[e1])
                            prev_act = e2
                            t_w[par] = e2
                        else:
                            e1 = p.op("v", "tensor_mul", g.SM2[:, 0:TSW],
                                      psG[:, par * 512:par * 512 + TSW],
                                      g.SM[:, 1536:1536 + TSW],
                                      deps=[tmm, tbq] +
                                           ([prev_act] if prev_act else []))
                            prev_act = None
                            e2 = p.op("v", "tensor_mul",
                                      g.AR[:, HOFF + fc * TSW:HOFF + fc * TSW + TSW],
                                      g.AR[:, HOFF + fc * TSW:HOFF + fc * TSW + TSW],
                                      g.SM2[:, 0:TSW], deps=[e1])
                            hslv = g.AR[:, HOFF + fc * TSW:HOFF + fc * TSW + TSW]
                            tab = p.op("v", "scalar_tensor_tensor",
                                       g.SM2[:, 512:512 + TSW], hslv, -1.0,
                                       hslv, ALU.mult, ALU.max, deps=[e2])
                            if runm is None:
                                runm = p.op("v", "tensor_copy", g.RO[:, 0:TSW],
                                            g.SM2[:, 512:512 + TSW], deps=[tab])
                            else:
                                runm = p.op("v", "tensor_max", g.RO[:, 0:TSW],
                                            g.RO[:, 0:TSW],
                                            g.SM2[:, 512:512 + TSW], deps=[tab])
                            t_w[par] = e2
                t_hm = _part_reduce_to_row(p, g, g.RO[:, 0:TSW], TSW, psT2,
                                           2 * TL, ALU.max, runm)
                t_hml = p.dma("y", g.ROWS[0:1, 2 * TL:2 * TL + TSW],
                              AP(g.rows_d, 2 * TL, [[1, 1], [1, TSW]]),
                              deps=[t_hm], ch="pr2")
                hm = g.ROWS[0:1, 2 * TL:2 * TL + TSW]
                if L == 0:
                    h1 = p.op("v", "tensor_scalar", hm, hm,
                              cm_u[0:1, :], None, ALU.mult, deps=[t_hml])
                    h2 = p.op("v", "tensor_scalar", hm, hm, EPS, None, ALU.max,
                              deps=[h1])
                    h3 = p.op("v", "reciprocal", g.SM2[0:1, 0:TSW], hm, deps=[h2])
                    h4 = p.op("v", "tensor_scalar", g.SM2[0:1, 0:TSW],
                              g.SM2[0:1, 0:TSW], 127.0, None, ALU.mult, deps=[h3])
                    h5 = p.op("v", "tensor_scalar", g.SM2[0:1, 0:TSW],
                              g.SM2[0:1, 0:TSW], cm_u[0:1, :], None, ALU.mult,
                              deps=[h4])
                    h6 = p.op("v", "tensor_scalar", hm, hm, 1.0 / 127.0, None,
                              ALU.mult, deps=[h5])
                    h7 = p.op("v", "tensor_scalar", hm, hm, cm_d[0:1, :], None,
                              ALU.mult, deps=[h6])
                else:
                    h1 = p.op("v", "tensor_scalar", hm, hm,
                              cm_u[0:1, :], None, ALU.mult, deps=[t_hml])
                    h2 = p.op("v", "tensor_scalar", hm, hm, 1.0 / 127.0, EPS,
                              ALU.mult, ALU.add, deps=[h1])
                    h3 = p.op("v", "reciprocal", g.SM2[0:1, 0:TSW], hm, deps=[h2])
                    h5 = p.op("v", "tensor_scalar", g.SM2[0:1, 0:TSW],
                              g.SM2[0:1, 0:TSW], cm_u[0:1, :], None, ALU.mult,
                              deps=[h3])
                    h7 = p.op("v", "tensor_scalar", hm, hm, cm_d[0:1, :], None,
                              ALU.mult, deps=[h5])
                tbc2 = _row_bcast(p, g, g.SM2[0:1, 0:TSW], psR,
                                  g.SM[:, 1536:1536 + TSW], TSW, h7)
                tbc3 = _row_bcast(p, g, hm, psR, g.SM[:, 1024:1024 + TSW],
                                  TSW, tbc2)
                tqh = None
                for fc in range(FC):
                    hsl = g.AR[:, HOFF + fc * TSW:HOFF + fc * TSW + TSW]
                    q1 = p.op("v", "tensor_mul", g.SM2[:, 0:TSW], hsl,
                              g.SM[:, 1536:1536 + TSW], deps=[tbc2])
                    q2 = p.op("v", "tensor_scalar", g.SM2[:, 0:TSW],
                              g.SM2[:, 0:TSW], MAGIC, None, ALU.add, deps=[q1])
                    q3 = p.op("v", "tensor_scalar", g.SM2[:, 0:TSW],
                              g.SM2[:, 0:TSW], MAGIC, None, ALU.subtract,
                              deps=[q2])
                    tqh = p.op("v", "tensor_copy", hsl, g.SM2[:, 0:TSW],
                               deps=[q3])
                t_w = [None, None]
                for oc in range(JC):
                    par = oc % 2
                    tmm = None
                    ngrp = (FC + 15) // 16
                    for gi in range(ngrp):
                        nw16 = min(16, FC - gi * 16)
                        t_wl = p.dma("g", g.WR[:, par * 2048:par * 2048 + nw16 * 128],
                                     AP(wd_q, (oc * FC + gi * 16) * 16384,
                                        [[128, 128], [16384, nw16], [1, 128]]),
                                     deps=([t_w[par]] if gi == 0 and t_w[par]
                                           else ([tmm] if tmm else [tqh])),
                                     ch=f"w{oc % 4}")
                        for jj in range(min(16, FC - gi * 16)):
                            fc = gi * 16 + jj
                            tmm = p.mm(psG[:, par * 512:par * 512 + TSW],
                                       g.WR[:, par * 2048 + jj * 128:par * 2048 + (jj + 1) * 128],
                                       g.AR[:, HOFF + fc * TSW:HOFF + fc * TSW + TSW],
                                       start=(fc == 0), stop=(fc == FC - 1),
                                       deps=[t_wl, tqh], inc=True)
                    e1 = p.op("v", "tensor_mul", g.SM2[:, 0:TSW],
                              psG[:, par * 512:par * 512 + TSW],
                              g.SM[:, 1024:1024 + TSW], deps=[tmm, tbc3])
                    e2 = p.op("v", "tensor_add",
                              g.XT[:, oc * TL + ts * TSW:oc * TL + ts * TSW + TSW],
                              g.XT[:, oc * TL + ts * TSW:oc * TL + ts * TSW + TSW],
                              g.SM2[:, 0:TSW], deps=[e1])
                    t_w[par] = e2


def _ffn_layer(p, g, L):
    if L in (0, 3):
        _ffn_ternary(p, g, L)
    elif L == 1:
        _ffn_moe(p, g)
    else:
        _ffn_fp8(p, g)


def _ffn_moe(p, g):
    d = DIMS
    nc = g.nc
    TL, SEQ = d.TL, d.SEQ
    un = 150
    TSW = min(512, TL)
    NSL = TL // TSW
    HOFF = 4 * SEQ
    nblk = TL // 128
    with p.phase("L1f_ln"):
        tv1, tv2, _ = _ln_rows(p, g)
        xn_toks = _emit_xn(p, g, tv1, tv2)
        g._txn = xn_toks[-1]
    _moe_gate2(p, g, un)
    cmb_d = g.cmb_d
    for e in range(9):
        shared = (e == 8)
        FC = 22 if shared else 11
        tsw = TL if not shared else min(512, TL)
        nsl = TL // tsw
        for ts in range(nsl):
            with p.phase(f"L1f_e{e}_{ts}"):
                nhalf = (tsw + 511) // 512
                ctx3 = ExitStack()
                psGh = [ctx3.enter_context(
                            nc.psum_tensor(f"me{un}_{e}_{ts}_{hh}",
                                           [128, 1024], F32))
                        for hh in range(nhalf)]
                psB = ctx3.enter_context(
                    nc.psum_tensor(f"mb{un}_{e}_{ts}", [128, tsw], F32))
                with ctx3:
                    fen = p.fence()
                    if not shared:
                        t_cl = p.dma("y", g.cmbr[0:1, 0:tsw],
                                     AP(cmb_d, e * TL + ts * tsw, [[1, 1], [1, tsw]]),
                                     deps=fen + [g._t_cmb], ch="pr2")
                        t_cb = _row_bcast(p, g, g.cmbr[0:1, 0:tsw], psB,
                                          g.SM[:, 1024:1024 + tsw], tsw, t_cl)
                    t_w = [None, None]
                    prev_act = None
                    for sub in range(2):
                        for fc in range(FC):
                            par = fc % 2
                            base_t = (sub * FC + fc) * JC
                            srcw = (g.inp("l1_sgu") if shared else
                                    g.inp("l1_guw"))
                            off = (0 if shared else e * 22 * JC * 16384) + base_t * 16384
                            t_wl = p.dma("g", g.WR[:, par * 2048:par * 2048 + 2048],
                                         AP(srcw, off, [[128, 128], [16384, JC], [1, 128]]),
                                         deps=([t_w[par]] if t_w[par] else fen),
                                         ch=f"w{(sub * FC + fc) % 4}")
                            tmm = None
                            for j in range(JC):
                                for h0 in range(0, tsw, 512):
                                    hw = min(512, tsw - h0)
                                    tmm = p.mm(psGh[h0 // 512][:, par * 512:par * 512 + hw],
                                               g.WR[:, par * 2048 + j * 128:par * 2048 + (j + 1) * 128],
                                               g.AR[:, j * TL + ts * tsw + h0:
                                                    j * TL + ts * tsw + h0 + hw],
                                               start=(j == 0), stop=(j == JC - 1),
                                               deps=[t_wl, g._txn], inc=True)
                            hsl = g.AR[:, HOFF + fc * tsw:HOFF + fc * tsw + tsw]
                            if sub == 0:
                                for h0 in range(0, tsw, 512):
                                    hw = min(512, tsw - h0)
                                    e2 = p.op("s", "activation",
                                              hsl[:, h0:h0 + hw],
                                              psGh[h0 // 512][:, par * 512:par * 512 + hw],
                                              AF.Silu,
                                              deps=[tmm] + ([prev_act] if prev_act else []))
                                    prev_act = e2
                                t_w[par] = e2
                            else:
                                e2 = None
                                for h0 in range(0, tsw, 512):
                                    hw = min(512, tsw - h0)
                                    pgs = psGh[h0 // 512][:, par * 512:par * 512 + hw]
                                    if not shared:
                                        e1 = p.op("v", "tensor_mul",
                                                  g.SM2[:, h0:h0 + hw], pgs,
                                                  g.SM[:, 1024 + h0:1024 + h0 + hw],
                                                  deps=[tmm, t_cb] +
                                                       ([e2] if e2 else []))
                                    else:
                                        e1 = p.op("v", "tensor_copy",
                                                  g.SM2[:, h0:h0 + hw], pgs,
                                                  deps=[tmm] + ([e2] if e2 else []))
                                    e2 = p.op("v", "tensor_mul",
                                              hsl[:, h0:h0 + hw], hsl[:, h0:h0 + hw],
                                              g.SM2[:, h0:h0 + hw], deps=[e1])
                                t_w[par] = e2
                    tact = t_w[0] if t_w[0] else t_w[1]
                    tact2 = p.tok("v")
                    wdn = g.inp("l1_sdown") if shared else g.inp("l1_down")
                    for oc in range(JC):
                        par = oc % 2
                        tmm = None
                        ngrp = (FC + 15) // 16
                        for gi in range(ngrp):
                            nw16 = min(16, FC - gi * 16)
                            off = ((0 if shared else e * JC * 11 * 16384) +
                                   (oc * FC + gi * 16) * 16384)
                            t_wl = p.dma("g", g.WR[:, par * 2048:par * 2048 + nw16 * 128],
                                         AP(wdn, off, [[128, 128], [16384, nw16], [1, 128]]),
                                         deps=([t_w[par]] if gi == 0 and t_w[par]
                                               else ([tmm] if tmm else [tact2])),
                                         ch=f"w{oc % 4}")
                            for jj in range(nw16):
                                fc = gi * 16 + jj
                                for h0 in range(0, tsw, 512):
                                    hw = min(512, tsw - h0)
                                    tmm = p.mm(psGh[h0 // 512][:, par * 512:par * 512 + hw],
                                               g.WR[:, par * 2048 + jj * 128:par * 2048 + (jj + 1) * 128],
                                               g.AR[:, HOFF + fc * tsw + h0:
                                                    HOFF + fc * tsw + h0 + hw],
                                               start=(fc == 0), stop=(fc == FC - 1),
                                               deps=[t_wl, tact2], inc=True)
                        e2 = None
                        for h0 in range(0, tsw, 512):
                            hw = min(512, tsw - h0)
                            e1 = p.op("v", "tensor_copy", g.SM2[:, h0:h0 + hw],
                                      psGh[h0 // 512][:, par * 512:par * 512 + hw],
                                      deps=[tmm] + ([e2] if e2 else []))
                            e2 = p.op("v", "tensor_add",
                                      g.XT[:, oc * TL + ts * tsw + h0:
                                           oc * TL + ts * tsw + h0 + hw],
                                      g.XT[:, oc * TL + ts * tsw + h0:
                                           oc * TL + ts * tsw + h0 + hw],
                                      g.SM2[:, h0:h0 + hw], deps=[e1])
                        t_w[par] = e2


def _moe_gate2(p, g, un):
    d = DIMS
    nc = g.nc
    TL = d.TL
    nblk = TL // 128
    with p.phase("L1f_gate2"):
        with nc.psum_tensor(f"mgg{un}", [8, 1024], F32) as psL, \
             nc.psum_tensor(f"mtt{un}", [128, 512], F32) as psT:
            fen = p.fence()
            t_gw = p.dma("g", g.cvt8[:, 0:JC * 8],
                         AP(g.inp("l1_gatew"), 0, [[8, 128], [1024, JC], [1, 8]]),
                         deps=fen, ch="w0")
            tmm = None
            for j in range(JC):
                for h0 in range(0, TL, 512):
                    hw = min(512, TL - h0)
                    tmm = p.mm(psL[0:8, h0:h0 + hw],
                               g.cvt8[:, j * 8:(j + 1) * 8],
                               g.AR[:, j * TL + h0:j * TL + h0 + hw],
                               start=(j == 0), stop=(j == JC - 1),
                               deps=[t_gw, g._txn], inc=True)
            t_ev = p.op("s", "copy", g.SM[0:8, 0:TL], psL[0:8, 0:TL], deps=[tmm])
            tts = []
            for blk in range(nblk):
                t1 = p.op("t", "transpose", psT[:, 0:8],
                          g.SM[0:8, blk * 128:(blk + 1) * 128],
                          g.idf[0:8, 0:8], deps=[t_ev] + tts[-1:])
                tts.append(p.op("v", "tensor_copy",
                                g.SM2[:, blk * 8:blk * 8 + 8], psT[:, 0:8],
                                deps=[t1]))
            G = g.SM2[:, 0:nblk * 8]
            Gv = G.rearrange("p (n c) -> p n c", n=nblk)
            M = g.RO[:, 0:nblk]
            Mv = M.rearrange("p n -> p n ()").broadcast_to([128, nblk, 8])
            W2 = g.SM[:, 0:nblk * 8]
            W2v = W2.rearrange("p (n c) -> p n c", n=nblk)
            t1 = p.op("v", "tensor_reduce", M, Gv, AX.X, ALU.max, deps=tts)
            t2 = p.op("v", "tensor_tensor", W2v, Gv, Mv, ALU.subtract, deps=[t1])
            t3 = p.op("s", "activation", W2, W2, AF.Exp, deps=[t2])
            t4 = p.op("v", "tensor_reduce", M, W2v, AX.X, ALU.add, deps=[t3])
            t5 = p.op("v", "reciprocal", M, M, deps=[t4])
            t6 = p.op("v", "tensor_tensor", G.rearrange("p (n c) -> p n c", n=nblk),
                      W2v, Mv, ALU.mult, deps=[t5])
            t7 = p.op("v", "tensor_reduce", M, Gv, AX.X, ALU.max, deps=[t6])
            t8 = p.op("v", "tensor_tensor", W2v, Gv, Mv, ALU.is_ge, deps=[t7])
            t9 = p.op("v", "scalar_tensor_tensor", W2, W2, NEG, G, ALU.mult,
                      ALU.add, deps=[t8])
            ta = p.op("v", "tensor_reduce", M, W2v, AX.X, ALU.max, deps=[t9])
            tb = p.op("v", "tensor_tensor", W2v, Gv, Mv, ALU.is_ge, deps=[ta])
            tc = p.op("v", "tensor_mul", G, G, W2, deps=[tb])
            tst = []
            for blk in range(nblk):
                t1 = p.op("t", "transpose", psT[0:8, 0:128],
                          g.SM2[:, blk * 8:blk * 8 + 8], g.idf[:, 0:128],
                          deps=[tc] + tst[-1:])
                t2 = p.op("v", "tensor_copy",
                          g.cvt8b[0:8, blk * 128:(blk + 1) * 128],
                          psT[0:8, 0:128], deps=[t1])
                tst.append(t2)
            g._t_cmb = p.dma("y", g.cmb_d.ap(), g.cvt8b[0:8, 0:TL],
                             deps=tst, ch="pr")


def _ffn_fp8(p, g):
    d = DIMS
    nc = g.nc
    TL, SEQ = d.TL, d.SEQ
    un = 250
    E4MAX = 448.0
    with p.phase("L2f_ln"):
        tv1, tv2, t_r = _ln_rows(p, g, rows_for_ffn=True)
        with nc.psum_tensor(f"qb{un}", [128, 1024], F32) as psB, \
             nc.psum_tensor(f"qt{un}", [128, 512], F32) as psT:
            trm = None
            fen0 = p.fence()
            for j in range(JC):
                t1 = p.op("v", "tensor_sub", g.SM2[:, 0:TL],
                          g.XT[:, j * TL:(j + 1) * TL], g.SM[:, TL:2 * TL],
                          deps=[tv2] + fen0)
                ta_ = p.op("v", "scalar_tensor_tensor", g.SM2[:, 0:TL],
                           g.SM2[:, 0:TL], -1.0, g.SM2[:, 0:TL], ALU.mult,
                           ALU.max, deps=[t1])
                if trm is None:
                    trm = p.op("v", "tensor_copy", g.RO[:, 0:TL],
                               g.SM2[:, 0:TL], deps=[ta_])
                else:
                    trm = p.op("v", "tensor_max", g.RO[:, 0:TL],
                               g.RO[:, 0:TL], g.SM2[:, 0:TL], deps=[ta_])
            t_m = _part_reduce_to_row(p, g, g.RO[:, 0:TL], TL, psT, 0,
                                      ALU.max, trm)
            t_ml = p.dma("y", g.ROWS[0:1, 2 * TL:3 * TL],
                         AP(g.rows_d, 0, [[1, 1], [1, TL]]), deps=[t_m],
                         ch="pr2")
            mm_ = g.ROWS[0:1, 2 * TL:3 * TL]
            rr = g.ROWS[0:1, TL:2 * TL]
            t1 = p.op("v", "tensor_mul", mm_, mm_, rr, deps=[t_ml, t_r])
            t2 = p.op("v", "tensor_reduce", g.ROWS[0:1, 3200:3201],
                      mm_, AX.X, ALU.max, deps=[t1])
            t3 = p.op("v", "memset", g.mrow_s[0:1, 0:128], 0.0, deps=[t2])
            t4 = p.op("v", "tensor_copy", g.mrow_s[0:1, 0:1],
                      g.ROWS[0:1, 3200:3201], deps=[t3])
            t5 = p.dma("y", g.ar_in.ap(), g.mrow_s[0:1, 0:128], deps=[t4],
                       ch="pr")
            p.wait("g", t5)
            t_cc = p.cc("AllReduce", ALU.max, [g.ar_in.ap()], [g.ar_out.ap()])
            t6 = p.dma("y", g.ROWS[0:1, 3200:3201],
                       AP(g.ar_out, 0, [[1, 1], [1, 1]]), deps=[t_cc], ch="pr2")
            am = g.ROWS[0:1, 3200:3201]
            t7 = p.op("v", "tensor_scalar", am, am, 1.0 / E4MAX, 1e-12,
                      ALU.mult, ALU.max, deps=[t6])
            t8 = p.op("v", "reciprocal", g.ROWS[0:1, 3201:3202],
                      am, deps=[t7])
            tb = p.mm(psT[:, 0:1], g.ones_rf, g.ROWS[0:1, 3201:3202],
                      deps=[t8])
            tb2 = p.op("s", "copy", g.smalls[:, 62:63], psT[:, 0:1], deps=[tb])
            t9 = p.op("v", "tensor_scalar", g.SM2[0:1, 0:TL], rr,
                      g.ROWS[0:1, 3201:3202], 0.5, ALU.mult, ALU.mult,
                      deps=[t8])
            tbc = _row_bcast(p, g, g.SM2[0:1, 0:TL], psB, g.SM[:, 0:TL], TL, t9)
            tqx = None
            t_q8 = [None, None]
            for j in range(JC):
                par = j % 2
                q1 = p.op("v", "tensor_sub", g.SM2[:, 0:TL],
                          g.XT[:, j * TL:(j + 1) * TL], g.SM[:, TL:2 * TL],
                          deps=[tv2, tbc])
                q2 = p.op("v", "tensor_mul", g.SM2[:, 0:TL], g.SM2[:, 0:TL],
                          g.SM[:, 0:TL], deps=[q1])
                q3 = p.op("v", "tensor_scalar", g.SM2[:, 0:TL], g.SM2[:, 0:TL],
                          224.0, -224.0, ALU.min, ALU.max, deps=[q2])
                q4 = p.op("v", "tensor_copy", g.X8[:, par * 1024:par * 1024 + TL],
                          g.SM2[:, 0:TL],
                          deps=[q3] + ([t_q8[par]] if t_q8[par] else []))
                tqx = p.dma("y", AP(g.xq8d, j * 128 * TL, [[TL, 128], [1, TL]]),
                            g.X8[:, par * 1024:par * 1024 + TL], deps=[q4],
                            ch=f"x8{par}")
                t_q8[par] = tqx
            g._tqx8 = [t for t in t_q8 if t]
        wf = g.inp("l2_wf")
    with p.phase("L2f_wm"):
        with nc.psum_tensor(f"qw{un}", [128, 512], F32) as psT:
            fen = p.fence()
            t_ld = [None, None]
            acc = None
            for s in range(JC):
                par = s % 2
                t_l = p.dma("y", g.WR[:, par * 2048:par * 2048 + 2048],
                            AP(wf, s * JC * 16384, [[128, 128], [16384, JC], [1, 128]]),
                            deps=([t_ld[par]] if t_ld[par] else fen),
                            ch=f"sl{par}")
                t_r2 = p.op("v", "tensor_reduce", g.SM2[:, 512:513],
                            g.WR[:, par * 2048:par * 2048 + 2048],
                            AX.X, ALU.max, apply_absolute_value=True,
                            deps=[t_l])
                if acc is None:
                    acc = p.op("v", "tensor_copy", g.SM2[:, 520:521],
                               g.SM2[:, 512:513], deps=[t_r2])
                else:
                    acc = p.op("v", "tensor_max", g.SM2[:, 520:521],
                               g.SM2[:, 520:521], g.SM2[:, 512:513],
                               deps=[t_r2])
                t_ld[par] = t_r2
            t5 = p.op("t", "transpose", psT[0:1, 0:128], g.SM2[:, 520:521],
                      g.idf[:, 0:128], deps=[acc])
            t6 = p.op("v", "tensor_reduce", g.ROWS[0:1, 2048:2049],
                      psT[0:1, 0:128], AX.X, ALU.max, deps=[t5])
            t7 = p.op("v", "tensor_scalar", g.ROWS[0:1, 2048:2049],
                      g.ROWS[0:1, 2048:2049], 1.0 / E4MAX, 1e-12,
                      ALU.mult, ALU.max, deps=[t6])
            t8 = p.op("v", "reciprocal", g.ROWS[0:1, 2049:2050],
                      g.ROWS[0:1, 2048:2049], deps=[t7])
            t8 = p.op("v", "tensor_scalar", g.ROWS[0:1, 2049:2050],
                      g.ROWS[0:1, 2049:2050], 0.5, None, ALU.mult, deps=[t8])
            for col, rofs in ((61, 2049),):
                tb = p.mm(psT[:, 1 + col - 61:2 + col - 61], g.ones_rf,
                          g.ROWS[0:1, rofs:rofs + 1], deps=[t8])
                p.op("s", "copy", g.smalls[:, col:col + 1],
                     psT[:, 1 + col - 61:2 + col - 61], deps=[tb])
            ta2 = p.op("v", "tensor_scalar", g.ROWS[0:1, 2051:2052],
                       g.ROWS[0:1, 3200:3201],
                       g.ROWS[0:1, 2048:2049], 4.0, ALU.mult, ALU.mult,
                       deps=[t8])
            tb3 = p.mm(psT[:, 8:9], g.ones_rf, g.ROWS[0:1, 2051:2052],
                       deps=[ta2])
            g._t_sc = p.op("s", "copy", g.smalls[:, 60:61], psT[:, 8:9],
                           deps=[tb3])
    with p.phase("L2f_mm"):
        ctx4 = ExitStack()
        psYs = [ctx4.enter_context(nc.psum_tensor(f"qm{un}_{ii}", [128, 512], F32))
                for ii in range(4)]
        with ctx4:
            fen = p.fence()
            t_w = [None, None]
            t_x8 = [None, None]
            prev_q = None
            prev_ev = getattr(g, "_fp8_ev", None)
            for oq in range(8):
                tqs = []
                for o4 in range(2):
                    oc = oq * 2 + o4
                    par = oc % 2
                    t_wl = p.dma("y", g.WR[:, par * 2048:par * 2048 + 2048],
                                 AP(wf, oc * JC * 16384,
                                    [[128, 128], [16384, JC], [1, 128]]),
                                 deps=([t_w[par]] if t_w[par] else fen),
                                 ch=f"sl{par}")
                    tq = None
                    for hh in range(2):
                        q1 = p.op("v", "tensor_scalar", g.SM2[:, 0:1024],
                                  g.WR[:, par * 2048 + hh * 1024:
                                       par * 2048 + hh * 1024 + 1024],
                                  g.smalls[:, 61:62], None, ALU.mult,
                                  deps=[t_wl] + ([prev_q] if prev_q else []))
                        q2 = p.op("v", "tensor_scalar", g.SM2[:, 0:1024],
                                  g.SM2[:, 0:1024], 224.0, -224.0,
                                  ALU.min, ALU.max, deps=[q1])
                        tq = p.op("v", "tensor_copy",
                                  g.W8[:, o4 * 2048 + hh * 1024:
                                       o4 * 2048 + hh * 1024 + 1024],
                                  g.SM2[:, 0:1024], deps=[q2])
                    t_w[par] = tq
                    tqs.append(tq)
                tmm = None
                for j in range(JC):
                    pj = j % 2
                    t_x8[pj] = p.dma("y", g.X8[:, pj * 1024:pj * 1024 + TL],
                                     AP(g.xq8d, j * 128 * TL, [[TL, 128], [1, TL]]),
                                     deps=g._tqx8 +
                                          ([t_x8[pj]] if t_x8[pj] else []) +
                                          ([tmm] if tmm else []) +
                                          ([prev_q] if prev_q and j < 2 else []),
                                     ch=f"x8{pj}")
                    for o4 in range(2):
                        for h0 in range(0, TL, 512):
                            hw = min(512, TL - h0)
                            tmm = p.mm(psYs[o4 * 2 + h0 // 512][:, 0:hw],
                                       g.W8[:, o4 * 2048 + j * 128:
                                            o4 * 2048 + (j + 1) * 128],
                                       g.X8[:, pj * 1024 + h0:pj * 1024 + h0 + hw],
                                       start=(j == 0), stop=(j == JC - 1),
                                       deps=[tqs[o4], t_x8[pj]] +
                                            ([prev_ev] if prev_ev and j == 0
                                             else []),
                                       inc=True)
                prev_q = tmm
                prev_ev = None
                elast = None
                for o4 in range(2):
                    oc = oq * 2 + o4
                    for h0 in range(0, TL, 512):
                        hw = min(512, TL - h0)
                        e1 = p.op("v", "tensor_scalar",
                                  g.SM2.bitcast(BF16)[:, 0:hw],
                                  psYs[o4 * 2 + h0 // 512][:, 0:hw],
                                  g.smalls[:, 60:61], None,
                                  ALU.mult, deps=[tmm, g._t_sc])
                        e2 = p.op("v", "tensor_add",
                                  g.XT[:, oc * TL + h0:oc * TL + h0 + hw],
                                  g.XT[:, oc * TL + h0:oc * TL + h0 + hw],
                                  g.SM2.bitcast(BF16)[:, 0:hw], deps=[e1])
                        elast = e2
                prev_q = elast
                g._fp8_ev = elast
def _build(nc, ctx, map0, layers, n_sub):
    p = P(nc)
    p.setup(ctx)
    map0 = {**map0, "cvals": np.zeros((128, 16), np.float32)}
    g = _declare(nc, map0, layers)
    _alloc_sbuf(g, ctx, layers)
    g.boff = (nc.partition_id() // 4) * DIMS.TB
    g.coff = nc.partition_id() * DIMS.TB
    _load_consts(p, g, layers)
    if (1 in layers) or (3 in layers):
        _bld_rope_tables(p, g)
    with p.phase("ones"):
        p.op("v", "memset", g.smalls[:, 63:64], 1.0)

    def _prequant(L):
        FC = 44 if L == 0 else 64
        names = (("l0_w1", "l0_w3", "l0_w2") if L == 0 else
                 ("l3_wg", "l3_wu", "l3_wd"))
        for nm, nt in ((names[0], FC * JC), (names[1], FC * JC),
                       (names[2], JC * FC)):
            if nm not in g.wq_done:
                g.wq_done[nm] = _wquant_ternary(p, g, nm, nt,
                                                "l0" if L == 0 else "l3")
    sub = 0
    for L in layers:
        if sub >= n_sub:
            break
        if L in (0, 3) and sub + 1 < n_sub:
            _prequant(L)
        if not SKIP_ATTN:
            _attn_layer(p, g, L)
        sub += 1
        if sub >= n_sub:
            break
        _ffn_layer(p, g, L)
        sub += 1

    with p.phase("out"):
        t_o = p.dma("y", _featmaj_ap(g.out, DIMS.TL),
                    g.XT[:, 0:JC * DIMS.TL], deps=p.fence(), ch="o")
        p.wait("y", t_o)
    return p, g


def _run_group(maps, x_cur, layers, n_sub, sim=False):
    nc = bass.Bass("TRN2", target_bir_lowering=False)
    with ExitStack() as ctx:
        p, g = _build(nc, ctx, {**maps[0], "x_in": x_cur[0]}, layers, n_sub)
    nc.finalize()
    cv = np.zeros((128, 16), np.float32)
    for idx, v in enumerate(g.cvals):
        cv[:, idx] = v
    in_maps = []
    for r in range(NCORE):
        m = {k: maps[r][k] for k in g.h if k != "x_in" and k != "cvals"}
        m["x_in"] = x_cur[r]
        m["cvals"] = cv
        in_maps.append(m)
    g_keep[0] = g
    if sim:
        from concourse.bass_interp import MultiCoreSim
        ms = MultiCoreSim(nc, num_cores=NCORE, num_workers=1)
        for r in range(NCORE):
            for k, v in in_maps[r].items():
                ms.cores[r].tensor(k)[:] = v
            ms.cores[r].tensor("partition_id")[:] = np.array([[r]], np.uint32)
        ms.simulate(check_with_hw=False)
        return [np.asarray(ms.cores[r].tensor("x_out")).copy()
                for r in range(NCORE)]
    res = run_bass_kernel_spmd(nc, in_maps, core_ids=list(range(NCORE)))
    g_keep.append(res)
    return [np.asarray(res.results[r]["x_out"]) for r in range(NCORE)]


SIM = False
g_keep = [None]

def kernel(**inputs):
    d = DIMS
    maps = _prep_host(inputs)
    x_cur = [m.pop("x_in") for m in maps]
    sub_left = UPTO
    for grp in SPLIT:
        if sub_left <= 0:
            break
        n_sub = min(sub_left, 2 * len(grp))
        x_cur = _run_group(maps, x_cur, grp, n_sub, sim=SIM)
        sub_left -= n_sub
    out = np.empty((d.B, d.SEQ, D), np.float32)
    for r in range(NCORE):
        for b in range(d.B):
            out[b, r * d.TB:(r + 1) * d.TB, :] = \
                x_cur[r][:, b * d.TB:(b + 1) * d.TB].T
    return out


# version 14
# speedup vs baseline: 1.0026x; 1.0026x over previous
import math
import numpy as np
import ml_dtypes
from contextlib import ExitStack
import concourse.bass as bass
import concourse.mybir as mybir
from concourse.bass_utils import run_bass_kernel_spmd

F32 = mybir.dt.float32
BF16 = mybir.dt.bfloat16
FP8 = mybir.dt.float8e4
AF = mybir.ActivationFunctionType
ALU = mybir.AluOpType
AX = mybir.AxisListType
bfdt = ml_dtypes.bfloat16

NCORE = 8
D = 2048
JC = D // 128
NEG = -1.0e30
MAGIC = float(3 * 2 ** 22)
EPS_LN = 1e-5
RT128 = 1.0 / math.sqrt(128.0)

class _Dims:
    def __init__(self, seq=4096):
        self.SEQ = seq
        self.B = 2
        self.TB = seq // NCORE
        self.TL = 2 * self.TB
        self.KC = seq // 128
        self.QG = seq // 512
        self.NCH = seq // 256
DIMS = _Dims(4096)

def _set_dims(seq):
    global DIMS
    DIMS = _Dims(seq)

SPLIT = [[0, 1, 2, 3]]
UPTO = 8
SKIP_ATTN = False
DEBUG_DUMP = False



class P:
    ENGS = ("t", "v", "s", "g", "y")

    def __init__(self, nc):
        self.nc = nc
        self.sems = {}
        self.cnt = {}
        self.obs = {e: {} for e in self.ENGS}
        self.queues = None
        self.phase_n = 0

    def setup(self, ctx):
        self.ctx = ctx
        for name in ("c_t", "c_v", "c_s", "c_g", "d_c"):
            self.sems[name] = ctx.enter_context(self.nc.semaphore(name))
            self.cnt[name] = 0

    def _dsem(self, e, ch):
        name = f"d{e}_{ch}"
        if name not in self.sems:
            self.sems[name] = self.ctx.enter_context(self.nc.semaphore(name))
            self.cnt[name] = 0
        return name

    def _chain(self, e):
        return {"t": "c_t", "v": "c_v", "s": "c_s", "g": "c_g"}[e]

    class _Phase:
        def __init__(self, p, name):
            self.p, self.name = p, name
        def __enter__(self):
            assert self.p.queues is None
            self.p.queues = {e: [] for e in P.ENGS}
            return self
        def __exit__(self, et, ev, tb):
            if et is not None:
                self.p.queues = None
                return False
            p = self.p
            q = p.queues
            p.queues = None
            if any(q[e] for e in P.ENGS):
                with p.nc.Block(f"ph{p.phase_n}_{self.name}") as blk:
                    for e, attr in (("t", "tensor"), ("v", "vector"),
                                    ("s", "scalar"), ("g", "gpsimd"),
                                    ("y", "sync")):
                        if q[e]:
                            def run(eng, _ops=q[e]):
                                for f in _ops:
                                    f(eng)
                            getattr(blk, attr)(run)
            p.phase_n += 1
            for e in P.ENGS:
                for cs in ("c_t", "c_v", "c_s", "c_g"):
                    p.obs[e][cs] = p.cnt[cs]
            return False

    def phase(self, name=""):
        return P._Phase(self, name)

    def waits(self, e, toks):
        best = {}
        for t in toks:
            if t is None:
                continue
            s, v = t
            if best.get(s, -1) < v:
                best[s] = v
        for s, v in best.items():
            self.wait(e, (s, v))

    def wait(self, e, tok):
        if tok is None:
            return
        sem, val = tok
        if self.obs[e].get(sem, 0) >= val:
            return
        self.obs[e][sem] = val
        h = self.sems[sem]
        self.queues[e].append(lambda eng, h=h, v=val: eng.wait_ge(h, v))

    def op(self, e, meth, *args, deps=(), inc=True, **kwargs):
        from concourse.bass_types import TensorHandle
        args = tuple(a.ap() if isinstance(a, TensorHandle) else a for a in args)
        kwargs = {k: (v.ap() if isinstance(v, TensorHandle) else v)
                  for k, v in kwargs.items()}
        if e in ("v", "s", "g"):
            inc = True
            deps = list(deps) + [self.tok(e)]
        self.waits(e, deps)
        cs = self._chain(e)
        if inc:
            self.cnt[cs] += 1
            h = self.sems[cs]
            def f(eng, m=meth, a=args, k=kwargs, h=h):
                getattr(eng, m)(*a, **k).then_inc(h, 1)
            self.queues[e].append(f)
            return (cs, self.cnt[cs])
        def f2(eng, m=meth, a=args, k=kwargs):
            getattr(eng, m)(*a, **k)
        self.queues[e].append(f2)
        return (cs, self.cnt[cs])

    def tok(self, e):
        cs = self._chain(e)
        return (cs, self.cnt[cs])

    def fence(self):
        return [self.tok("t"), self.tok("v"), self.tok("s"), self.tok("g")]

    def dma(self, e, out, in_, deps=(), ch="c", **kwargs):
        self.waits(e, deps)
        sem = self._dsem(e, ch)
        self.cnt[sem] += 16
        h = self.sems[sem]
        def f(eng, o=out, i=in_, h=h, k=kwargs):
            try:
                eng.dma_start(o, i, **k).then_inc(h, 16)
            except Exception:
                import sys
                print("DMA FAIL out=", o, "in=", i, file=sys.stderr)
                raise
        self.queues[e].append(f)
        return (sem, self.cnt[sem])

    def dtok(self, e, ch):
        sem = self._dsem(e, ch)
        return (sem, self.cnt[sem])

    def cc(self, kind, aop, ins, outs, deps=()):
        self.waits("g", deps)
        self.cnt["d_c"] += 1
        h = self.sems["d_c"]
        def f(eng, k=kind, o=aop, i=ins, u=outs, h=h):
            eng.collective_compute(k, o, replica_groups=[list(range(NCORE))],
                                  ins=i, outs=u).then_inc(h)
        self.queues["g"].append(f)
        return ("d_c", self.cnt["d_c"])

    def mm(self, out, lhsT, rhs, start=True, stop=True, deps=(), inc=True):
        from concourse.bass_types import TensorHandle
        if isinstance(out, TensorHandle):
            out = out.ap()
        if isinstance(rhs, TensorHandle):
            rhs = rhs.ap()
        if isinstance(lhsT, TensorHandle):
            lhsT = lhsT.ap()
        n = out.shape[-1]
        lim = 512 if rhs.dtype == F32 else 1024
        if n <= lim:
            return self.op("t", "matmul", out, lhsT, rhs, start=start,
                           stop=stop, deps=deps, inc=inc)
        t = None
        nch = (n + lim - 1) // lim
        for c in range(nch):
            c0 = c * lim
            cw = min(lim, n - c0)
            t = self.op("t", "matmul", out[:, c0:c0 + cw], lhsT,
                        rhs[:, c0:c0 + cw], start=start, stop=stop,
                        deps=deps if c == 0 else (), inc=(inc and c == nch - 1))
        return t

    def reset_sems(self):
        with self.phase("flush"):
            for name in list(self.sems):
                if name.startswith("dy") or name.startswith("ds"):
                    self.wait("y", (name, self.cnt[name]))
                elif name.startswith("dg"):
                    self.wait("g", (name, self.cnt[name]))
            self.wait("g", ("d_c", self.cnt["d_c"]))
        with self.phase("semclr"):
            for name in self.sems:
                h = self.sems[name]
                self.queues["y"].append(lambda eng, h=h: eng.sem_clear(h))
        for name in self.sems:
            self.cnt[name] = 0
        for e in self.ENGS:
            self.obs[e] = {}


def _later(a, b):
    if a is None:
        return b
    if b is None:
        return a
    if a[0] == b[0]:
        return a if a[1] >= b[1] else b
    return b



def _tile_lhsT(Wkm):
    K, M = Wkm.shape
    return np.ascontiguousarray(
        Wkm.reshape(K // 128, 128, M // 128, 128).transpose(2, 0, 1, 3))


def _prep_host(inp):
    d = DIMS
    B, S, TB = d.B, d.SEQ, d.TB
    f32 = np.float32
    C = np.ascontiguousarray
    g = lambda k: np.asarray(inp[k], dtype=f32)
    sh = {}

    sh["ident_f"] = np.eye(128, dtype=f32)
    sh["ident_b"] = np.eye(128, dtype=f32).astype(bfdt)
    sh["ones_col_b"] = np.ones((128, 1), dtype=bfdt)
    sh["ones_row_f"] = np.ones((1, 128), dtype=f32)
    sh["ones_row_b"] = np.ones((1, 128), dtype=bfdt)
    tri = np.zeros((4, 128, 512), dtype=f32)
    for r in range(4):
        allow = np.arange(512)[None, :] <= (r * 128 + np.arange(128))[:, None]
        tri[r] = np.where(allow, 0.0, NEG)
    sh["tri4"] = C(tri.transpose(1, 0, 2).reshape(128, 4 * 512).astype(bfdt))
    ind = np.zeros((d.NCH, S), dtype=f32)
    for t in range(S // 512):
        ind[2 * t, t * 512:t * 512 + 256] = 1.0
        ind[2 * t + 1, t * 512 + 256:(t + 1) * 512] = 1.0
    sh["ind16"] = C(ind.astype(bfdt))
    sh["d2_row"] = (2.0 * (np.arange(128) % 64)).astype(f32).reshape(1, 128)
    sh["pos_row"] = np.arange(S, dtype=f32).reshape(1, S)
    NCH = d.NCH
    pos = np.arange(S)
    ch_end = (np.arange(NCH) + 1) * 256
    gmask = np.where(pos[None, :] >= ch_end[:, None], 0.0, NEG).astype(f32)
    cur = (pos // 256)[None, :] == np.arange(NCH)[:, None]
    sh["l3_gmask"] = C(np.where(cur, -NEG, gmask).astype(bfdt))

    sh["l0_gout_cols"] = C(g("L0_attn_g_out").reshape(JC, 128).T)
    sh["l0_gnorm_cols"] = C(g("L0_attn_g_norm").reshape(JC, 128).T)
    sh["l0_wout"] = _tile_lhsT(g("L0_attn_Wout"))
    sh["l0_w1"] = _tile_lhsT(C(g("L0_ffn_w1").T))
    sh["l0_w3"] = _tile_lhsT(C(g("L0_ffn_w3").T))
    sh["l0_w2"] = _tile_lhsT(C(g("L0_ffn_w2").T))
    sh["l1_gatew"] = C(g("L1_ffn_gate_weight").T.reshape(JC, 128, 8))
    sh["l1_guw"] = np.stack([_tile_lhsT(C(g("L1_ffn_gate_up_weights")[e].T))
                             for e in range(8)])
    sh["l1_down"] = np.stack([_tile_lhsT(C(g("L1_ffn_down_weights")[e].T))
                              for e in range(8)])
    sh["l1_sgu"] = _tile_lhsT(C(g("L1_ffn_shared_gate_up").T))
    sh["l1_sdown"] = _tile_lhsT(C(g("L1_ffn_shared_down").T))
    sh["l2_wo"] = C(g("L2_attn_W_O").reshape(16, 128, JC, 128).transpose(2, 0, 1, 3))
    sh["l2_bO_cols"] = C(g("L2_attn_b_O").reshape(JC, 128).T)
    sh["l2_wf"] = C(_tile_lhsT(C(np.asarray(inp["L2_ffn_weight"]).T)))
    sh["l3_wo"] = _tile_lhsT(g("L3_attn_wo"))
    sh["l3_wg"] = _tile_lhsT(C(g("L3_ffn_w_gate").T))
    sh["l3_wu"] = _tile_lhsT(C(g("L3_ffn_w_up").T))
    sh["l3_wd"] = _tile_lhsT(C(g("L3_ffn_w_down").T))

    Wq0 = g("L0_attn_Wq")
    Wkv0 = g("L0_attn_Wkv")
    wqkvT = C(g("L1_attn_w_qkv").T)
    bqkv = g("L1_attn_b_qkv")
    WQ2, WK2, WV2 = g("L2_attn_W_Q"), g("L2_attn_W_K"), g("L2_attn_W_V")
    bQ2, bK2, bV2 = g("L2_attn_b_Q"), g("L2_attn_b_K"), g("L2_attn_b_V")
    wq3, wk3, wv3 = g("L3_attn_wq"), g("L3_attn_wk"), g("L3_attn_wv")
    x = np.asarray(inp["L0_attn_x"], dtype=f32)

    def qkv_pack(qt, kt, vt):
        nkv = len(kt)
        out = np.empty((JC, 4 + 2 * nkv, 128, 128), f32)
        for j in range(JC):
            for i in range(4):
                out[j, i] = qt[i][j]
            for i in range(nkv):
                out[j, 4 + i] = kt[i][j]
                out[j, 4 + nkv + i] = vt[i][j]
        return C(out)

    maps = []
    for r in range(NCORE):
        m = dict(sh)
        xs = np.concatenate([x[b, r * TB:(r + 1) * TB, :].T for b in range(B)], 1)
        m["x_in"] = C(xs)
        heads = [4 * (r % 4) + i for i in range(4)]
        mkv = r % 4
        tl = lambda W: _tile_lhsT(W)[0]
        m["l0_qkv"] = qkv_pack([tl(Wq0[:, h * 128:(h + 1) * 128]) for h in heads],
                               [_tile_lhsT(Wkv0)[0]], [_tile_lhsT(Wkv0)[1]])
        m["l1_qkv"] = qkv_pack(
            [tl(wqkvT[:, h * 384:h * 384 + 128]) for h in heads],
            [tl(wqkvT[:, h * 384 + 128:h * 384 + 256]) for h in heads],
            [tl(wqkvT[:, h * 384 + 256:h * 384 + 384]) for h in heads])
        m["l2_qkv"] = qkv_pack([tl(WQ2[h]) for h in heads],
                               [tl(WK2[h]) for h in heads],
                               [tl(WV2[h]) for h in heads])
        m["l3_qkv"] = qkv_pack([tl(wq3[:, h * 128:(h + 1) * 128]) for h in heads],
                               [tl(wk3[:, mkv * 128:(mkv + 1) * 128])],
                               [tl(wv3[:, mkv * 128:(mkv + 1) * 128])])
        m["l1_bq"] = C(np.stack([bqkv[h * 384:h * 384 + 128] for h in heads], 1))
        m["l1_bk"] = C(np.stack([bqkv[h * 384 + 128:h * 384 + 256] for h in heads], 1))
        m["l1_bv"] = C(np.stack([bqkv[h * 384 + 256:h * 384 + 384] for h in heads], 1))
        m["l2_bq"] = C(bQ2[heads].T)
        m["l2_bk"] = C(bK2[heads].T)
        m["l2_bv"] = C(bV2[heads].T)
        maps.append(m)
    return maps



class G:
    pass


def _declare(nc, map0, layers):
    g = G()
    g.nc = nc
    g.map0 = map0
    d = DIMS
    g.h = {}
    def inp(k):
        if k not in g.h:
            v = map0[k]
            dt = F32 if v.dtype == np.float32 else BF16
            g.h[k] = nc.dram_tensor(k, list(v.shape), dt, kind="ExternalInput")
        return g.h[k]
    g.inp = inp
    g.out = nc.dram_tensor("x_out", [D, d.TL], F32, kind="ExternalOutput")
    if (1 in layers) or (3 in layers):
        g.rope_sin = nc.dram_tensor("sc_sin", [128, d.SEQ], F32)
        g.rope_cos = nc.dram_tensor("sc_cos", [128, d.SEQ], F32)
    g.agx_in = {}; g.agx_out = {}; g.kvs = {}; g.agh_in = {}; g.agh_out = {}
    g.qs = {}
    for L in layers:
        g.agx_in[L] = nc.dram_tensor(f"agxi{L}", [D, d.TL], BF16)
        g.agx_out[L] = nc.dram_tensor(f"agxo{L}", [NCORE * D, d.TL], BF16)
        kk = "ExternalOutput" if DEBUG_DUMP else "Internal"
        g.kvs[L] = nc.dram_tensor(f"kvs{L}", [8 * 128, d.SEQ], BF16, kind=kk)
        g.qs[L] = nc.dram_tensor(f"qs{L}", [4 * 128, d.SEQ], BF16, kind=kk)
        g.agh_in[L] = nc.dram_tensor(f"aghi{L}", [4 * 128, d.SEQ], BF16)
        if DEBUG_DUMP:
            g.dbg_hid = nc.dram_tensor(f"dbg_hid{L}", [4 * 128, d.SEQ], BF16,
                                       kind="ExternalOutput")
            g.dbg_bias = nc.dram_tensor(f"dbg_bias{L}", [128, 32], F32,
                                        kind="ExternalOutput")
        g.agh_out[L] = nc.dram_tensor(f"agho{L}", [NCORE * 4 * 128, d.SEQ], BF16)
    if 3 in layers:
        g.mvd = nc.dram_tensor("mvd", [d.NCH, d.SEQ], BF16)
    g.axl = nc.dram_tensor("axl", [NCORE * D, d.TB], BF16)
    g.hidl = nc.dram_tensor("hidl", [2 * 16 * 128, d.TB], BF16)
    g.rows_d = nc.dram_tensor("rows_d", [1, 8192], F32)
    if 1 in layers:
        g.cmb_d = nc.dram_tensor("cmb_d", [8, d.TL], BF16)
    g.wq_slot = 48
    g.wq_done = {}
    g.qw = {}
    for nm, nt in (("l0_w1", 44 * JC), ("l0_w3", 44 * JC), ("l0_w2", JC * 44),
                   ("l3_wg", 64 * JC), ("l3_wu", 64 * JC), ("l3_wd", JC * 64)):
        if (nm.startswith("l0") and 0 in layers) or (nm.startswith("l3") and 3 in layers):
            g.qw[nm] = nc.dram_tensor("q_" + nm, [nt * 128, 128], BF16)
    if (0 in layers) or (3 in layers):
        g.qshd = nc.dram_tensor("qshard", [16384, 128], BF16)
        g.qsf = nc.dram_tensor("qsf", [3 * 16384, 128], F32)
        g.wqar_i = nc.dram_tensor("wqar_i", [1, 16], F32)
        g.wqar_o = nc.dram_tensor("wqar_o", [1, 16], F32)
        g._wqoff = {}
    if 2 in layers:
        g.ar_in = nc.dram_tensor("ar_in", [1, 128], F32)
        g.ar_out = nc.dram_tensor("ar_out", [1, 128], F32)
        g.xq8d = nc.dram_tensor("xq8d", [D, d.TL], FP8)
    return g


def _alloc_sbuf(g, ctx, layers):
    nc = g.nc
    d = DIMS
    sb = lambda n, shp, dt=F32: ctx.enter_context(nc.sbuf_tensor(n, shp, dt))
    g.cst = sb("cst", [128, 16])
    g.cvals = []
    def cconst(val):
        val = float(val)
        if val in g.cvals:
            return
        idx = len(g.cvals)
        g.cvals.append(val)
        nc.const_aps.register(F32, val, g.cst[:, idx:idx + 1])
    g.cconst = cconst
    g.XT = sb("XT", [128, JC * d.TL])
    g.ARW = max(8 * d.SEQ, 4 * d.SEQ + 16384, 12 * JC * 128)
    g.AR = sb("AR", [128, g.ARW], BF16)
    g.SM = sb("SMa", [128, 2048])
    g.SM2 = sb("SM2", [128, 1024])
    g.WR = sb("WRa", [128, 4096], BF16)
    g.RO = sb("ROa", [128, 1024])
    g.ROWS = sb("ROWS", [1, 3232])
    g.E = sb("Ea", [128, 2048], BF16)
    g.idf = sb("idf", [128, 128])
    g.idb = sb("idb", [128, 128], BF16)
    g.ones_cb = sb("ones_cb", [128, 1], BF16)
    g.ones_rf = sb("ones_rf", [1, 128])
    g.ones_rb = sb("ones_rb", [1, 128], BF16)
    g.tri = sb("tri", [128, 2048], BF16)
    g.smalls = sb("smalls", [128, 96])
    g.mrow_s = sb("mrow_s", [8, 128])
    if 2 in layers:
        g.X8 = sb("X8", [128, 2048], FP8)
        g.W8 = sb("W8", [128, 4096], FP8)
    if 1 in layers:
        g.cvt8 = sb("cvt8", [128, 128], BF16)
        g.cvt8b = sb("cvt8b", [8, 1024], BF16)
        g.cmbr = sb("cmbr", [1, 1024], BF16)
    if 3 in layers:
        g.gms = sb("gms", [16, 512], BF16)
        g.mvs = sb("mvs", [16, 512], BF16)
        g.MV = sb("MVsb", [d.NCH, d.SEQ], BF16)
        g.ind16 = sb("ind16sb", [d.NCH, d.SEQ], BF16)
    if 1 in layers or 2 in layers or 3 in layers:
        g.Et = sb("Eta", [128, 2048], BF16)
    return g



def _featmaj_ap(t, TL):
    return bass.AP(t, 0, [[TL, 128], [128 * TL, JC], [1, TL]])

def AP(t, off, pat):
    return bass.AP(t, off, pat)


def _load_consts(p, g, layers):
    d = DIMS
    for v in (EPS_LN, 1e-12, -16.0, 16.0, RT128, -math.log(10000.0) / 128.0):
        g.cconst(v)
    with p.phase("init"):
        toks = []
        toks.append(p.dma("y", g.cst[:, 0:16], g.inp("cvals")[:]))
        loads = [(g.idf, "ident_f"), (g.idb, "ident_b"),
                 (g.ones_cb, "ones_col_b"), (g.ones_rf, "ones_row_f"),
                 (g.ones_rb, "ones_row_b"), (g.tri, "tri4")]
        if 3 in layers:
            loads.append((g.ind16, "ind16"))
        for dst, src in loads:
            toks.append(p.dma("y", dst[:], g.inp(src)[:]))
        t_x = p.dma("y", g.XT[:, 0:JC * d.TL],
                    _featmaj_ap(g.inp("x_in"), d.TL))
        p.wait("y", t_x)


def _bld_rope_tables(p, g):
    d = DIMS
    nc = g.nc
    LN4 = math.log(10000.0)
    with p.phase("rope_tab"):
        t_d2 = p.dma("y", g.ROWS[0:1, 0:128], g.inp("d2_row")[:])
        t_if = p.op("s", "activation", g.ROWS[0:1, 128:256], g.ROWS[0:1, 0:128],
                    AF.Exp, scale=-LN4 / 128.0, deps=[t_d2])
        with nc.psum_tensor("ps_rope", [128, 512], F32) as ps:
            t_prev = None
            for c in range(d.SEQ // 512):
                t_pr = p.dma("y", g.ROWS[0:1, 256:768],
                             g.inp("pos_row")[0:1, c * 512:(c + 1) * 512],
                             deps=[t_prev] if t_prev else [], ch="ro0")
                t_mm = p.mm(ps[:, 0:512], g.ROWS[0:1, 128:256],
                            g.ROWS[0:1, 256:768],
                            deps=[x for x in (t_if, t_pr, t_prev) if x])
                a = g.SM[:, 0:512]; b = g.SM[:, 512:1024]
                t1 = p.op("v", "tensor_scalar", a, ps[:, 0:512],
                          1.0 / (2 * math.pi), MAGIC, ALU.mult, ALU.add,
                          deps=[t_mm])
                t2 = p.op("v", "tensor_scalar", a, a, MAGIC, None, ALU.subtract,
                          deps=[t1])
                t3 = p.op("v", "scalar_tensor_tensor", b, a, -2 * math.pi,
                          ps[:, 0:512], ALU.mult, ALU.add, deps=[t2])
                t3 = p.op("v", "tensor_scalar", b, b, 3.141592, -3.141592,
                          ALU.min, ALU.max, deps=[t3])
                t4 = p.op("s", "activation", g.SM2[:, 0:512], b, AF.Sin, deps=[t3])
                td = p.dma("y", g.rope_sin[:, c * 512:(c + 1) * 512],
                           g.SM2[:, 0:512], deps=[t4], ch="rt0")
                t0 = p.op("v", "tensor_scalar", a, ps[:, 0:512],
                          1.0 / (2 * math.pi), 0.25, ALU.mult, ALU.add,
                          deps=[t_mm, t3])
                t1 = p.op("v", "tensor_scalar", a, a, MAGIC, None, ALU.add,
                          deps=[t0])
                t2 = p.op("v", "tensor_scalar", a, a, MAGIC, None,
                          ALU.subtract, deps=[t1])
                t3b = p.op("v", "scalar_tensor_tensor", b, a, -2 * math.pi,
                           ps[:, 0:512], ALU.mult, ALU.add, deps=[t2, t4])
                t3c = p.op("v", "tensor_scalar", b, b, math.pi / 2.0, None,
                           ALU.add, deps=[t3b])
                t3c = p.op("v", "tensor_scalar", b, b, 3.141592, -3.141592,
                           ALU.min, ALU.max, deps=[t3c])
                t4b = p.op("s", "activation", g.SM2[:, 512:1024], b, AF.Sin,
                           deps=[t3c])
                td2 = p.dma("y", g.rope_cos[:, c * 512:(c + 1) * 512],
                            g.SM2[:, 512:1024], deps=[t4b], ch="rt1")
                t_prev = t4b
                p.wait("y", td)
                p.wait("y", td2)


def _ln_rows(p, g, rows_for_ffn=False, fold2=False):
    d = DIMS
    nc = g.nc
    TL = d.TL
    un = p.phase_n
    with nc.psum_tensor(f"lns{un}", [1, 1024], F32) as ps_s, \
         nc.psum_tensor(f"lnq{un}", [1, 1024], F32) as ps_q, \
         nc.psum_tensor(f"lnb{un}", [128, 1024], F32) as ps_b:
        halves = [(h0, min(512, TL - h0)) for h0 in range(0, TL, 512)]
        t_s = t_q = None
        stage_use = [[], []]
        sq_use = None
        fen = p.fence() + ([g._wq_st] if getattr(g, "_wq_st", None) else [])
        for j in range(JC):
            par = j % 2
            sl = g.E[:, par * 1024:par * 1024 + TL]
            tc = p.op("v", "tensor_copy", sl, g.XT[:, j * TL:(j + 1) * TL],
                      deps=fen + stage_use[par])
            sqv = g.SM2.bitcast(BF16)[:, 0:TL]
            tsq = p.op("s", "square", sqv, sl,
                       deps=[tc] + ([sq_use] if sq_use else []))
            for h0, hw in halves:
                t_s = p.mm(ps_s[0:1, h0:h0 + hw], g.ones_cb, sl[:, h0:h0 + hw],
                           start=(j == 0), stop=(j == JC - 1), deps=[tc])
                t_q = p.mm(ps_q[0:1, h0:h0 + hw], g.ones_cb, sqv[:, h0:h0 + hw],
                           start=(j == 0), stop=(j == JC - 1), deps=[tsq])
            stage_use[par] = [t_s, tsq]
            sq_use = t_q
        R = g.ROWS
        mu = R[0:1, 0:TL]
        rr = R[0:1, TL:2 * TL]
        t1 = p.op("v", "tensor_scalar", mu, ps_s[0:1, 0:TL], 1.0 / D, None,
                  ALU.mult, deps=[t_s])
        t2 = p.op("v", "tensor_mul", rr, mu, mu, deps=[t1])
        t3 = p.op("v", "scalar_tensor_tensor", rr, ps_q[0:1, 0:TL], 1.0 / D,
                  rr, ALU.mult, ALU.subtract, deps=[t_q, t2])
        if fold2:
            v2 = g.ROWS[0:1, 2 * TL:3 * TL]
            ta = p.op("v", "tensor_scalar", v2, rr, EPS_LN, None, ALU.add,
                      deps=[t3])
            tb = p.op("s", "activation", v2, v2, AF.Sqrt, deps=[ta])
            tcx = p.op("v", "reciprocal", v2, v2, deps=[tb])
            td = p.op("v", "tensor_mul", rr, rr, v2, deps=[tcx])
            te = p.op("v", "tensor_mul", rr, rr, v2, deps=[td])
            tf = p.op("s", "activation", rr, rr, AF.Sqrt, bias=EPS_LN, deps=[te])
            tg2 = p.op("v", "reciprocal", rr, rr, deps=[tf])
            t5 = p.op("v", "tensor_mul", rr, rr, v2, deps=[tg2])
        else:
            t4 = p.op("s", "activation", rr, rr, AF.Sqrt, bias=EPS_LN, deps=[t3])
            t5 = p.op("v", "reciprocal", rr, rr, deps=[t4])
        tb1 = p.mm(ps_b[:, 0:TL], g.ones_rf, rr, deps=[t5])
        tv1 = p.op("s", "copy", g.SM[:, 0:TL], ps_b[:, 0:TL], deps=[tb1])
        if not rows_for_ffn:
            t6 = p.op("v", "scalar_tensor_tensor", mu, mu, -1.0, rr, ALU.mult,
                      ALU.mult, deps=[t5])
            tb2 = p.mm(ps_b[:, 0:TL], g.ones_rf, mu, deps=[t6, tv1])
            tv2 = p.op("s", "copy", g.SM[:, TL:2 * TL], ps_b[:, 0:TL], deps=[tb2])
        else:
            tb2 = p.mm(ps_b[:, 0:TL], g.ones_rf, mu, deps=[t1, tv1])
            tv2 = p.op("s", "copy", g.SM[:, TL:2 * TL], ps_b[:, 0:TL], deps=[tb2])
        return tv1, tv2, t5



ACFG = {
    0: dict(kv="mq", rope=False, bias=False, l2n=True, causal=False,
            gate=False, wout="l0_wout", post_ln=True, scale=16.0),
    1: dict(kv="mha", rope=True, bias=True, l2n=False, causal=True,
            gate=False, wout=None, post_ln=False, scale=RT128, exact=True),
    2: dict(kv="mha", rope=False, bias=True, l2n=False, causal=True,
            gate=False, wout="l2_wo", post_ln=False, scale=RT128, exact=True),
    3: dict(kv="gqa", rope=True, bias=False, l2n=False, causal=True,
            gate=True, wout="l3_wo", post_ln=False, scale=RT128, exact=True),
}
S_Q0, S_Q1, S_H0, S_H1, S_K0, S_K1, S_V0, S_V1 = range(8)


def _attn_layer(p, g, L):
    d = DIMS
    nc = g.nc
    TL, TB, SEQ, KC, QG, NCH = d.TL, d.TB, d.SEQ, d.KC, d.QG, d.NCH
    cfg = ACFG[L]
    nkv = 1 if cfg["kv"] in ("mq", "gqa") else 4
    NT = 4 + 2 * nkv
    qkv_h = g.inp(f"l{L}_qkv")
    kvs = g.kvs[L]
    qs = g.qs[L]
    un = L * 100

    with p.phase(f"L{L}a_ln"):
        gain = (L == 0)
        if gain:
            t_gn = p.dma("y", g.smalls[:, 16:32], g.inp("l0_gnorm_cols")[:])
        tv1, tv2, _ = _ln_rows(p, g, fold2=gain)
        xn_toks = []
        for j in range(JC):
            ta = p.op("v", "tensor_mul", g.SM2[:, 0:TL],
                      g.XT[:, j * TL:(j + 1) * TL], g.SM[:, 0:TL], deps=[tv1])
            if gain:
                tb = p.op("v", "tensor_add", g.SM2[:, 0:TL],
                          g.SM2[:, 0:TL], g.SM[:, TL:2 * TL], deps=[ta, tv2])
                tb = p.op("v", "tensor_scalar", g.AR[:, j * TL:(j + 1) * TL],
                          g.SM2[:, 0:TL], g.smalls[:, 16 + j:17 + j], None,
                          ALU.mult, deps=[tb, t_gn])
            else:
                tb = p.op("v", "tensor_add", g.AR[:, j * TL:(j + 1) * TL],
                          g.SM2[:, 0:TL], g.SM[:, TL:2 * TL], deps=[ta, tv2])
            xn_toks.append(tb)
        t_st = p.dma("y", _featmaj_ap(g.agx_in[L], TL),
                     g.AR[:, 0:JC * TL], deps=xn_toks, ch="cp")
        p.wait("g", t_st)
        g.t_agx = p.cc("AllGather", ALU.bypass, [g.agx_in[L].ap()],
                       [g.agx_out[L].ap()])
        wa0 = g.ARW - NT * JC * 128
        t_w = [None, None]
        for w in range(NT):
            wpar = w % 2
            dd = [x for x in (t_w[wpar],) if x]
            if wa0 + w * JC * 128 < JC * TL:
                dd = dd + [t_st]
            t_w[wpar] = p.dma("g", g.AR[:, wa0 + w * JC * 128:
                                        wa0 + (w + 1) * JC * 128],
                              AP(qkv_h, w * 16384,
                                 [[128, 128], [NT * 16384, JC], [1, 128]]),
                              deps=dd, ch=f"w{wpar}")
        g._wa_toks = [x for x in t_w if x]

    agxo = g.agx_out[L]

    def slab_ap(tc, j):
        return AP(g.axl, (tc * D + j * 128) * TB, [[TB, 128], [1, TB]])

    def wtile(w, j):
        c = g.ARW - NT * JC * 128 + (w * JC + j) * 128
        return g.AR[:, c:c + 128]

    with p.phase(f"L{L}a_proj"):
        wqst = getattr(g, "_wq_st", None)
        if wqst:
            p.wait("v", wqst)
        t_parts = []
        for piece in range(4):
            rr0 = piece * 4096
            t_parts.append(p.dma("y" if piece % 2 == 0 else "s",
                                 AP(g.axl, rr0 * TB, [[TB, 4096], [1, TB]]),
                                 AP(agxo, rr0 * TL + g.boff, [[TL, 4096], [1, TB]]),
                                 deps=[g.t_agx], ch="cp"))
        g.t_agx = t_parts[0]
        g._agx2 = t_parts
        if cfg["bias"]:
            p.dma("y", g.smalls[:, 0:4], g.inp(f"l{L}_bq")[:])
            p.dma("y", g.smalls[:, 4:8], g.inp(f"l{L}_bk")[:])
            t_bv = p.dma("y", g.smalls[:, 8:12], g.inp(f"l{L}_bv")[:])
            t_bias = [t_bv]
        else:
            t_bias = []
        if nkv == 1:
            if cfg["l2n"]:
                passes = [[("q", 0, 0), ("q", 1, 1), ("q", 2, 2), ("q", 3, 3)],
                          [("k", 0, 4), ("v", 0, 5)]]
            else:
                passes = [[("q", 0, 0), ("q", 1, 1), ("q", 2, 2), ("q", 3, 3),
                           ("k", 0, 4), ("v", 0, 5)]]
        else:
            passes = [[("q", i, i) for i in range(4)] +
                      [("k", i, 4 + i) for i in range(4)],
                      [("v", i, 8 + i) for i in range(4)]]
        for pn, outs in enumerate(passes):
            nw = len(outs)
            has_v = any(o[0] == "v" for o in outs)
            ctx2 = ExitStack()
            psA = ctx2.enter_context(
                nc.psum_tensor(f"pA{un}_{pn}", [128, nw * 512], F32))
            psT = ctx2.enter_context(
                nc.psum_tensor(f"pT{un}_{pn}", [128, 512], BF16)) if has_v else None
            psB = ctx2.enter_context(
                nc.psum_tensor(f"pB{un}_{pn}", [128, 1024], F32)) if cfg["l2n"] else None
            bcp = psB[:, 0:512] if psB is not None else None
            rowp = psB[0:1, 512:1024] if psB is not None else None
            with ctx2:
                for tc in range(NCORE):
                    if cfg["rope"] and outs[0][0] != "v":
                        t_cos = p.dma("y", g.RO[:, 0:TB],
                                      g.rope_cos[:, tc * TB:(tc + 1) * TB],
                                      deps=p.fence(), ch="ro0")
                        t_sin = p.dma("y", g.RO[:, 512:512 + TB],
                                      g.rope_sin[:, tc * TB:(tc + 1) * TB],
                                      ch="ro1")
                    else:
                        t_cos = t_sin = None
                    t_slb = [None, None]
                    fen = p.fence()
                    guard = getattr(g, "_eslab_guard", [])
                    tlast = None
                    for j in range(JC):
                        par = j % 2
                        t_slb[par] = p.dma("y", g.E[:, par * 512:par * 512 + TB],
                                           slab_ap(tc, j),
                                           deps=g._agx2 + guard +
                                                ([t_slb[par]] if t_slb[par] else fen) +
                                                ([tlast] if tlast else []),
                                           ch=f"sl{par}")
                        dd = [t_slb[par]] + g._wa_toks
                        st = (j == 0); sp = (j == JC - 1)
                        for wi in range(nw):
                            tlast = p.mm(psA[:, wi * 512:wi * 512 + TB],
                                         wtile(outs[wi][2], j),
                                         g.E[:, par * 512:par * 512 + TB],
                                         start=st, stop=sp, deps=dd,
                                         inc=(wi == nw - 1))
                    t_prev_e = None
                    t_rowfree = None
                    t_bcfree = None
                    prev_std = getattr(g, "_prev_std", None)
                    prev_ttr = getattr(g, "_prev_ttr", None)
                    for n, (kind, i, _w) in enumerate(outs):
                        src = psA[:, n * 512:n * 512 + TB]
                        bias = 0.0
                        if cfg["bias"]:
                            bb = {"q": 0, "k": 4, "v": 8}[kind] + i
                            bias = g.smalls[:, bb:bb + 1]
                        stg = g.SM[:, 1024 + (n % 2) * 512:1024 + (n % 2) * 512 + TB]
                        dd = [tlast] + t_bias + ([t_prev_e] if t_prev_e else [])
                        if cfg["bias"]:
                            t_e = p.op("v", "tensor_scalar", stg, src, bias,
                                       None, ALU.add, deps=dd)
                        else:
                            t_e = p.op("s", "copy", stg, src, deps=dd)
                        if cfg["rope"] and kind in ("q", "k"):
                            t_e = _rope_apply(p, g, stg, t_e, t_cos, t_sin, TB)
                        if cfg["l2n"] and kind in ("q", "k"):
                            t_e = _l2n_tile(p, g, stg, t_e, bcp, rowp, TB, un,
                                            t_bcfree, t_rowfree)
                            t_bcfree = t_e
                            t_rowfree = t_e
                        t_c = p.op("v", "tensor_copy", g.E[:, 1024:1024 + TB],
                                   stg, deps=[t_e] +
                                   ([t_prev_e] if t_prev_e else []) +
                                   ([prev_std] if prev_std else []) +
                                   ([prev_ttr] if prev_ttr else []))
                        prev_ttr = None
                        if kind == "q":
                            t_d = p.dma("y", AP(qs, i * 128 * SEQ + tc * TB,
                                                [[SEQ, 128], [1, TB]]),
                                        g.E[:, 1024:1024 + TB], deps=[t_c],
                                        ch="sq")
                            t_prev_e = t_c
                            prev_std = t_d
                        elif kind == "k":
                            t_d = p.dma("y", AP(kvs, i * 128 * SEQ + tc * TB,
                                                [[SEQ, 128], [1, TB]]),
                                        g.E[:, 1024:1024 + TB], deps=[t_c],
                                        ch="sk")
                            t_prev_e = t_c
                            prev_std = t_d
                        else:
                            ntr = TB // 128
                            ttr = []
                            for q in range(ntr):
                                ttr.append(p.op("t", "transpose",
                                                psT[:, q * 128:(q + 1) * 128],
                                                g.E[:, 1024 + q * 128:1024 + (q + 1) * 128],
                                                g.idb[:], deps=[t_c]))
                            t_c2 = p.op("v", "tensor_copy", g.E[:, 0:ntr * 128],
                                        psT[:, 0:ntr * 128], deps=ttr)
                            t_d = p.dma("y", AP(kvs, (4 + i) * 128 * SEQ + tc * TB * 128,
                                                [[128, 128], [16384, ntr], [1, 128]]),
                                        g.E[:, 0:ntr * 128]
                                        .rearrange("p (n f) -> p n f", n=ntr),
                                        deps=[t_c2], ch="sv")
                            t_prev_e = t_c2
                            prev_std = t_d
                            prev_ttr = ttr[-1]
                            g._eslab_guard = [t_d]
                        tlast = _later(tlast, p.tok("t"))
                        g._prev_std = prev_std
                        g._prev_ttr = prev_ttr
        g._tk_st = [p.dtok("y", "sq"), p.dtok("y", "sk"), p.dtok("y", "sv")]

    with p.phase(f"L{L}a_att"):
        if DEBUG_DUMP and cfg["scale"] != 16.0:
            t_d1 = p.dma("y", AP(g.dbg_bias, 0, [[32, 128], [1, 4]]),
                         g.SM2[:, 896:900], ch="dbg")
            t_d2 = p.dma("y", AP(g.dbg_bias, 4, [[32, 1], [1, 28]]),
                         g.ROWS[0:1, 2048:2076], ch="dbg")
            p.wait("y", t_d2)
        hid_toks = _attn_inner(p, g, L, cfg, nkv, un)
        g.t_agh = p.cc("AllGather", ALU.bypass, [g.agh_in[L].ap()],
                       [g.agh_out[L].ap()], deps=[hid_toks[-1]])

    with p.phase(f"L{L}a_out"):
        if DEBUG_DUMP:
            t_dbg = p.dma("y", g.dbg_hid[:], g.agh_in[L][:], ch="dbg")
            p.wait("y", t_dbg)
        _attn_out(p, g, L, cfg, un)


def _rope_apply(p, g, stg, t_src, t_cos, t_sin, W):
    rot = g.SM2[:, 0:W]
    t1 = p.op("v", "tensor_scalar", rot[0:64, :], stg[64:128, :], -1.0, None,
              ALU.mult, deps=[t_src])
    t2 = p.op("v", "tensor_copy", rot[64:128, :], stg[0:64, :], deps=[t_src])
    t3 = p.op("v", "tensor_mul", rot, rot, g.RO[:, 512:512 + W],
              deps=[t1, t2] + ([t_sin] if t_sin else []))
    t4 = p.op("v", "tensor_mul", stg, stg, g.RO[:, 0:W],
              deps=([t_cos] if t_cos else []) + [t3])
    t5 = p.op("v", "tensor_add", stg, stg, rot, deps=[t4])
    return t5


def _l2n_tile(p, g, stg, t_src, bcp, rowp, W, un, t_bcfree, t_rowfree):
    sq = g.SM2[:, 512:512 + W]
    t1 = p.op("s", "square", sq, stg, deps=[t_src])
    t2 = p.mm(rowp[0:1, 0:W], g.smalls[:, 63:64], sq,
              deps=[t1] + ([t_rowfree] if t_rowfree else []))
    t3 = p.op("s", "activation", g.ROWS[0:1, 2560:2560 + W],
              rowp[0:1, 0:W], AF.Sqrt, bias=1e-12, deps=[t2])
    t4 = p.op("v", "reciprocal", g.ROWS[0:1, 2560:2560 + W],
              g.ROWS[0:1, 2560:2560 + W], deps=[t3])
    t5 = p.mm(bcp[:, 0:W], g.ones_rf, g.ROWS[0:1, 2560:2560 + W],
              deps=[t4] + ([t_bcfree] if t_bcfree else []))
    t6 = p.op("s", "copy", g.SM2[:, 512:512 + W], bcp[:, 0:W], deps=[t5])
    t7 = p.op("v", "tensor_mul", stg, stg, g.SM2[:, 512:512 + W], deps=[t6])
    return t7


def _bound_finalize(p, g, cfg, nkv, un):
    nc = g.nc
    scale = cfg["scale"]
    with nc.psum_tensor(f"bc{un}", [128, 64], F32) as ps_c:
        fen = p.fence()
        for i in range(4):
            kvi = 0 if nkv == 1 else i
            t1 = p.op("v", "tensor_mul", g.ROWS[0:1, 2080 + i:2081 + i],
                      g.ROWS[0:1, 2048 + i:2049 + i],
                      g.ROWS[0:1, 2056 + kvi:2057 + kvi], deps=fen)
            t2 = p.op("s", "activation", g.ROWS[0:1, 2080 + i:2081 + i],
                      g.ROWS[0:1, 2080 + i:2081 + i], AF.Sqrt, deps=[t1])
            t3 = p.op("v", "tensor_scalar", g.ROWS[0:1, 2080 + i:2081 + i],
                      g.ROWS[0:1, 2080 + i:2081 + i], -scale, None, ALU.mult,
                      deps=[t2])
            t4 = p.mm(ps_c[:, i:i + 1], g.ones_rf, g.ROWS[0:1, 2080 + i:2081 + i],
                      deps=[t3])
            p.op("s", "copy", g.SM2[:, 896 + i:897 + i], ps_c[:, i:i + 1],
                 deps=[t4])


def _l3_gate(p, g, h, t_kld, t_qld, un):
    d = DIMS
    nc = g.nc
    SEQ, NCH = d.SEQ, d.NCH
    nblk = SEQ // 128
    qoff = (h % 2) * SEQ
    koff = S_K0 * SEQ
    t_out = []
    with nc.psum_tensor(f"gg{un}_{h}", [16, 512], F32) as ps_g, \
         nc.psum_tensor(f"gt{un}_{h}", [128, 512], F32) as ps_t:
        fen = p.fence()
        t1 = p.op("v", "tensor_reduce", g.SM2[:, 0:NCH],
                  g.AR[:, koff:koff + SEQ].rearrange("p (n c) -> p n c", n=NCH),
                  AX.X, ALU.add, deps=fen + [t_kld])
        t2 = p.op("v", "tensor_copy", g.E[:, 0:NCH], g.SM2[:, 0:NCH], deps=[t1])
        tgl = []
        for c in range(SEQ // 512):
            t_gm = p.dma("y", g.gms[0:NCH, 0:512],
                         g.inp("l3_gmask")[0:NCH, c * 512:(c + 1) * 512],
                         deps=[tgl[-1]] if tgl else fen, ch="gm")
            tg1 = p.mm(ps_g[0:NCH, 0:512], g.E[:, 0:NCH],
                       g.AR[:, qoff + c * 512:qoff + (c + 1) * 512],
                       deps=[t2, t_qld] + ([tgl[-1]] if tgl else []))
            tg2 = p.op("v", "tensor_add", ps_g[0:NCH, 0:512], ps_g[0:NCH, 0:512],
                       g.gms[0:NCH, 0:512], deps=[tg1, t_gm])
            tg3 = p.op("s", "copy", g.SM2[0:NCH, 0:512], ps_g[0:NCH, 0:512],
                       deps=[tg2])
            for q in range(4):
                blk = c * 4 + q
                tg4 = p.op("t", "transpose", ps_t[:, 0:NCH],
                           g.SM2[0:NCH, q * 128:(q + 1) * 128],
                           g.idf[0:NCH, 0:NCH],
                           deps=[tg3] + ([tgl[-1]] if tgl else []))
                tgl.append(p.op("s", "copy", g.SM[:, blk * NCH:blk * NCH + NCH],
                                ps_t[:, 0:NCH], deps=[tg4]))
        TG = g.SM[:, 0:nblk * NCH]
        TGv = TG.rearrange("p (n c) -> p n c", n=nblk)
        W = g.SM2[:, 0:nblk * NCH]
        Wv = W.rearrange("p (n c) -> p n c", n=nblk)
        M = g.RO[:, 0:nblk]
        Mv = M.rearrange("p n -> p n ()").broadcast_to([128, nblk, NCH])
        mk = g.RO[:, 512:512 + nblk * NCH]
        mkv = mk.rearrange("p (n c) -> p n c", n=nblk)
        tw = p.op("v", "tensor_copy", W, TG, deps=tgl)
        for it in range(3):
            tm = p.op("v", "tensor_reduce", M, Wv, AX.X, ALU.max, deps=[tw])
            tmask = p.op("v", "tensor_tensor", mkv, Wv, Mv, ALU.is_ge, deps=[tm])
            tw = p.op("v", "scalar_tensor_tensor", W, mk, NEG, W, ALU.mult,
                      ALU.add, deps=[tmask])
        tm4 = p.op("v", "tensor_reduce", M, Wv, AX.X, ALU.max, deps=[tw])
        tal = p.op("v", "tensor_tensor", mkv, TGv, Mv, ALU.is_ge, deps=[tm4])
        tmv = p.op("v", "tensor_scalar", g.SM2[:, 0:nblk * NCH], mk, 1.0, -NEG,
                   ALU.subtract, ALU.mult, deps=[tal])
        for blk in range(nblk):
            tb1 = p.op("t", "transpose", ps_t[0:NCH, 0:128],
                       g.SM2[:, blk * NCH:blk * NCH + NCH], g.idf[:, 0:128],
                       deps=[tmv] + ([t_out[-1]] if t_out else []))
            tb2 = p.op("v", "tensor_copy", g.mvs[0:NCH, 0:128],
                       ps_t[0:NCH, 0:128], deps=[tb1])
            mvg = getattr(g, "_mv2g", None)
            t_out.append(p.dma("y", AP(g.mvd, blk * 128, [[SEQ, NCH], [1, 128]]),
                               g.mvs[0:NCH, 0:128],
                               deps=[tb2, p.tok("t")] + ([mvg] if mvg else []),
                               ch="mv"))
    return t_out


def _attn_inner_exact(p, g, L, cfg, nkv, un):
    d = DIMS
    nc = g.nc
    SEQ, QG, NCH = d.SEQ, d.QG, d.NCH
    kvs = g.kvs[L]
    qs = g.qs[L]
    scale = cfg["scale"]
    gate = cfg["gate"]
    hid_st = []
    last_q = {}
    last_k = {}
    last_v = {}
    last_hid = {}
    kv_state = {}
    for i in range(4):
        par = i % 2
        kvi = 0 if nkv == 1 else i
        fen = p.fence()
        t_qld = p.dma("y", g.AR[:, par * SEQ:par * SEQ + SEQ],
                      AP(qs, i * 128 * SEQ, [[SEQ, 128], [1, SEQ]]),
                      deps=fen + g._tk_st +
                           [x for x in (last_q.get(par),) if x],
                      ch=f"qq{par}")
        if nkv == 4 or i == 0:
            t_kld = p.dma("y", g.AR[:, (S_K0 + par) * SEQ:(S_K0 + par + 1) * SEQ],
                          AP(kvs, kvi * 128 * SEQ, [[SEQ, 128], [1, SEQ]]),
                          deps=fen + g._tk_st +
                               [x for x in (last_k.get(par),) if x],
                          ch=f"kk{par}")
            t_vld = p.dma("y", g.AR[:, (S_V0 + par) * SEQ:(S_V0 + par + 1) * SEQ],
                          AP(kvs, (4 + kvi) * 128 * SEQ,
                             [[128, 128], [16384, SEQ // 128], [1, 128]]),
                          deps=fen + g._tk_st +
                               [x for x in (last_v.get(par),) if x],
                          ch=f"vv{par}")
            kv_state = (S_K0 + par, S_V0 + par, t_kld, t_vld)
        kslot, vslot, t_kld, t_vld = kv_state
        Kh = g.AR[:, kslot * SEQ:(kslot + 1) * SEQ]
        Vh = g.AR[:, vslot * SEQ:(vslot + 1) * SEQ]
        Qh = g.AR[:, par * SEQ:(par + 1) * SEQ]
        if gate:
            t_mv = _l3_gate(p, g, i, t_kld, t_qld, un)
            t_mvl = p.dma("y", g.MV[0:NCH, 0:SEQ], g.mvd[:],
                          deps=[t_mv[-1], p.tok("t"), p.tok("s")], ch="mv2")
            g._mv2g = p.dtok("y", "mv2")
        hslot = S_H0 + par
        Hh = g.AR[:, hslot * SEQ:(hslot + 1) * SEQ]
        t_hid_prev = last_hid.get(par)
        with nc.psum_tensor(f"xA{un}_{i}", [128, 1024], F32) as psA, \
             nc.psum_tensor(f"xT{un}_{i}", [128, 2048], BF16) as psT, \
             nc.psum_tensor(f"xO{un}_{i}", [128, 512], F32) as pO, \
             nc.psum_tensor(f"xR{un}_{i}", [1, 512], F32) as pR, \
             nc.psum_tensor(f"xB{un}_{i}", [128, 512], F32) as pB:
            t_red = [None, None]
            t_etr = [None, None]
            t_cp = None
            t_epi = None
            sp = 0

            def score_mm(qb, t, qg, dd):
                q0 = qg * 512 + qb * 128
                S = psA[:, (sp % 2) * 512:(sp % 2) * 512 + 512]
                tm = p.mm(S, Qh[:, q0:q0 + 128], Kh[:, t * 512:(t + 1) * 512],
                          start=True, stop=not gate, deps=dd, inc=not gate)
                if gate:
                    tm = p.mm(S, g.MV[0:NCH, q0:q0 + 128],
                              g.ind16[0:NCH, t * 512:(t + 1) * 512],
                              start=False, stop=True, deps=[t_mvl], inc=True)
                tt = None
                if t == qg:
                    tt = p.op("v", "tensor_add", S, S,
                              g.tri[:, qb * 512:(qb + 1) * 512], deps=[tm])
                return S, tm, tt

            for qg in range(QG):
                ntile = qg + 1
                kc_hi = 4 * ntile
                fen_s = p.tok("s")
                t_mx = [None] * 4
                for qb in range(4):
                    for t in range(ntile):
                        dd = [t_qld, t_kld] + \
                             ([t_red[sp % 2]] if t_red[sp % 2] else [])
                        S, tm, tt = score_mm(qb, t, qg, dd)
                        rdd = [tm] + ([tt] if tt else [])
                        if t == 0:
                            tr = p.op("v", "tensor_reduce",
                                      g.SM2[:, 896 + qb:897 + qb], S,
                                      AX.X, ALU.max,
                                      deps=rdd + [x for x in (fen_s,) if x])
                        else:
                            tr = p.op("v", "tensor_reduce", g.SM2[:, 904:905],
                                      S, AX.X, ALU.max, deps=rdd)
                            tr = p.op("v", "tensor_max",
                                      g.SM2[:, 896 + qb:897 + qb],
                                      g.SM2[:, 896 + qb:897 + qb],
                                      g.SM2[:, 904:905], deps=[tr])
                        t_red[sp % 2] = tr
                        sp += 1
                    t_mx[qb] = p.op("v", "tensor_scalar",
                                    g.SM2[:, 896 + qb:897 + qb],
                                    g.SM2[:, 896 + qb:897 + qb], -scale, None,
                                    ALU.mult, deps=[t_red[(sp - 1) % 2]])
                t_sum = t_pv = None
                for t in range(ntile):
                    for qb in range(4):
                        dd = [t_qld, t_kld] + \
                             ([t_red[sp % 2]] if t_red[sp % 2] else [])
                        S, tm, tt = score_mm(qb, t, qg, dd)
                        es = qb % 2
                        Etile = g.E[:, es * 512:(es + 1) * 512]
                        t_e = p.op("s", "activation", Etile, S, AF.Exp,
                                   scale=scale,
                                   bias=g.SM2[:, 896 + qb:897 + qb],
                                   deps=[tm] + ([tt] if tt else []) +
                                        [t_mx[qb]] +
                                        ([t_etr[es]] if t_etr[es] else []))
                        t_red[sp % 2] = t_e
                        sp += 1
                        ttr = None
                        for ks in range(4):
                            ttr = p.op("t", "transpose",
                                       psT[:, ks * 512 + qb * 128:
                                           ks * 512 + qb * 128 + 128],
                                       Etile[:, ks * 128:(ks + 1) * 128],
                                       g.idb[:],
                                       deps=[t_e] + ([t_cp] if t_cp else []))
                        t_etr[es] = ttr
                    t_cp = p.op("v", "tensor_copy", g.Et[:, 0:2048],
                                psT[:, 0:2048], deps=[p.tok("t")])
                    for ks in range(4):
                        kc = t * 4 + ks
                        t_sum = p.mm(pR[0:1, 0:512], g.ones_cb,
                                     g.Et[:, ks * 512:(ks + 1) * 512],
                                     start=(kc == 0), stop=(kc == kc_hi - 1),
                                     deps=[t_cp] +
                                          ([t_epi] if t_epi and kc == 0 else []))
                        t_pv = p.mm(pO[:, 0:512], Vh[:, kc * 128:(kc + 1) * 128],
                                    g.Et[:, ks * 512:(ks + 1) * 512],
                                    start=(kc == 0), stop=(kc == kc_hi - 1),
                                    deps=[t_cp, t_vld])
                t_r = p.op("v", "reciprocal", g.ROWS[0:1, 2560:3072],
                           pR[0:1, 0:512], deps=[t_sum])
                t_b = p.mm(pB[:, 0:512], g.ones_rf, g.ROWS[0:1, 2560:3072],
                           deps=[t_r] + ([t_epi] if t_epi else []))
                t_bc = p.op("s", "copy", g.SM[:, 1024:1536], pB[:, 0:512],
                            deps=[t_b])
                t_h = p.op("v", "scalar_tensor_tensor",
                           Hh[:, qg * 512:(qg + 1) * 512],
                           pO[:, 0:512], 1.0, g.SM[:, 1024:1536],
                           ALU.mult, ALU.mult,
                           deps=[t_bc, t_pv] +
                                ([p.dtok("y", "hs")] if t_hid_prev else []))
                t_hid_prev = None
                t_epi = t_h
            t_hs = p.dma("y", AP(g.agh_in[L], i * 128 * SEQ,
                                 [[SEQ, 128], [1, SEQ]]),
                         Hh, deps=[t_epi], ch="hs")
        hid_st.append(t_hs)
        last_hid[par] = True
        last_q[par] = p.tok("t")
        last_k[par] = p.tok("t")
        last_v[par] = p.tok("t")
    return hid_st


def _attn_inner(p, g, L, cfg, nkv, un):
    if cfg.get("exact"):
        return _attn_inner_exact(p, g, L, cfg, nkv, un)
    d = DIMS
    nc = g.nc
    SEQ, KC, QG, NCH = d.SEQ, d.KC, d.QG, d.NCH
    kvs = g.kvs[L]
    qs = g.qs[L]
    scale = cfg["scale"]
    hid_st = []
    last_q = {}
    last_k = {}
    last_v = {}
    last_hid = {}
    kv_state = {}
    for i in range(4):
        par = i % 2
        kvi = 0 if nkv == 1 else i
        if True:
            fen = p.fence()
            t_qld = p.dma("y", g.AR[:, par * SEQ:par * SEQ + SEQ],
                          AP(qs, i * 128 * SEQ, [[SEQ, 128], [1, SEQ]]),
                          deps=fen + g._tk_st +
                               [x for x in (last_q.get(par),) if x],
                          ch=f"qq{par}")
            if nkv == 4 or i == 0:
                t_kld = p.dma("y", g.AR[:, (S_K0 + par) * SEQ:(S_K0 + par + 1) * SEQ],
                              AP(kvs, kvi * 128 * SEQ, [[SEQ, 128], [1, SEQ]]),
                              deps=fen + g._tk_st +
                                   [x for x in (last_k.get(par),) if x],
                              ch=f"kk{par}")
                t_vld = p.dma("y", g.AR[:, (S_V0 + par) * SEQ:(S_V0 + par + 1) * SEQ],
                              AP(kvs, (4 + kvi) * 128 * SEQ,
                                 [[128, 128], [16384, SEQ // 128], [1, 128]]),
                              deps=fen + g._tk_st +
                                   [x for x in (last_v.get(par),) if x],
                              ch=f"vv{par}")
                kv_state = (S_K0 + par, S_V0 + par, t_kld, t_vld)
            kslot, vslot, t_kld, t_vld = kv_state
            Kh = g.AR[:, kslot * SEQ:(kslot + 1) * SEQ]
            Vh = g.AR[:, vslot * SEQ:(vslot + 1) * SEQ]
            Qh = g.AR[:, par * SEQ:(par + 1) * SEQ]
            t_mv = _l3_gate(p, g, i, t_kld, t_qld, un) if cfg["gate"] else []
            gate_guard = [p.tok("t"), p.tok("v"), p.tok("s")] if cfg["gate"] else []
        with nc.psum_tensor(f"aS0_{un}_{i}", [128, 1024], F32) as pS0, \
             nc.psum_tensor(f"aS1_{un}_{i}", [128, 1024], F32) as pS1, \
             nc.psum_tensor(f"aO_{un}_{i}", [128, 512], F32) as pO, \
             nc.psum_tensor(f"aR_{un}_{i}", [1, 512], F32) as pR, \
             nc.psum_tensor(f"aB_{un}_{i}", [128, 512], F32) as pB:
            pS = [pS0, pS1]
            bias_kw = (dict(bias=g.SM2[:, 896 + i:897 + i]) if scale != 16.0
                       else dict(bias=-16.0))
            hslot = S_H0 + par
            Hh = g.AR[:, hslot * SEQ:(hslot + 1) * SEQ]
            t_exp = [None, None]
            t_hid_prev = last_hid.get(par)
            t_epi = None
            for qg in range(QG):
                kc_hi = (4 * qg + 4) if cfg["causal"] else KC
                t_sum = t_pv = None
                t_mrl = None
                if cfg["gate"]:
                    nch_need = (kc_hi + 1) // 2
                    t_mrl = p.dma("y", g.MR[0:1, 0:nch_need * 512],
                                  AP(g.mvd, qg * 512, [[SEQ, nch_need], [1, 512]]),
                                  deps=[t_mv[-1], p.tok("t")], ch="mr")
                nstep = (kc_hi + 1) // 2
                for stp in range(nstep):
                    kc0 = 2 * stp
                    ncur = min(2, kc_hi - kc0)
                    sp = stp % 2
                    S2 = pS[sp]
                    jobs = []
                    for u in range(ncur):
                        kc = kc0 + u
                        jobs.append((S2[:, u * 512:(u + 1) * 512],
                                     Kh[:, kc * 128:(kc + 1) * 128],
                                     Qh[:, qg * 512:(qg + 1) * 512],
                                     True, not cfg["gate"], None))
                        if cfg["gate"]:
                            n = kc // 2
                            jobs.append((S2[:, u * 512:(u + 1) * 512],
                                         g.ones_rb,
                                         g.MR[0:1, n * 512:(n + 1) * 512],
                                         False, True, t_mrl))
                    t_score = None
                    for nj, (o_, l_, r_, st_, sp_, xd) in enumerate(jobs):
                        dd = ([t_exp[sp]] if t_exp[sp] else []) + \
                             [t_kld, t_qld] + ([xd] if xd else [])
                        t_score = p.mm(o_, l_, r_, start=st_, stop=sp_,
                                       deps=dd, inc=(nj == len(jobs) - 1))
                    t_tri = None
                    if cfg["causal"]:
                        for u in range(ncur):
                            kc = kc0 + u
                            if kc >= 4 * qg:
                                pp = kc - 4 * qg
                                t_tri = p.op("v", "tensor_add",
                                             S2[:, u * 512:(u + 1) * 512],
                                             S2[:, u * 512:(u + 1) * 512],
                                             g.tri[:, pp * 512:(pp + 1) * 512],
                                             deps=[t_score])
                    dd = [t_score] + ([t_tri] if t_tri else []) + gate_guard
                    gate_guard = []
                    t_e = p.op("s", "activation",
                               g.E[:, sp * 1024:sp * 1024 + ncur * 512],
                               S2[:, 0:ncur * 512], AF.Exp, scale=scale,
                               deps=dd, **bias_kw)
                    t_exp[sp] = t_e
                    for u in range(ncur):
                        kc = kc0 + u
                        Eh = g.E[:, sp * 1024 + u * 512:sp * 1024 + (u + 1) * 512]
                        dsum = [t_e] + ([t_epi] if t_epi and kc == 0 else [])
                        t_sum = p.mm(pR[0:1, 0:512], g.ones_cb, Eh,
                                     start=(kc == 0), stop=(kc == kc_hi - 1),
                                     deps=dsum)
                        t_pv = p.mm(pO[:, 0:512], Vh[:, kc * 128:(kc + 1) * 128],
                                    Eh, start=(kc == 0), stop=(kc == kc_hi - 1),
                                    deps=[t_e, t_vld])
                    t_exp[sp] = _later(t_exp[sp], t_pv)
                t_r = p.op("v", "reciprocal", g.ROWS[0:1, 2560:3072],
                           pR[0:1, 0:512], deps=[t_sum])
                t_b = p.mm(pB[:, 0:512], g.ones_rf, g.ROWS[0:1, 2560:3072],
                           deps=[t_r] + ([t_epi] if t_epi else []))
                t_bc = p.op("s", "copy", g.SM[:, 1024:1536], pB[:, 0:512],
                            deps=[t_b])
                t_h = p.op("v", "scalar_tensor_tensor",
                           Hh[:, qg * 512:(qg + 1) * 512],
                           pO[:, 0:512], 1.0, g.SM[:, 1024:1536],
                           ALU.mult, ALU.mult,
                           deps=[t_bc, t_pv] +
                                ([p.dtok("y", "hs")] if t_hid_prev else []))
                t_hid_prev = None
                t_epi = t_h
                t_exp[0] = _later(t_exp[0], t_h)
                t_exp[1] = _later(t_exp[1], t_h)
            t_hs = p.dma("y", AP(g.agh_in[L], i * 128 * SEQ,
                                 [[SEQ, 128], [1, SEQ]]),
                         Hh, deps=[t_epi], ch="hs")
            hid_st.append(t_hs)
            last_hid[par] = True
            last_q[par] = p.tok("t")
            last_k[par] = p.tok("t")
            last_v[par] = p.tok("t")
    return hid_st


def _attn_out(p, g, L, cfg, un):
    d = DIMS
    nc = g.nc
    SEQ, TB, TL = d.SEQ, d.TB, d.TL
    agho = g.agh_out[L]

    def hid_ap(b, hc):
        return AP(g.hidl, (b * 16 + hc) * 128 * TB, [[TB, 128], [1, TB]])

    t_hp = []
    for b in range(2):
        t_hp.append(p.dma("y" if b == 0 else "s",
                          AP(g.hidl, b * 16 * 128 * TB,
                             [[4 * 128 * TB, 4], [128 * TB, 4], [TB, 128], [1, TB]]),
                          AP(agho, (4 * b * 512) * SEQ + g.coff,
                             [[512 * SEQ, 4], [128 * SEQ, 4], [SEQ, 128], [1, TB]]),
                          deps=[g.t_agh], ch="cp"))
    g._agh2 = t_hp

    if cfg["wout"] is None:
        t_pr = None
        for b in range(2):
            for j in range(JC):
                t_l = p.dma("y", g.E[:, 0:TB], hid_ap(b, j),
                            deps=g._agh2 + ([t_pr] if t_pr else []),
                            ch="hl0")
                t_pr = p.op("v", "tensor_add",
                            g.XT[:, j * TL + b * TB:j * TL + b * TB + TB],
                            g.XT[:, j * TL + b * TB:j * TL + b * TB + TB],
                            g.E[:, 0:TB], deps=[t_l])
        return

    wo = g.inp(cfg["wout"])
    post_ln = cfg["post_ln"]
    if L == 2:
        t_bo = p.dma("y", g.smalls[:, 32:48], g.inp("l2_bO_cols")[:])
    with nc.psum_tensor(f"po{un}", [128, 1024], F32) as ps_o:
        t_wu = [None, None]
        t_y = []
        t_ld = [None, None]
        for b in range(2):
            for oc in range(JC):
                par = oc % 2
                t_w = p.dma("g", g.WR[:, par * 2048:par * 2048 + 2048],
                            AP(wo, oc * JC * 16384, [[128, 128], [16384, JC], [1, 128]]),
                            deps=[x for x in (t_wu[par],) if x], ch=f"w{par}")
                tmm = None
                for hc in range(JC):
                    hpar = hc % 2
                    t_l = p.dma("y", g.E[:, hpar * 512:hpar * 512 + TB],
                                hid_ap(b, hc),
                                deps=g._agh2 + ([t_ld[hpar]] if t_ld[hpar] else []),
                                ch=f"hl{hpar}")
                    tmm = p.mm(ps_o[:, par * 512:par * 512 + TB],
                               g.WR[:, par * 2048 + hc * 128:par * 2048 + (hc + 1) * 128],
                               g.E[:, hpar * 512:hpar * 512 + TB],
                               start=(hc == 0), stop=(hc == JC - 1),
                               deps=[t_w, t_l], inc=True)
                    t_ld[hpar] = tmm
                t_wu[par] = tmm
                if post_ln:
                    ty = p.op("s", "copy",
                              g.AR[:, oc * TL + b * TB:oc * TL + b * TB + TB],
                              ps_o[:, par * 512:par * 512 + TB], deps=[tmm])
                else:
                    if L == 2:
                        ty = p.op("v", "tensor_scalar", g.SM2[:, 0:TB],
                                  ps_o[:, par * 512:par * 512 + TB],
                                  g.smalls[:, 32 + oc:33 + oc], None, ALU.add,
                                  deps=[tmm, t_bo])
                    else:
                        ty = p.op("v", "tensor_copy", g.SM2[:, 0:TB],
                                  ps_o[:, par * 512:par * 512 + TB], deps=[tmm])
                    ty = p.op("v", "tensor_add",
                              g.XT[:, oc * TL + b * TB:oc * TL + b * TB + TB],
                              g.XT[:, oc * TL + b * TB:oc * TL + b * TB + TB],
                              g.SM2[:, 0:TB], deps=[ty])
                t_wu[par] = _later(t_wu[par], ty)
                t_y.append(ty)
    if post_ln:
        _post_ln_add(p, g, t_y, un)


def _post_ln_add(p, g, y_toks, un):
    d = DIMS
    nc = g.nc
    TL = d.TL
    t_g = p.dma("y", g.smalls[:, 16:32], g.inp("l0_gout_cols")[:])
    with nc.psum_tensor(f"pl{un}s", [1, 1024], F32) as ps_s, \
         nc.psum_tensor(f"pl{un}q", [1, 1024], F32) as ps_q, \
         nc.psum_tensor(f"pl{un}b", [128, 1024], F32) as ps_b:
        halves = [(h0, min(512, TL - h0)) for h0 in range(0, TL, 512)]
        t_s = t_q = None
        sq_use = None
        for j in range(JC):
            Y = g.AR[:, j * TL:(j + 1) * TL]
            sq = g.SM2.bitcast(BF16)[:, 0:TL]
            tsq = p.op("s", "square", sq, Y,
                       deps=y_toks + ([sq_use] if sq_use else []))
            for h0, hw in halves:
                t_s = p.mm(ps_s[0:1, h0:h0 + hw], g.ones_cb, Y[:, h0:h0 + hw],
                           start=(j == 0), stop=(j == JC - 1), deps=y_toks)
                t_q = p.mm(ps_q[0:1, h0:h0 + hw], g.ones_cb, sq[:, h0:h0 + hw],
                           start=(j == 0), stop=(j == JC - 1), deps=[tsq])
            sq_use = t_q
        R = g.ROWS
        mu = R[0:1, 0:TL]
        rr = R[0:1, TL:2 * TL]
        t1 = p.op("v", "tensor_scalar", mu, ps_s[0:1, 0:TL], 1.0 / D, None,
                  ALU.mult, deps=[t_s])
        t2 = p.op("v", "tensor_mul", rr, mu, mu, deps=[t1])
        t3 = p.op("v", "scalar_tensor_tensor", rr, ps_q[0:1, 0:TL], 1.0 / D,
                  rr, ALU.mult, ALU.subtract, deps=[t_q, t2])
        t4 = p.op("s", "activation", rr, rr, AF.Sqrt, bias=EPS_LN, deps=[t3])
        t5 = p.op("v", "reciprocal", rr, rr, deps=[t4])
        t6 = p.op("v", "scalar_tensor_tensor", mu, mu, -1.0, rr, ALU.mult,
                  ALU.mult, deps=[t5])
        tb1 = p.mm(ps_b[:, 0:TL], g.ones_rf, rr, deps=[t5])
        tv1 = p.op("s", "copy", g.SM[:, 0:TL], ps_b[:, 0:TL], deps=[tb1])
        tb2 = p.mm(ps_b[:, 0:TL], g.ones_rf, mu, deps=[t6, tv1])
        tv2 = p.op("s", "copy", g.SM[:, TL:2 * TL], ps_b[:, 0:TL], deps=[tb2])
        for j in range(JC):
            Y = g.AR[:, j * TL:(j + 1) * TL]
            ta = p.op("v", "tensor_mul", g.SM2[:, 0:TL], Y, g.SM[:, 0:TL],
                      deps=[tv1])
            tb = p.op("v", "tensor_add", g.SM2[:, 0:TL], g.SM2[:, 0:TL],
                      g.SM[:, TL:2 * TL], deps=[ta, tv2])
            tc = p.op("v", "tensor_scalar", g.SM2[:, 0:TL], g.SM2[:, 0:TL],
                      g.smalls[:, 16 + j:17 + j], None, ALU.mult,
                      deps=[tb, t_g])
            p.op("v", "tensor_add", g.XT[:, j * TL:(j + 1) * TL],
                 g.XT[:, j * TL:(j + 1) * TL], g.SM2[:, 0:TL], deps=[tc])



def _part_reduce_to_row(p, g, src, W, psT, dst_off, op, dep):
    nb = W // 128
    cols = []
    for q in range(nb):
        t3 = p.op("t", "transpose", psT[:, 0:128],
                  src[:, q * 128:(q + 1) * 128], g.idf[:],
                  deps=[dep] + cols[-1:])
        t4 = p.op("v", "tensor_reduce", g.SM2[:, 896 + q:897 + q],
                  psT[:, 0:128], AX.X, op, deps=[t3])
        cols.append(t4)
    t5 = p.op("t", "transpose", psT[0:nb, 0:128],
              g.SM2[:, 896:896 + nb], g.idf[:, 0:128], deps=cols)
    t6 = p.op("v", "tensor_copy", g.mrow_s[0:nb, 0:128], psT[0:nb, 0:128],
              deps=[t5])
    t7 = p.dma("y", AP(g.rows_d, dst_off, [[128, nb], [1, 128]]),
               g.mrow_s[0:nb, 0:128], deps=[t6], ch="pr")
    return t7


def _row_bcast(p, g, row_ap, psB, dst_sm, W, dep):
    ones = g.ones_rb if row_ap.dtype == BF16 else g.ones_rf
    t1 = None
    for h0 in range(0, W, 512):
        hw = min(512, W - h0)
        t1 = p.mm(psB[:, h0:h0 + hw], ones, row_ap[0:1, h0:h0 + hw],
                  deps=([dep] if dep else []) + ([t1] if t1 else []))
    t2 = p.op("s", "copy", dst_sm, psB[:, 0:W], deps=[t1])
    return t2


def _wq_mean(p, g, wname, n_tiles, si):
    nc = g.nc
    wsrc = g.inp(wname)
    TPS = 8
    shard = n_tiles // NCORE
    if shard * 16384 not in g._wqoff:
        g._wqoff[shard * 16384] = nc.partition_id() * (shard * 16384)
    woff = g._wqoff[shard * 16384]
    nslab = (shard + TPS - 1) // TPS
    qbase = si * 16384 * 128
    WRf = g.WR.bitcast(F32)
    with nc.psum_tensor(f"wqm{p.phase_n}", [128, 512], F32) as psT:
        with p.phase(f"wq_{wname}_m"):
            fen = p.fence()
            qsl = getattr(g, "_qsf_loads", [])
            t_cpy = p.dma("g", AP(g.qsf, qbase, [[128, shard * 128], [1, 128]]),
                          AP(wsrc, woff, [[128, shard * 128], [1, 128]]),
                          deps=qsl, ch="wqc")
            t_ld = [None, None]
            acc = None
            for s in range(nslab):
                par = s % 2
                nt = min(TPS, shard - s * TPS)
                t_l = p.dma("y", WRf[:, par * 1024:par * 1024 + nt * 128],
                            AP(g.qsf, qbase + s * TPS * 16384,
                               [[128, 128], [16384, nt], [1, 128]]),
                            deps=[t_cpy] +
                                 ([t_ld[par]] if t_ld[par] else fen),
                            ch=f"wq{par}")
                t_r = p.op("v", "tensor_reduce", g.SM2[:, 512 + s % 4:513 + s % 4],
                           WRf[:, par * 1024:par * 1024 + nt * 128],
                           AX.X, ALU.add, apply_absolute_value=True, deps=[t_l])
                if acc is None:
                    acc = p.op("v", "tensor_copy", g.SM2[:, 520:521],
                               g.SM2[:, 512 + s % 4:513 + s % 4], deps=[t_r])
                else:
                    acc = p.op("v", "tensor_add", g.SM2[:, 520:521],
                               g.SM2[:, 520:521], g.SM2[:, 512 + s % 4:513 + s % 4],
                               deps=[t_r])
                t_ld[par] = t_r
            t5 = p.op("t", "transpose", psT[0:1, 0:128], g.SM2[:, 520:521],
                      g.idf[:, 0:128], deps=[acc])
            t6p = getattr(g, "_wq_t6d", None)
            t6 = p.op("v", "tensor_reduce", g.ROWS[0:1, 2048:2049],
                      psT[0:1, 0:128], AX.X, ALU.add,
                      deps=[t5] + ([t6p] if t6p else []))
            arg = getattr(g, "_wqar_g", None)
            dd = [t6] + ([arg] if arg else [])
            if si == 0:
                t6z = p.op("v", "memset", g.ROWS[0:1, 2049:2062], 0.0, deps=[t6])
                p.dma("y", g.wqar_i[0:1, 3:16], g.ROWS[0:1, 2049:2062],
                      deps=[t6z] + ([arg] if arg else []), ch="wqr")
            t6d = p.dma("y", g.wqar_i[0:1, si:si + 1], g.ROWS[0:1, 2048:2049],
                        deps=dd, ch="wqr")
            g._wq_t6d = t6d
    return t6d


def _wq_quant(p, g, wname, n_tiles, mode, si):
    nc = g.nc
    qdst = g.qw[wname]
    EPS = 1e-5
    base = g.wq_slot
    g.wq_slot += 3
    TPS = 8
    shard = n_tiles // NCORE
    nslab = (shard + TPS - 1) // TPS
    qbase = si * 16384 * 128
    WRf = g.WR.bitcast(F32)
    with nc.psum_tensor(f"wqq{p.phase_n}", [128, 512], F32) as psT:
        with p.phase(f"wq_{wname}_q"):
            fen = p.fence()
            t6l = p.dma("y", g.ROWS[0:1, 2048:2049],
                        g.wqar_o[0:1, si:si + 1],
                        deps=[g._wq_tcc], ch="wqr")
            N = n_tiles * 16384
            t7 = p.op("v", "tensor_scalar", g.ROWS[0:1, 2048:2049],
                      g.ROWS[0:1, 2048:2049], 1.0 / N, None, ALU.mult,
                      deps=[t6l] + fen)
            if mode == "l0":
                t8 = p.op("v", "tensor_scalar", g.ROWS[0:1, 2049:2050],
                          g.ROWS[0:1, 2048:2049], EPS, None, ALU.max, deps=[t7])
            else:
                t8 = p.op("v", "tensor_scalar", g.ROWS[0:1, 2049:2050],
                          g.ROWS[0:1, 2048:2049], EPS, None, ALU.add, deps=[t7])
            t9 = p.op("v", "tensor_scalar", g.ROWS[0:1, 2050:2051],
                      g.ROWS[0:1, 2049:2050], 0.5, None, ALU.mult, deps=[t8])
            ta = p.op("v", "tensor_scalar", g.ROWS[0:1, 2051:2052],
                      g.ROWS[0:1, 2050:2051], -1.0, None, ALU.mult, deps=[t9])
            t_bc = None
            for ii, rofs in enumerate((2049, 2050, 2051)):
                tb = p.mm(psT[:, 128 + ii:129 + ii], g.ones_rf,
                          g.ROWS[0:1, rofs:rofs + 1], deps=[ta])
                t_bc = p.op("s", "copy", g.smalls[:, base + ii:base + ii + 1],
                            psT[:, 128 + ii:129 + ii], deps=[tb])
            t_ld = [None, None]
            t_st = getattr(g, "_wq_st", None)
            qg_prev = getattr(g, "_qsh_guard", None)
            qsf_rd = [None, None]
            for s in range(nslab):
                par = s % 2
                nt = min(TPS, shard - s * TPS)
                t_l = p.dma("y", WRf[:, par * 1024:par * 1024 + nt * 128],
                            AP(g.qsf, qbase + s * TPS * 16384,
                               [[128, 128], [16384, nt], [1, 128]]),
                            deps=([t_ld[par]] if t_ld[par] else fen), ch=f"wq{par}")
                qsf_rd[par] = t_l
                Wv = WRf[:, par * 1024:par * 1024 + nt * 128]
                A = g.E[:, 0:nt * 128]
                B = g.SM[:, 0:nt * 128]
                t1 = p.op("v", "tensor_scalar", A, Wv,
                          g.smalls[:, base + 1:base + 2], None, ALU.is_gt,
                          deps=[t_l, t_bc])
                t2 = p.op("v", "tensor_scalar", B, Wv,
                          g.smalls[:, base + 2:base + 3], None, ALU.is_gt,
                          deps=[t_l])
                t3 = p.op("v", "scalar_tensor_tensor",
                          g.E[:, 1024:1024 + nt * 128],
                          A, -1.0, B, ALU.add, ALU.add,
                          deps=[t1, t2] + ([t_st] if t_st else []))
                t_st = p.dma("y", AP(g.qshd, s * TPS * 16384,
                                     [[128, 128], [16384, nt], [1, 128]]),
                             g.E[:, 1024:1024 + nt * 128]
                             .rearrange("p (n f) -> p n f", n=nt),
                             deps=[t3] + ([qg_prev] if qg_prev else []),
                             ch="wqs")
                qg_prev = None
                t_ld[par] = t3
            g._wq_st = t_st
            g._qsf_loads = g._qsf_loads + [x for x in qsf_rd if x] \
                if getattr(g, "_qsf_loads", None) else [x for x in qsf_rd if x]
            p.wait("g", t_st)
            t_ag = p.cc("AllGather", ALU.bypass,
                        [AP(g.qshd, 0, [[128, shard * 128], [1, 128]])],
                        [qdst.ap()])
            g._qsh_guard = t_ag
    return t_ag, base


def _wq_layer(p, g, L):
    FC = 44 if L == 0 else 64
    mode = "l0" if L == 0 else "l3"
    names = (("l0_w1", "l0_w3", "l0_w2") if L == 0 else
             ("l3_wg", "l3_wu", "l3_wd"))
    specs = [(names[0], FC * JC), (names[1], FC * JC), (names[2], JC * FC)]
    if specs[0][0] in g.wq_done:
        return
    t6d = None
    for si, (nm, nt) in enumerate(specs):
        t6d = _wq_mean(p, g, nm, nt, si)
    with p.phase(f"wq_ar{L}"):
        p.wait("g", t6d)
        t_cc = p.cc("AllReduce", ALU.add, [g.wqar_i.ap()], [g.wqar_o.ap()])
        g._wqar_g = t_cc
        g._wq_tcc = t_cc
    for si, (nm, nt) in enumerate(specs):
        g.wq_done[nm] = _wq_quant(p, g, nm, nt, mode, si)


def _emit_xn(p, g, tv1, tv2):
    d = DIMS
    TL = d.TL
    toks = []
    for j in range(JC):
        ta = p.op("v", "tensor_mul", g.SM2[:, 0:TL],
                  g.XT[:, j * TL:(j + 1) * TL], g.SM[:, 0:TL], deps=[tv1])
        tb = p.op("v", "tensor_add", g.AR[:, j * TL:(j + 1) * TL],
                  g.SM2[:, 0:TL], g.SM[:, TL:2 * TL], deps=[ta, tv2])
        toks.append(tb)
    return toks


def _ffn_ternary(p, g, L):
    d = DIMS
    nc = g.nc
    TL, SEQ = d.TL, d.SEQ
    EPS = 1e-5
    if L == 0:
        FC = 44
        wg_n, wu_n, wd_n = "l0_w1", "l0_w3", "l0_w2"
    else:
        FC = 64
        wg_n, wu_n, wd_n = "l3_wg", "l3_wu", "l3_wd"
    _wq_layer(p, g, L)
    cm_g = g.smalls[:, g.wq_done[wg_n][1]:g.wq_done[wg_n][1] + 1]
    cm_u = g.smalls[:, g.wq_done[wu_n][1]:g.wq_done[wu_n][1] + 1]
    cm_d = g.smalls[:, g.wq_done[wd_n][1]:g.wq_done[wd_n][1] + 1]
    un = L * 100 + 50

    with p.phase(f"L{L}f_ln"):
        tv1, tv2, t_r = _ln_rows(p, g, rows_for_ffn=True)
        with nc.psum_tensor(f"fb{un}", [128, 1024], F32) as psB, \
             nc.psum_tensor(f"ft{un}", [128, 512], F32) as psT:
            trm = None
            fen0 = p.fence()
            for j in range(JC):
                t1 = p.op("v", "tensor_sub", g.SM2[:, 0:TL],
                          g.XT[:, j * TL:(j + 1) * TL], g.SM[:, TL:2 * TL],
                          deps=[tv2] + fen0)
                ta_ = p.op("v", "scalar_tensor_tensor", g.SM2[:, 0:TL],
                           g.SM2[:, 0:TL], -1.0, g.SM2[:, 0:TL], ALU.mult,
                           ALU.max, deps=[t1])
                if trm is None:
                    trm = p.op("v", "tensor_copy", g.RO[:, 0:TL],
                               g.SM2[:, 0:TL], deps=[ta_])
                else:
                    trm = p.op("v", "tensor_max", g.RO[:, 0:TL],
                               g.RO[:, 0:TL], g.SM2[:, 0:TL], deps=[ta_])
            t_m = _part_reduce_to_row(p, g, g.RO[:, 0:TL], TL, psT, 0,
                                      ALU.max, trm)
            t_ml = p.dma("y", g.ROWS[0:1, 2 * TL:3 * TL],
                         AP(g.rows_d, 0, [[1, 1], [1, TL]]), deps=[t_m],
                         ch="pr2")
            R = g.ROWS
            mu = R[0:1, 0:TL]; rr = R[0:1, TL:2 * TL]; mm_ = R[0:1, 2 * TL:3 * TL]
            if L == 0:
                t1 = p.op("v", "tensor_mul", mm_, mm_, rr, deps=[t_ml, t_r])
                t2 = p.op("v", "tensor_scalar", mm_, mm_, EPS, None, ALU.max,
                          deps=[t1])
                t3 = p.op("v", "reciprocal", g.SM2[0:1, 0:TL], mm_, deps=[t2])
                t4 = p.op("v", "tensor_scalar", g.SM2[0:1, 0:TL],
                          g.SM2[0:1, 0:TL], 127.0, None, ALU.mult, deps=[t3])
                t5 = p.op("v", "tensor_mul", g.SM2[0:1, 0:TL],
                          g.SM2[0:1, 0:TL], rr, deps=[t4])
                t6 = p.op("v", "tensor_scalar", mm_, mm_, 1.0 / 127.0, None,
                          ALU.mult, deps=[t5])
            else:
                t1 = p.op("v", "tensor_mul", mm_, mm_, rr, deps=[t_ml, t_r])
                t2 = p.op("v", "tensor_scalar", mm_, mm_, 1.0 / 127.0, EPS,
                          ALU.mult, ALU.add, deps=[t1])
                t3 = p.op("v", "reciprocal", g.SM2[0:1, 0:TL], mm_, deps=[t2])
                t5 = p.op("v", "tensor_mul", g.SM2[0:1, 0:TL],
                          g.SM2[0:1, 0:TL], rr, deps=[t3])
            tbc = _row_bcast(p, g, g.SM2[0:1, 0:TL], psB, g.SM[:, 0:TL], TL, t5)
            tqx = []
            for j in range(JC):
                q1 = p.op("v", "tensor_sub", g.SM2[:, 0:TL],
                          g.XT[:, j * TL:(j + 1) * TL], g.SM[:, TL:2 * TL],
                          deps=[tv2, tbc])
                q2 = p.op("v", "tensor_mul", g.SM2[:, 0:TL], g.SM2[:, 0:TL],
                          g.SM[:, 0:TL], deps=[q1])
                q3 = p.op("v", "tensor_scalar", g.SM2[:, 0:TL], g.SM2[:, 0:TL],
                          MAGIC, None, ALU.add, deps=[q2])
                q4 = p.op("v", "tensor_scalar", g.SM2[:, 0:TL], g.SM2[:, 0:TL],
                          MAGIC, None, ALU.subtract, deps=[q3])
                q5 = p.op("v", "tensor_copy", g.AR[:, j * TL:(j + 1) * TL],
                          g.SM2[:, 0:TL], deps=[q4])
                tqx.append(q5)
            t_dq = p.dma("y", AP(g.rows_d, TL, [[1, 1], [1, TL]]), mm_,
                         deps=[t5] if L else [t6], ch="pr")
            g._t_dq = t_dq
            g._tqx = tqx[-1]

    wg_q, wu_q, wd_q = g.qw[wg_n], g.qw[wu_n], g.qw[wd_n]
    t_wq_done = [g.wq_done[wg_n][0], g.wq_done[wu_n][0], g.wq_done[wd_n][0]]
    TSW = min(256, TL)
    NSL = TL // TSW
    HOFF = 4 * SEQ
    for ts in range(NSL):
        with p.phase(f"L{L}f_s{ts}"):
            with nc.psum_tensor(f"fg{un}_{ts}", [128, 1024], F32) as psG, \
                 nc.psum_tensor(f"fr{un}_{ts}", [128, 1024], F32) as psR, \
                 nc.psum_tensor(f"fp{un}_{ts}", [128, 512], F32) as psT2:
                fen = p.fence()
                t_dql = p.dma("y", g.ROWS[0:1, 2 * TL:2 * TL + TSW],
                              AP(g.rows_d, TL + ts * TSW, [[1, 1], [1, TSW]]),
                              deps=[g._t_dq] + fen, ch="pr2")
                tbq = _row_bcast(p, g, g.ROWS[0:1, 2 * TL:2 * TL + TSW],
                                 psR, g.SM[:, 1536:1536 + TSW], TSW, t_dql)
                t_w = [None, None]
                runm = None
                prev_act = None
                for sub, (wq_t, cmcol) in enumerate(((wg_q, cm_g), (wu_q, cm_u))):
                    for fc in range(FC):
                        par = fc % 2
                        t_wl = p.dma("g", g.WR[:, par * 2048:par * 2048 + 2048],
                                     AP(wq_t, fc * JC * 16384,
                                        [[128, 128], [16384, JC], [1, 128]]),
                                     deps=([t_w[par]] if t_w[par] else
                                           fen + t_wq_done),
                                     ch=f"w{(sub * FC + fc) % 4}")
                        tmm = None
                        for j in range(JC):
                            tmm = p.mm(psG[:, par * 512:par * 512 + TSW],
                                       g.WR[:, par * 2048 + j * 128:par * 2048 + (j + 1) * 128],
                                       g.AR[:, j * TL + ts * TSW:j * TL + ts * TSW + TSW],
                                       start=(j == 0), stop=(j == JC - 1),
                                       deps=[t_wl, g._tqx], inc=(j == JC - 1))
                        if sub == 0:
                            e1 = p.op("v", "tensor_mul", g.SM2[:, 0:TSW],
                                      psG[:, par * 512:par * 512 + TSW],
                                      g.SM[:, 1536:1536 + TSW],
                                      deps=[tmm, tbq] +
                                           ([prev_act] if prev_act else []))
                            e2 = p.op("s", "activation",
                                      g.AR[:, HOFF + fc * TSW:HOFF + fc * TSW + TSW],
                                      g.SM2[:, 0:TSW], AF.Silu,
                                      scale=cmcol, deps=[e1])
                            prev_act = e2
                            t_w[par] = e2
                        else:
                            e1 = p.op("v", "tensor_mul", g.SM2[:, 0:TSW],
                                      psG[:, par * 512:par * 512 + TSW],
                                      g.SM[:, 1536:1536 + TSW],
                                      deps=[tmm, tbq] +
                                           ([prev_act] if prev_act else []))
                            prev_act = None
                            e2 = p.op("v", "tensor_mul",
                                      g.AR[:, HOFF + fc * TSW:HOFF + fc * TSW + TSW],
                                      g.AR[:, HOFF + fc * TSW:HOFF + fc * TSW + TSW],
                                      g.SM2[:, 0:TSW], deps=[e1])
                            hslv = g.AR[:, HOFF + fc * TSW:HOFF + fc * TSW + TSW]
                            tab = p.op("v", "scalar_tensor_tensor",
                                       g.SM2[:, 512:512 + TSW], hslv, -1.0,
                                       hslv, ALU.mult, ALU.max, deps=[e2])
                            if runm is None:
                                runm = p.op("v", "tensor_copy", g.RO[:, 0:TSW],
                                            g.SM2[:, 512:512 + TSW], deps=[tab])
                            else:
                                runm = p.op("v", "tensor_max", g.RO[:, 0:TSW],
                                            g.RO[:, 0:TSW],
                                            g.SM2[:, 512:512 + TSW], deps=[tab])
                            t_w[par] = e2
                t_hm = _part_reduce_to_row(p, g, g.RO[:, 0:TSW], TSW, psT2,
                                           2 * TL, ALU.max, runm)
                t_hml = p.dma("y", g.ROWS[0:1, 2 * TL:2 * TL + TSW],
                              AP(g.rows_d, 2 * TL, [[1, 1], [1, TSW]]),
                              deps=[t_hm], ch="pr2")
                hm = g.ROWS[0:1, 2 * TL:2 * TL + TSW]
                if L == 0:
                    h1 = p.op("v", "tensor_scalar", hm, hm,
                              cm_u[0:1, :], None, ALU.mult, deps=[t_hml])
                    h2 = p.op("v", "tensor_scalar", hm, hm, EPS, None, ALU.max,
                              deps=[h1])
                    h3 = p.op("v", "reciprocal", g.SM2[0:1, 0:TSW], hm, deps=[h2])
                    h4 = p.op("v", "tensor_scalar", g.SM2[0:1, 0:TSW],
                              g.SM2[0:1, 0:TSW], 127.0, None, ALU.mult, deps=[h3])
                    h5 = p.op("v", "tensor_scalar", g.SM2[0:1, 0:TSW],
                              g.SM2[0:1, 0:TSW], cm_u[0:1, :], None, ALU.mult,
                              deps=[h4])
                    h6 = p.op("v", "tensor_scalar", hm, hm, 1.0 / 127.0, None,
                              ALU.mult, deps=[h5])
                    h7 = p.op("v", "tensor_scalar", hm, hm, cm_d[0:1, :], None,
                              ALU.mult, deps=[h6])
                else:
                    h1 = p.op("v", "tensor_scalar", hm, hm,
                              cm_u[0:1, :], None, ALU.mult, deps=[t_hml])
                    h2 = p.op("v", "tensor_scalar", hm, hm, 1.0 / 127.0, EPS,
                              ALU.mult, ALU.add, deps=[h1])
                    h3 = p.op("v", "reciprocal", g.SM2[0:1, 0:TSW], hm, deps=[h2])
                    h5 = p.op("v", "tensor_scalar", g.SM2[0:1, 0:TSW],
                              g.SM2[0:1, 0:TSW], cm_u[0:1, :], None, ALU.mult,
                              deps=[h3])
                    h7 = p.op("v", "tensor_scalar", hm, hm, cm_d[0:1, :], None,
                              ALU.mult, deps=[h5])
                tbc2 = _row_bcast(p, g, g.SM2[0:1, 0:TSW], psR,
                                  g.SM[:, 1536:1536 + TSW], TSW, h7)
                tbc3 = _row_bcast(p, g, hm, psR, g.SM[:, 1024:1024 + TSW],
                                  TSW, tbc2)
                tqh = None
                for fc in range(FC):
                    hsl = g.AR[:, HOFF + fc * TSW:HOFF + fc * TSW + TSW]
                    q1 = p.op("v", "tensor_mul", g.SM2[:, 0:TSW], hsl,
                              g.SM[:, 1536:1536 + TSW], deps=[tbc2])
                    q2 = p.op("v", "tensor_scalar", g.SM2[:, 0:TSW],
                              g.SM2[:, 0:TSW], MAGIC, None, ALU.add, deps=[q1])
                    q3 = p.op("v", "tensor_scalar", g.SM2[:, 0:TSW],
                              g.SM2[:, 0:TSW], MAGIC, None, ALU.subtract,
                              deps=[q2])
                    tqh = p.op("v", "tensor_copy", hsl, g.SM2[:, 0:TSW],
                               deps=[q3])
                t_w = [None, None]
                for oc in range(JC):
                    par = oc % 2
                    tmm = None
                    ngrp = (FC + 15) // 16
                    for gi in range(ngrp):
                        nw16 = min(16, FC - gi * 16)
                        t_wl = p.dma("g", g.WR[:, par * 2048:par * 2048 + nw16 * 128],
                                     AP(wd_q, (oc * FC + gi * 16) * 16384,
                                        [[128, 128], [16384, nw16], [1, 128]]),
                                     deps=([t_w[par]] if gi == 0 and t_w[par]
                                           else ([tmm] if tmm else [tqh])),
                                     ch=f"w{oc % 4}")
                        for jj in range(min(16, FC - gi * 16)):
                            fc = gi * 16 + jj
                            tmm = p.mm(psG[:, par * 512:par * 512 + TSW],
                                       g.WR[:, par * 2048 + jj * 128:par * 2048 + (jj + 1) * 128],
                                       g.AR[:, HOFF + fc * TSW:HOFF + fc * TSW + TSW],
                                       start=(fc == 0), stop=(fc == FC - 1),
                                       deps=[t_wl, tqh], inc=True)
                    e1 = p.op("v", "tensor_mul", g.SM2[:, 0:TSW],
                              psG[:, par * 512:par * 512 + TSW],
                              g.SM[:, 1024:1024 + TSW], deps=[tmm, tbc3])
                    e2 = p.op("v", "tensor_add",
                              g.XT[:, oc * TL + ts * TSW:oc * TL + ts * TSW + TSW],
                              g.XT[:, oc * TL + ts * TSW:oc * TL + ts * TSW + TSW],
                              g.SM2[:, 0:TSW], deps=[e1])
                    t_w[par] = e2


def _ffn_layer(p, g, L):
    if L in (0, 3):
        _ffn_ternary(p, g, L)
    elif L == 1:
        _ffn_moe(p, g)
    else:
        _ffn_fp8(p, g)


def _ffn_moe(p, g):
    d = DIMS
    nc = g.nc
    TL, SEQ = d.TL, d.SEQ
    un = 150
    TSW = min(512, TL)
    NSL = TL // TSW
    HOFF = 4 * SEQ
    nblk = TL // 128
    with p.phase("L1f_ln"):
        tv1, tv2, _ = _ln_rows(p, g)
        xn_toks = _emit_xn(p, g, tv1, tv2)
        g._txn = xn_toks[-1]
    _moe_gate2(p, g, un)
    cmb_d = g.cmb_d
    for e in range(9):
        shared = (e == 8)
        FC = 22 if shared else 11
        tsw = TL if not shared else min(512, TL)
        nsl = TL // tsw
        for ts in range(nsl):
            with p.phase(f"L1f_e{e}_{ts}"):
                nhalf = (tsw + 511) // 512
                ctx3 = ExitStack()
                psGh = [ctx3.enter_context(
                            nc.psum_tensor(f"me{un}_{e}_{ts}_{hh}",
                                           [128, 1024], F32))
                        for hh in range(nhalf)]
                psB = ctx3.enter_context(
                    nc.psum_tensor(f"mb{un}_{e}_{ts}", [128, tsw], F32))
                with ctx3:
                    fen = p.fence()
                    if not shared:
                        t_cl = p.dma("y", g.cmbr[0:1, 0:tsw],
                                     AP(cmb_d, e * TL + ts * tsw, [[1, 1], [1, tsw]]),
                                     deps=fen + [g._t_cmb], ch="pr2")
                        t_cb = _row_bcast(p, g, g.cmbr[0:1, 0:tsw], psB,
                                          g.SM[:, 1024:1024 + tsw], tsw, t_cl)
                    t_w = [None, None]
                    prev_act = None
                    for sub in range(2):
                        for fc in range(FC):
                            par = fc % 2
                            base_t = (sub * FC + fc) * JC
                            srcw = (g.inp("l1_sgu") if shared else
                                    g.inp("l1_guw"))
                            off = (0 if shared else e * 22 * JC * 16384) + base_t * 16384
                            t_wl = p.dma("g", g.WR[:, par * 2048:par * 2048 + 2048],
                                         AP(srcw, off, [[128, 128], [16384, JC], [1, 128]]),
                                         deps=([t_w[par]] if t_w[par] else fen),
                                         ch=f"w{(sub * FC + fc) % 4}")
                            tmm = None
                            for j in range(JC):
                                for h0 in range(0, tsw, 512):
                                    hw = min(512, tsw - h0)
                                    tmm = p.mm(psGh[h0 // 512][:, par * 512:par * 512 + hw],
                                               g.WR[:, par * 2048 + j * 128:par * 2048 + (j + 1) * 128],
                                               g.AR[:, j * TL + ts * tsw + h0:
                                                    j * TL + ts * tsw + h0 + hw],
                                               start=(j == 0), stop=(j == JC - 1),
                                               deps=[t_wl, g._txn], inc=True)
                            hsl = g.AR[:, HOFF + fc * tsw:HOFF + fc * tsw + tsw]
                            if sub == 0:
                                for h0 in range(0, tsw, 512):
                                    hw = min(512, tsw - h0)
                                    e2 = p.op("s", "activation",
                                              hsl[:, h0:h0 + hw],
                                              psGh[h0 // 512][:, par * 512:par * 512 + hw],
                                              AF.Silu,
                                              deps=[tmm] + ([prev_act] if prev_act else []))
                                    prev_act = e2
                                t_w[par] = e2
                            else:
                                e2 = None
                                for h0 in range(0, tsw, 512):
                                    hw = min(512, tsw - h0)
                                    pgs = psGh[h0 // 512][:, par * 512:par * 512 + hw]
                                    if not shared:
                                        e1 = p.op("v", "tensor_mul",
                                                  g.SM2[:, h0:h0 + hw], pgs,
                                                  g.SM[:, 1024 + h0:1024 + h0 + hw],
                                                  deps=[tmm, t_cb] +
                                                       ([e2] if e2 else []))
                                    else:
                                        e1 = p.op("v", "tensor_copy",
                                                  g.SM2[:, h0:h0 + hw], pgs,
                                                  deps=[tmm] + ([e2] if e2 else []))
                                    e2 = p.op("v", "tensor_mul",
                                              hsl[:, h0:h0 + hw], hsl[:, h0:h0 + hw],
                                              g.SM2[:, h0:h0 + hw], deps=[e1])
                                t_w[par] = e2
                    tact = t_w[0] if t_w[0] else t_w[1]
                    tact2 = p.tok("v")
                    wdn = g.inp("l1_sdown") if shared else g.inp("l1_down")
                    for oc in range(JC):
                        par = oc % 2
                        tmm = None
                        ngrp = (FC + 15) // 16
                        for gi in range(ngrp):
                            nw16 = min(16, FC - gi * 16)
                            off = ((0 if shared else e * JC * 11 * 16384) +
                                   (oc * FC + gi * 16) * 16384)
                            t_wl = p.dma("g", g.WR[:, par * 2048:par * 2048 + nw16 * 128],
                                         AP(wdn, off, [[128, 128], [16384, nw16], [1, 128]]),
                                         deps=([t_w[par]] if gi == 0 and t_w[par]
                                               else ([tmm] if tmm else [tact2])),
                                         ch=f"w{oc % 4}")
                            for jj in range(nw16):
                                fc = gi * 16 + jj
                                for h0 in range(0, tsw, 512):
                                    hw = min(512, tsw - h0)
                                    tmm = p.mm(psGh[h0 // 512][:, par * 512:par * 512 + hw],
                                               g.WR[:, par * 2048 + jj * 128:par * 2048 + (jj + 1) * 128],
                                               g.AR[:, HOFF + fc * tsw + h0:
                                                    HOFF + fc * tsw + h0 + hw],
                                               start=(fc == 0), stop=(fc == FC - 1),
                                               deps=[t_wl, tact2], inc=True)
                        e2 = None
                        for h0 in range(0, tsw, 512):
                            hw = min(512, tsw - h0)
                            e1 = p.op("v", "tensor_copy", g.SM2[:, h0:h0 + hw],
                                      psGh[h0 // 512][:, par * 512:par * 512 + hw],
                                      deps=[tmm] + ([e2] if e2 else []))
                            e2 = p.op("v", "tensor_add",
                                      g.XT[:, oc * TL + ts * tsw + h0:
                                           oc * TL + ts * tsw + h0 + hw],
                                      g.XT[:, oc * TL + ts * tsw + h0:
                                           oc * TL + ts * tsw + h0 + hw],
                                      g.SM2[:, h0:h0 + hw], deps=[e1])
                        t_w[par] = e2


def _moe_gate2(p, g, un):
    d = DIMS
    nc = g.nc
    TL = d.TL
    nblk = TL // 128
    with p.phase("L1f_gate2"):
        with nc.psum_tensor(f"mgg{un}", [8, 1024], F32) as psL, \
             nc.psum_tensor(f"mtt{un}", [128, 512], F32) as psT:
            fen = p.fence()
            t_gw = p.dma("g", g.cvt8[:, 0:JC * 8],
                         AP(g.inp("l1_gatew"), 0, [[8, 128], [1024, JC], [1, 8]]),
                         deps=fen, ch="w0")
            tmm = None
            for j in range(JC):
                for h0 in range(0, TL, 512):
                    hw = min(512, TL - h0)
                    tmm = p.mm(psL[0:8, h0:h0 + hw],
                               g.cvt8[:, j * 8:(j + 1) * 8],
                               g.AR[:, j * TL + h0:j * TL + h0 + hw],
                               start=(j == 0), stop=(j == JC - 1),
                               deps=[t_gw, g._txn], inc=True)
            t_ev = p.op("s", "copy", g.SM[0:8, 0:TL], psL[0:8, 0:TL], deps=[tmm])
            tts = []
            for blk in range(nblk):
                t1 = p.op("t", "transpose", psT[:, 0:8],
                          g.SM[0:8, blk * 128:(blk + 1) * 128],
                          g.idf[0:8, 0:8], deps=[t_ev] + tts[-1:])
                tts.append(p.op("v", "tensor_copy",
                                g.SM2[:, blk * 8:blk * 8 + 8], psT[:, 0:8],
                                deps=[t1]))
            G = g.SM2[:, 0:nblk * 8]
            Gv = G.rearrange("p (n c) -> p n c", n=nblk)
            M = g.RO[:, 0:nblk]
            Mv = M.rearrange("p n -> p n ()").broadcast_to([128, nblk, 8])
            W2 = g.SM[:, 0:nblk * 8]
            W2v = W2.rearrange("p (n c) -> p n c", n=nblk)
            t1 = p.op("v", "tensor_reduce", M, Gv, AX.X, ALU.max, deps=tts)
            t2 = p.op("v", "tensor_tensor", W2v, Gv, Mv, ALU.subtract, deps=[t1])
            t3 = p.op("s", "activation", W2, W2, AF.Exp, deps=[t2])
            t4 = p.op("v", "tensor_reduce", M, W2v, AX.X, ALU.add, deps=[t3])
            t5 = p.op("v", "reciprocal", M, M, deps=[t4])
            t6 = p.op("v", "tensor_tensor", G.rearrange("p (n c) -> p n c", n=nblk),
                      W2v, Mv, ALU.mult, deps=[t5])
            t7 = p.op("v", "tensor_reduce", M, Gv, AX.X, ALU.max, deps=[t6])
            t8 = p.op("v", "tensor_tensor", W2v, Gv, Mv, ALU.is_ge, deps=[t7])
            t9 = p.op("v", "scalar_tensor_tensor", W2, W2, NEG, G, ALU.mult,
                      ALU.add, deps=[t8])
            ta = p.op("v", "tensor_reduce", M, W2v, AX.X, ALU.max, deps=[t9])
            tb = p.op("v", "tensor_tensor", W2v, Gv, Mv, ALU.is_ge, deps=[ta])
            tc = p.op("v", "tensor_mul", G, G, W2, deps=[tb])
            tst = []
            for blk in range(nblk):
                t1 = p.op("t", "transpose", psT[0:8, 0:128],
                          g.SM2[:, blk * 8:blk * 8 + 8], g.idf[:, 0:128],
                          deps=[tc] + tst[-1:])
                t2 = p.op("v", "tensor_copy",
                          g.cvt8b[0:8, blk * 128:(blk + 1) * 128],
                          psT[0:8, 0:128], deps=[t1])
                tst.append(t2)
            g._t_cmb = p.dma("y", g.cmb_d.ap(), g.cvt8b[0:8, 0:TL],
                             deps=tst, ch="pr")


def _ffn_fp8(p, g):
    d = DIMS
    nc = g.nc
    TL, SEQ = d.TL, d.SEQ
    un = 250
    E4MAX = 448.0
    with p.phase("L2f_ln"):
        tv1, tv2, t_r = _ln_rows(p, g, rows_for_ffn=True)
        with nc.psum_tensor(f"qb{un}", [128, 1024], F32) as psB, \
             nc.psum_tensor(f"qt{un}", [128, 512], F32) as psT:
            trm = None
            fen0 = p.fence()
            for j in range(JC):
                t1 = p.op("v", "tensor_sub", g.SM2[:, 0:TL],
                          g.XT[:, j * TL:(j + 1) * TL], g.SM[:, TL:2 * TL],
                          deps=[tv2] + fen0)
                ta_ = p.op("v", "scalar_tensor_tensor", g.SM2[:, 0:TL],
                           g.SM2[:, 0:TL], -1.0, g.SM2[:, 0:TL], ALU.mult,
                           ALU.max, deps=[t1])
                if trm is None:
                    trm = p.op("v", "tensor_copy", g.RO[:, 0:TL],
                               g.SM2[:, 0:TL], deps=[ta_])
                else:
                    trm = p.op("v", "tensor_max", g.RO[:, 0:TL],
                               g.RO[:, 0:TL], g.SM2[:, 0:TL], deps=[ta_])
            t_m = _part_reduce_to_row(p, g, g.RO[:, 0:TL], TL, psT, 0,
                                      ALU.max, trm)
            t_ml = p.dma("y", g.ROWS[0:1, 2 * TL:3 * TL],
                         AP(g.rows_d, 0, [[1, 1], [1, TL]]), deps=[t_m],
                         ch="pr2")
            mm_ = g.ROWS[0:1, 2 * TL:3 * TL]
            rr = g.ROWS[0:1, TL:2 * TL]
            t1 = p.op("v", "tensor_mul", mm_, mm_, rr, deps=[t_ml, t_r])
            t2 = p.op("v", "tensor_reduce", g.ROWS[0:1, 3200:3201],
                      mm_, AX.X, ALU.max, deps=[t1])
            t3 = p.op("v", "memset", g.mrow_s[0:1, 0:128], 0.0, deps=[t2])
            t4 = p.op("v", "tensor_copy", g.mrow_s[0:1, 0:1],
                      g.ROWS[0:1, 3200:3201], deps=[t3])
            t5 = p.dma("y", g.ar_in.ap(), g.mrow_s[0:1, 0:128], deps=[t4],
                       ch="pr")
            p.wait("g", t5)
            t_cc = p.cc("AllReduce", ALU.max, [g.ar_in.ap()], [g.ar_out.ap()])
            t6 = p.dma("y", g.ROWS[0:1, 3200:3201],
                       AP(g.ar_out, 0, [[1, 1], [1, 1]]), deps=[t_cc], ch="pr2")
            am = g.ROWS[0:1, 3200:3201]
            t7 = p.op("v", "tensor_scalar", am, am, 1.0 / E4MAX, 1e-12,
                      ALU.mult, ALU.max, deps=[t6])
            t8 = p.op("v", "reciprocal", g.ROWS[0:1, 3201:3202],
                      am, deps=[t7])
            tb = p.mm(psT[:, 0:1], g.ones_rf, g.ROWS[0:1, 3201:3202],
                      deps=[t8])
            tb2 = p.op("s", "copy", g.smalls[:, 62:63], psT[:, 0:1], deps=[tb])
            t9 = p.op("v", "tensor_scalar", g.SM2[0:1, 0:TL], rr,
                      g.ROWS[0:1, 3201:3202], 0.5, ALU.mult, ALU.mult,
                      deps=[t8])
            tbc = _row_bcast(p, g, g.SM2[0:1, 0:TL], psB, g.SM[:, 0:TL], TL, t9)
            tqx = None
            t_q8 = [None, None]
            for j in range(JC):
                par = j % 2
                q1 = p.op("v", "tensor_sub", g.SM2[:, 0:TL],
                          g.XT[:, j * TL:(j + 1) * TL], g.SM[:, TL:2 * TL],
                          deps=[tv2, tbc])
                q2 = p.op("v", "tensor_mul", g.SM2[:, 0:TL], g.SM2[:, 0:TL],
                          g.SM[:, 0:TL], deps=[q1])
                q3 = p.op("v", "tensor_scalar", g.SM2[:, 0:TL], g.SM2[:, 0:TL],
                          224.0, -224.0, ALU.min, ALU.max, deps=[q2])
                q4 = p.op("v", "tensor_copy", g.X8[:, par * 1024:par * 1024 + TL],
                          g.SM2[:, 0:TL],
                          deps=[q3] + ([t_q8[par]] if t_q8[par] else []))
                tqx = p.dma("y", AP(g.xq8d, j * 128 * TL, [[TL, 128], [1, TL]]),
                            g.X8[:, par * 1024:par * 1024 + TL], deps=[q4],
                            ch=f"x8{par}")
                t_q8[par] = tqx
            g._tqx8 = [t for t in t_q8 if t]
        wf = g.inp("l2_wf")
    with p.phase("L2f_wm"):
        with nc.psum_tensor(f"qw{un}", [128, 512], F32) as psT:
            fen = p.fence()
            t_ld = [None, None]
            acc = None
            for s in range(JC):
                par = s % 2
                t_l = p.dma("y", g.WR[:, par * 2048:par * 2048 + 2048],
                            AP(wf, s * JC * 16384, [[128, 128], [16384, JC], [1, 128]]),
                            deps=([t_ld[par]] if t_ld[par] else fen),
                            ch=f"sl{par}")
                t_r2 = p.op("v", "tensor_reduce", g.SM2[:, 512:513],
                            g.WR[:, par * 2048:par * 2048 + 2048],
                            AX.X, ALU.max, apply_absolute_value=True,
                            deps=[t_l])
                if acc is None:
                    acc = p.op("v", "tensor_copy", g.SM2[:, 520:521],
                               g.SM2[:, 512:513], deps=[t_r2])
                else:
                    acc = p.op("v", "tensor_max", g.SM2[:, 520:521],
                               g.SM2[:, 520:521], g.SM2[:, 512:513],
                               deps=[t_r2])
                t_ld[par] = t_r2
            t5 = p.op("t", "transpose", psT[0:1, 0:128], g.SM2[:, 520:521],
                      g.idf[:, 0:128], deps=[acc])
            t6 = p.op("v", "tensor_reduce", g.ROWS[0:1, 2048:2049],
                      psT[0:1, 0:128], AX.X, ALU.max, deps=[t5])
            t7 = p.op("v", "tensor_scalar", g.ROWS[0:1, 2048:2049],
                      g.ROWS[0:1, 2048:2049], 1.0 / E4MAX, 1e-12,
                      ALU.mult, ALU.max, deps=[t6])
            t8 = p.op("v", "reciprocal", g.ROWS[0:1, 2049:2050],
                      g.ROWS[0:1, 2048:2049], deps=[t7])
            t8 = p.op("v", "tensor_scalar", g.ROWS[0:1, 2049:2050],
                      g.ROWS[0:1, 2049:2050], 0.5, None, ALU.mult, deps=[t8])
            for col, rofs in ((61, 2049),):
                tb = p.mm(psT[:, 1 + col - 61:2 + col - 61], g.ones_rf,
                          g.ROWS[0:1, rofs:rofs + 1], deps=[t8])
                p.op("s", "copy", g.smalls[:, col:col + 1],
                     psT[:, 1 + col - 61:2 + col - 61], deps=[tb])
            ta2 = p.op("v", "tensor_scalar", g.ROWS[0:1, 2051:2052],
                       g.ROWS[0:1, 3200:3201],
                       g.ROWS[0:1, 2048:2049], 4.0, ALU.mult, ALU.mult,
                       deps=[t8])
            tb3 = p.mm(psT[:, 8:9], g.ones_rf, g.ROWS[0:1, 2051:2052],
                       deps=[ta2])
            g._t_sc = p.op("s", "copy", g.smalls[:, 60:61], psT[:, 8:9],
                           deps=[tb3])
    with p.phase("L2f_mm"):
        ctx4 = ExitStack()
        psYs = [ctx4.enter_context(nc.psum_tensor(f"qm{un}_{ii}", [128, 512], F32))
                for ii in range(4)]
        with ctx4:
            fen = p.fence()
            t_w = [None, None]
            t_x8 = [None, None]
            prev_q = None
            prev_ev = getattr(g, "_fp8_ev", None)
            for oq in range(8):
                tqs = []
                for o4 in range(2):
                    oc = oq * 2 + o4
                    par = oc % 2
                    t_wl = p.dma("y", g.WR[:, par * 2048:par * 2048 + 2048],
                                 AP(wf, oc * JC * 16384,
                                    [[128, 128], [16384, JC], [1, 128]]),
                                 deps=([t_w[par]] if t_w[par] else fen),
                                 ch=f"sl{par}")
                    tq = None
                    for hh in range(2):
                        q1 = p.op("v", "tensor_scalar", g.SM2[:, 0:1024],
                                  g.WR[:, par * 2048 + hh * 1024:
                                       par * 2048 + hh * 1024 + 1024],
                                  g.smalls[:, 61:62], None, ALU.mult,
                                  deps=[t_wl] + ([prev_q] if prev_q else []))
                        q2 = p.op("v", "tensor_scalar", g.SM2[:, 0:1024],
                                  g.SM2[:, 0:1024], 224.0, -224.0,
                                  ALU.min, ALU.max, deps=[q1])
                        tq = p.op("v", "tensor_copy",
                                  g.W8[:, o4 * 2048 + hh * 1024:
                                       o4 * 2048 + hh * 1024 + 1024],
                                  g.SM2[:, 0:1024], deps=[q2])
                    t_w[par] = tq
                    tqs.append(tq)
                tmm = None
                for j in range(JC):
                    pj = j % 2
                    t_x8[pj] = p.dma("y", g.X8[:, pj * 1024:pj * 1024 + TL],
                                     AP(g.xq8d, j * 128 * TL, [[TL, 128], [1, TL]]),
                                     deps=g._tqx8 +
                                          ([t_x8[pj]] if t_x8[pj] else []) +
                                          ([tmm] if tmm else []) +
                                          ([prev_q] if prev_q and j < 2 else []),
                                     ch=f"x8{pj}")
                    for o4 in range(2):
                        for h0 in range(0, TL, 512):
                            hw = min(512, TL - h0)
                            tmm = p.mm(psYs[o4 * 2 + h0 // 512][:, 0:hw],
                                       g.W8[:, o4 * 2048 + j * 128:
                                            o4 * 2048 + (j + 1) * 128],
                                       g.X8[:, pj * 1024 + h0:pj * 1024 + h0 + hw],
                                       start=(j == 0), stop=(j == JC - 1),
                                       deps=[tqs[o4], t_x8[pj]] +
                                            ([prev_ev] if prev_ev and j == 0
                                             else []),
                                       inc=True)
                prev_q = tmm
                prev_ev = None
                elast = None
                for o4 in range(2):
                    oc = oq * 2 + o4
                    for h0 in range(0, TL, 512):
                        hw = min(512, TL - h0)
                        e1 = p.op("v", "tensor_scalar",
                                  g.SM2.bitcast(BF16)[:, 0:hw],
                                  psYs[o4 * 2 + h0 // 512][:, 0:hw],
                                  g.smalls[:, 60:61], None,
                                  ALU.mult, deps=[tmm, g._t_sc])
                        e2 = p.op("v", "tensor_add",
                                  g.XT[:, oc * TL + h0:oc * TL + h0 + hw],
                                  g.XT[:, oc * TL + h0:oc * TL + h0 + hw],
                                  g.SM2.bitcast(BF16)[:, 0:hw], deps=[e1])
                        elast = e2
                prev_q = elast
                g._fp8_ev = elast
def _build(nc, ctx, map0, layers, n_sub):
    p = P(nc)
    p.setup(ctx)
    map0 = {**map0, "cvals": np.zeros((128, 16), np.float32)}
    g = _declare(nc, map0, layers)
    _alloc_sbuf(g, ctx, layers)
    g.boff = (nc.partition_id() // 4) * DIMS.TB
    g.coff = nc.partition_id() * DIMS.TB
    _load_consts(p, g, layers)
    if (1 in layers) or (3 in layers):
        _bld_rope_tables(p, g)
    with p.phase("ones"):
        p.op("v", "memset", g.smalls[:, 63:64], 1.0)

    def _prequant(L):
        _wq_layer(p, g, L)
    sub = 0
    for L in layers:
        if sub >= n_sub:
            break
        if L in (0, 3) and sub + 1 < n_sub:
            _prequant(L)
        if not SKIP_ATTN:
            _attn_layer(p, g, L)
        sub += 1
        if sub >= n_sub:
            break
        _ffn_layer(p, g, L)
        sub += 1

    with p.phase("out"):
        t_o = p.dma("y", _featmaj_ap(g.out, DIMS.TL),
                    g.XT[:, 0:JC * DIMS.TL], deps=p.fence(), ch="o")
        p.wait("y", t_o)
    return p, g


def _run_group(maps, x_cur, layers, n_sub, sim=False):
    nc = bass.Bass("TRN2", target_bir_lowering=False)
    with ExitStack() as ctx:
        p, g = _build(nc, ctx, {**maps[0], "x_in": x_cur[0]}, layers, n_sub)
    nc.finalize()
    cv = np.zeros((128, 16), np.float32)
    for idx, v in enumerate(g.cvals):
        cv[:, idx] = v
    in_maps = []
    for r in range(NCORE):
        m = {k: maps[r][k] for k in g.h if k != "x_in" and k != "cvals"}
        m["x_in"] = x_cur[r]
        m["cvals"] = cv
        in_maps.append(m)
    g_keep[0] = g
    if sim:
        from concourse.bass_interp import MultiCoreSim
        ms = MultiCoreSim(nc, num_cores=NCORE, num_workers=1)
        for r in range(NCORE):
            for k, v in in_maps[r].items():
                ms.cores[r].tensor(k)[:] = v
            ms.cores[r].tensor("partition_id")[:] = np.array([[r]], np.uint32)
        ms.simulate(check_with_hw=False)
        return [np.asarray(ms.cores[r].tensor("x_out")).copy()
                for r in range(NCORE)]
    res = run_bass_kernel_spmd(nc, in_maps, core_ids=list(range(NCORE)))
    g_keep.append(res)
    return [np.asarray(res.results[r]["x_out"]) for r in range(NCORE)]


SIM = False
g_keep = [None]

def kernel(**inputs):
    d = DIMS
    maps = _prep_host(inputs)
    x_cur = [m.pop("x_in") for m in maps]
    sub_left = UPTO
    for grp in SPLIT:
        if sub_left <= 0:
            break
        n_sub = min(sub_left, 2 * len(grp))
        x_cur = _run_group(maps, x_cur, grp, n_sub, sim=SIM)
        sub_left -= n_sub
    out = np.empty((d.B, d.SEQ, D), np.float32)
    for r in range(NCORE):
        for b in range(d.B):
            out[b, r * d.TB:(r + 1) * d.TB, :] = \
                x_cur[r][:, b * d.TB:(b + 1) * d.TB].T
    return out
